# version 11
# baseline (speedup 1.0000x reference)
import numpy as np
from contextlib import ExitStack

import concourse.bass as bass
import concourse.mybir as mybir
from concourse.bass_utils import run_bass_kernel_spmd

F32 = mybir.dt.float32
BF16 = mybir.dt.bfloat16
AF = mybir.ActivationFunctionType
ALU = mybir.AluOpType
AX = mybir.AxisListType

T = 4096
D = 1024
TT = 512
NTT = T // TT
NCH = D // 128
LN_EPS = 1e-5
RMS_EPS0 = 1.1920929e-07

DEBUG_OUTPUTS = set()


class KB:

    NDMA = 8

    def __init__(self):
        self.nc = bass.Bass("TRN2", target_bir_lowering=False)
        self.stack = ExitStack()
        self.sems = {}
        self.cnt = {}
        names = ["sT", "sS", "sV", "sP"]
        names += [f"dQ{i}" for i in range(self.NDMA)]
        names += [f"dP{i}" for i in range(self.NDMA)]
        for name in names:
            self.sems[name] = self.stack.enter_context(self.nc.semaphore(name))
            self.cnt[name] = 0
        self._rrq = 0
        self._rrp = 0
        nc = self.nc
        self.T, self.S, self.V, self.P, self.Q = nc.tensor, nc.scalar, nc.vector, nc.gpsimd, nc.sync

    def sbuf(self, name, shape, dtype):
        return self.stack.enter_context(self.nc.sbuf_tensor(name, shape, dtype))

    def psum(self, name, shape, dtype=F32):
        return self.stack.enter_context(self.nc.psum_tensor(name, shape, dtype))

    def dram(self, name, shape, dtype):
        kind = "ExternalOutput" if name in DEBUG_OUTPUTS else "Internal"
        return self.nc.dram_tensor(name, shape, dtype, kind=kind).ap()

    def inc(self, ins, sem, n=1):
        self.cnt[sem] += n
        ins.then_inc(self.sems[sem], n)
        return (sem, self.cnt[sem])

    def tok(self, sem):
        return (sem, self.cnt[sem])

    def wait(self, eng, *toks):
        flat = []
        def _add(t):
            if t is None:
                return
            if isinstance(t, (list, tuple)) and t and isinstance(t[0], (list, tuple)):
                for u in t:
                    _add(u)
            elif isinstance(t, list):
                for u in t:
                    _add(u)
            else:
                flat.append(t)
        for t in toks:
            _add(t)
        best = {}
        for s, v in flat:
            if v > 0:
                best[s] = max(best.get(s, 0), v)
        for s, v in best.items():
            eng.wait_ge(self.sems[s], v)

    def barrier(self):
        toks = [self.tok(s) for s in self.sems]
        for eng in (self.T, self.S, self.V, self.P, self.Q):
            self.wait(eng, *toks)

    def fence_loads(self):
        toks = [self.tok(s) for s in self.sems]
        self.wait(self.Q, *toks)
        self.wait(self.P, *toks)

    def ldq(self, dst, src, *toks):
        self.wait(self.Q, *toks)
        lane = f"dQ{self._rrq}"
        self._rrq = (self._rrq + 1) % self.NDMA
        return self.inc(self.Q.dma_start(dst, src), lane, 16)

    def ldp(self, dst, src, *toks, accum=False):
        self.wait(self.P, *toks)
        kw = {}
        if accum:
            kw["accum_op"] = ALU.add
        lane = f"dP{self._rrp}"
        self._rrp = (self._rrp + 1) % self.NDMA
        return self.inc(self.P.dma_start(dst, src, **kw), lane, 16)

    def stq(self, dst, src, *toks):
        return self.ldq(dst, src, *toks)


def ceil_div(a, b):
    return (a + b - 1) // b


class nullctx:
    def __init__(self, v):
        self.v = v
    def __enter__(self):
        return self.v
    def __exit__(self, *a):
        return False


_UNIQ = [0]


def uniq(name):
    _UNIQ[0] += 1
    return name + "_u" + str(_UNIQ[0])



def prep_common(kb, dram_vecs, consts):
    nc = kb.nc
    c = {}
    c["ident"] = kb.sbuf("ident", [128, 128], F32)
    c["identb"] = kb.sbuf("identb", [128, 128], BF16)
    c["hmask"] = kb.sbuf("hmask", [128, 2], BF16)
    c["hsel"] = kb.sbuf("hsel", [2, 128], BF16)
    c["tri"] = kb.sbuf("tri", [128, 128], BF16)
    c["sel8"] = kb.sbuf("sel8", [8, 8 * 128], BF16)
    c["ones_f"] = kb.sbuf("ones_f", [128, 1], F32)
    c["ones_b"] = kb.sbuf("ones_b", [128, 1], BF16)
    c["ones512"] = kb.sbuf("ones512", [128, TT], F32)
    c["ones1_f"] = kb.sbuf("ones1_f", [1, 128], F32)
    c["ones1_b"] = kb.sbuf("ones1_b", [1, 128], BF16)
    c["cpool"] = kb.sbuf("cpool", [128, 8], F32)
    kb.V.memset(c["cpool"][:, 0:1], 1.0)
    kb.V.memset(c["cpool"][:, 1:2], LN_EPS)
    kb.V.memset(c["cpool"][:, 2:3], RMS_EPS0)
    kb.V.memset(c["ones_f"][:], 1.0)
    kb.V.memset(c["ones_b"][:], 1.0)
    kb.V.memset(c["ones512"][:], 1.0)
    kb.V.memset(c["ones1_f"][:], 1.0)
    tvm = kb.inc(kb.V.memset(c["ones1_b"][:], 1.0), "sV")
    tcl = [kb.ldq(c["ident"][:], consts["c_ident"]),
           kb.ldq(c["hmask"][:], consts["c_hmask"]),
           kb.ldq(c["hsel"][:], consts["c_hsel"]),
           kb.ldq(c["tri"][:], consts["c_tri"]),
           kb.ldq(c["sel8"][:], consts["c_sel8"]),
           kb.ldp(c["identb"][:], consts["c_ident"])]
    vec_tiles = {}
    for name, (ap, total) in dram_vecs.items():
        rows = ceil_div(total, 128)
        vec_tiles[name] = kb.sbuf("vec_" + name, [128, rows], F32)
    with ExitStack() as st:
        ps = st.enter_context(nc.psum_tensor("ps_prep", [128, 128], F32))
        stage = st.enter_context(nc.sbuf_tensor("stage_prep", [128, 128], F32))
        last_ev = None
        kb.wait(kb.T, tcl, tvm)
        for name, (ap, total) in dram_vecs.items():
            rows = ceil_div(total, 128)
            tile = vec_tiles[name]
            tl = kb.ldq(stage[:rows, :], ap.rearrange("(c p) -> c p", p=128), last_ev)
            kb.wait(kb.T, tl)
            tt_ = kb.inc(kb.T.transpose(ps[:, 0:rows], stage[:rows, :],
                                        c["ident"][:rows, :rows]), "sT")
            kb.wait(kb.V, tt_)
            last_ev = kb.inc(kb.V.tensor_copy(tile[:], ps[:, 0:rows]), "sV")
        kb.barrier()
    c["vecs"] = vec_tiles
    return c


def host_consts():
    import ml_dtypes
    b16 = ml_dtypes.bfloat16
    ident = np.eye(128, dtype=np.float32)
    hmask = np.zeros((128, 2), dtype=b16)
    hmask[0:64, 0] = 1; hmask[64:128, 1] = 1
    hsel = np.zeros((2, 128), dtype=b16)
    hsel[0, 0:64] = 1; hsel[1, 64:128] = 1
    tri = (np.arange(128)[None, :] >= np.arange(128)[:, None]).astype(b16)
    sel8 = np.zeros((8, 8 * 128), dtype=b16)
    for e in range(8):
        sel8[e, e * 128:(e + 1) * 128] = 1
    return {"c_ident": ident, "c_hmask": hmask, "c_hsel": hsel,
            "c_tri": tri, "c_sel8": sel8}


def declare_consts(nc):
    return {
        "c_ident": nc.dram_tensor("c_ident", [128, 128], F32, kind="ExternalInput").ap(),
        "c_hmask": nc.dram_tensor("c_hmask", [128, 2], BF16, kind="ExternalInput").ap(),
        "c_hsel": nc.dram_tensor("c_hsel", [2, 128], BF16, kind="ExternalInput").ap(),
        "c_tri": nc.dram_tensor("c_tri", [128, 128], BF16, kind="ExternalInput").ap(),
        "c_sel8": nc.dram_tensor("c_sel8", [8, 8 * 128], BF16, kind="ExternalInput").ap(),
    }



def phase_ln(kb, com, src_f32, dst_bf16, dst_f32=None, affine=None, st=None,
             nbuf=2):
    nc = kb.nc
    with (ExitStack() if st is None else nullctx(st)) as st:
        sb = lambda n, s, d: st.enter_context(nc.sbuf_tensor(uniq(n), s, d))
        pst = lambda n, s: st.enter_context(nc.psum_tensor(uniq(n), s, F32))
        XF = [sb(f"ln_x{p}", [128, NCH, TT], F32) for p in range(nbuf)]
        SQ = [sb(f"ln_sq{p}", [128, NCH, TT], F32) for p in range(nbuf)]
        EV = [sb(f"ln_ev{p}", [128, 2 * TT], F32) for p in range(nbuf)]
        OB = [sb(f"ln_ob{p}", [128, NCH, TT], BF16) for p in range(nbuf)]
        OF = [sb(f"ln_of{p}", [128, NCH, TT], F32) for p in range(nbuf)] if dst_f32 is not None else None
        psS = [pst(f"ln_psS{p}", [1, TT]) for p in range(nbuf)]
        psQ = [pst(f"ln_psQ{p}", [1, TT]) for p in range(nbuf)]
        psM = [pst(f"ln_psM{p}", [128, TT]) for p in range(nbuf)]
        psR = [pst(f"ln_psR{p}", [128, TT]) for p in range(nbuf)]
        MR = [sb(f"ln_mr{p}", [1, 2 * TT], F32) for p in range(nbuf)]

        src_r = src_f32.rearrange("(c p) t -> p c t", p=128)
        dst_r = dst_bf16.rearrange("(c p) t -> p c t", p=128) if dst_bf16 is not None else None
        dstf_r = dst_f32.rearrange("(c p) t -> p c t", p=128) if dst_f32 is not None else None

        done = {}
        for tt in range(NTT):
            par = tt % nbuf
            ts = slice(tt * TT, (tt + 1) * TT)
            free = done.get(par)
            tl = kb.ldq(XF[par][:], src_r[:, :, ts], *(free or []))
            kb.wait(kb.S, tl)
            tsq = kb.inc(kb.S.activation(SQ[par][:], XF[par][:], AF.Square), "sS")
            kb.wait(kb.T, tl)
            for ch in range(NCH):
                kb.T.matmul(psS[par][:], com["ones_f"][:], XF[par][:, ch, :],
                            start=(ch == 0), stop=(ch == NCH - 1))
            kb.wait(kb.T, tsq)
            for ch in range(NCH):
                ins = kb.T.matmul(psQ[par][:], com["ones_f"][:], SQ[par][:, ch, :],
                                  start=(ch == 0), stop=(ch == NCH - 1))
            tstat = kb.inc(ins, "sT")
            m = MR[par][:, 0:TT]
            sq_m = EV[par][:1, 0:TT]
            var = EV[par][:1, TT:2 * TT]
            kb.wait(kb.V, tstat)
            tv1 = kb.inc(kb.V.tensor_scalar_mul(m, psS[par][:], 1.0 / D), "sV")
            kb.wait(kb.S, tv1)
            tm2 = kb.inc(kb.S.activation(sq_m, m, AF.Square), "sS")
            kb.wait(kb.V, tm2)
            tvar = kb.inc(kb.V.scalar_tensor_tensor(var, psQ[par][:], 1.0 / D, sq_m,
                                                    ALU.mult, ALU.subtract), "sV")
            kb.wait(kb.S, tvar)
            sd = EV[par][:1, 0:TT]
            tsd = kb.inc(kb.S.activation(sd, var, AF.Sqrt, bias=com["cpool"][:1, 1:2]), "sS")
            kb.wait(kb.V, tsd)
            r = MR[par][:, TT:2 * TT]
            trec = kb.inc(kb.V.reciprocal(r, sd), "sV")
            kb.wait(kb.T, trec)
            kb.T.matmul(psM[par][:], com["ones1_f"][:], m, start=True, stop=True)
            tbc = kb.inc(kb.T.matmul(psR[par][:], com["ones1_f"][:], r,
                                     start=True, stop=True), "sT")
            kb.wait(kb.V, tbc)
            tmp = EV[par][:, 0:TT]
            for ch in range(NCH):
                kb.V.tensor_sub(tmp, XF[par][:, ch, :], psM[par][:])
                if OF is not None:
                    ins = kb.V.tensor_mul(OF[par][:, ch, :], tmp, psR[par][:])
                    src_n = OF[par][:, ch, :]
                else:
                    ins = kb.V.tensor_mul(tmp, tmp, psR[par][:])
                    src_n = tmp
                if affine is not None:
                    w_t, b_t = affine
                    ins = kb.V.tensor_scalar(OB[par][:, ch, :], src_n,
                                             w_t[:, ch:ch + 1], b_t[:, ch:ch + 1],
                                             ALU.mult, ALU.add)
                elif dst_bf16 is not None:
                    ins = kb.V.tensor_copy(OB[par][:, ch, :], src_n)
            tap = kb.inc(ins, "sV")
            outs = []
            if dst_bf16 is not None:
                outs.append(kb.stq(dst_r[:, :, ts], OB[par][:], tap))
            if dstf_r is not None:
                outs.append(kb.stq(dstf_r[:, :, ts], OF[par][:], tap))
            done[par] = outs + [tap]
            yield



def phase_l0_proj(kb, com, xn0T, wq, wk, wv, qnT, knT, v0):
    nc = kb.nc
    with ExitStack() as st:
        sb = lambda n, s, d: st.enter_context(nc.sbuf_tensor(uniq(n), s, d))
        pst = lambda n, s: st.enter_context(nc.psum_tensor(uniq(n), s, F32))
        WQ = sb("p1_wq", [128, NCH, D], BF16)
        WK = sb("p1_wk", [128, NCH, D], BF16)
        WV = sb("p1_wv", [128, NCH, D], BF16)
        XB = [sb(f"p1_xb{p}", [128, NCH, TT], BF16) for p in range(2)]
        SQT = [sb(f"p1_sq{p}", [128, TT], BF16) for p in range(2)]
        QF = [sb(f"p1_qf{p}", [128, TT], F32) for p in range(2)]
        OB = [sb(f"p1_ob{p}", [128, TT], BF16) for p in range(2)]
        SD = [sb(f"p1_sd{p}", [2, 2 * TT], F32) for p in range(2)]
        RRB = [sb(f"p1_rr{p}", [2, TT], BF16) for p in range(2)]
        VO = [sb(f"p1_vo{p}", [128, 16, 65], BF16) for p in range(2)]
        psA = [pst(f"p1_psA{p}", [128, TT]) for p in range(2)]
        psST = [pst(f"p1_psST{p}", [2, TT]) for p in range(2)]
        psB = [pst(f"p1_psB{p}", [128, TT]) for p in range(2)]

        twq = kb.ldp(WQ[:], wq.rearrange("(c p) j -> p c j", p=128))
        twk = kb.ldp(WK[:], wk.rearrange("(c p) j -> p c j", p=128))
        twv = kb.ldp(WV[:], wv.rearrange("(c p) j -> p c j", p=128))
        xn_r = xn0T.rearrange("(c p) t -> p c t", p=128)
        qn_r = qnT.rearrange("(c p) t -> p c t", p=128)
        kn_r = knT.rearrange("(c p) t -> p c t", p=128)

        free = {}
        unit = 0
        for tt in range(NTT):
            ts = slice(tt * TT, (tt + 1) * TT)
            tl = kb.ldq(XB[tt % 2][:], xn_r[:, :, ts], *free.get(("xb", tt % 2), []))
            xb = XB[tt % 2]
            for W, wtok, dst_r in ((WQ, twq, qn_r), (WK, twk, kn_r)):
                for jc in range(NCH):
                    par = unit % 2
                    unit += 1
                    kb.wait(kb.T, tl, wtok, *free.get(("qk", par), []))
                    for ch in range(NCH):
                        ins = kb.T.matmul(psA[par][:], W[:, ch, jc * 128:(jc + 1) * 128],
                                          xb[:, ch, :], start=(ch == 0), stop=(ch == NCH - 1))
                    tmm = kb.inc(ins, "sT")
                    kb.wait(kb.S, tmm)
                    tsq = kb.inc(kb.S.activation(SQT[par][:], psA[par][:], AF.Square), "sS")
                    kb.wait(kb.T, tsq)
                    tst = kb.inc(kb.T.matmul(psST[par][:], com["hmask"][:], SQT[par][:],
                                             start=True, stop=True), "sT")
                    kb.wait(kb.S, tst)
                    sd = SD[par][:, 0:TT]
                    tsd = kb.inc(kb.S.activation(sd, psST[par][:], AF.Sqrt,
                                                 scale=1.0 / 64, bias=com["cpool"][:2, 2:3]), "sS")
                    kb.wait(kb.V, tsd)
                    rr = SD[par][:, TT:2 * TT]
                    tre = kb.inc(kb.V.reciprocal(rr, sd), "sV")
                    kb.wait(kb.V, tre)
                    trr = kb.inc(kb.V.tensor_copy(RRB[par][:], rr), "sV")
                    kb.wait(kb.T, trr)
                    tbc = kb.inc(kb.T.matmul(psB[par][:], com["hsel"][:], RRB[par][:],
                                             start=True, stop=True), "sT")
                    kb.wait(kb.V, tbc)
                    kb.V.tensor_copy(QF[par][:], psA[par][:])
                    tev = kb.inc(kb.V.tensor_mul(OB[par][:], QF[par][:], psB[par][:]), "sV")
                    tst_o = kb.stq(dst_r[:, jc, ts], OB[par][:], tev)
                    free[("qk", par)] = [tev, tst_o]
            for s4 in range(4):
                par = s4 % 2
                vo = VO[par]
                kb.wait(kb.V, *free.get(("vo", par), []))
                tvm = kb.inc(kb.V.memset(vo[:, :, 64:65], 1.0), "sV")
                toks_half = []
                for half in range(2):
                    parp = (unit + half) % 2
                    kb.wait(kb.T, tl, twv, *free.get(("qk", parp), []))
                    for ch in range(NCH):
                        ins = kb.T.matmul(psA[parp][:],
                                          xb[:, ch, s4 * 128:(s4 + 1) * 128],
                                          WV[:, ch, half * 512:(half + 1) * 512],
                                          start=(ch == 0), stop=(ch == NCH - 1))
                    tmm = kb.inc(ins, "sT")
                    kb.wait(kb.V, tmm, tvm)
                    tev = kb.inc(kb.V.tensor_copy(
                        vo[:, half * 8:(half + 1) * 8, 0:64],
                        psA[parp][:].rearrange("p (h v) -> p h v", v=64)), "sV")
                    free[("qk", parp)] = [tev]
                    toks_half.append(tev)
                unit += 2
                row = tt * TT + s4 * 128
                tso = kb.stq(v0[row:row + 128, :], vo[:], *toks_half)
                free[("vo", par)] = [tso]
            free[("xb", tt % 2)] = [kb.tok("sT")]
        kb.barrier()



def phase_l0_attn(kb, com, qnT, knT, v0, yrT, recT, st=None):
    nc = kb.nc
    H = 16
    QTL = 512
    NQI = T // QTL
    ND = 3
    with (ExitStack() if st is None else nullctx(st)) as st:
        sb = lambda n, s, d: st.enter_context(nc.sbuf_tensor(uniq(n), s, d))
        pst = lambda n, s: st.enter_context(nc.psum_tensor(uniq(n), s, F32))
        KT = [sb(f"p2_kt{p}", [64, T], BF16) for p in range(2)]
        VT = [sb(f"p2_vt{p}", [128, 32, 65], BF16) for p in range(2)]
        QT = [sb(f"p2_qt{p}", [64, QTL], BF16) for p in range(2)]
        TS = sb("p2_ts", [128, QTL], F32)
        PB = [sb(f"p2_pb{p}", [128, QTL], BF16) for p in range(4)]
        YB = [sb(f"p2_yb{p}", [64, QTL], BF16) for p in range(2)]
        RC = [sb(f"p2_rc{p}", [1, QTL], F32) for p in range(2)]
        psS = [pst(f"p2_psS{p}", [128, QTL]) for p in range(ND)]
        psY = [pst(f"p2_psY{p}", [65, QTL]) for p in range(2)]

        freeS = {i: None for i in range(ND)}
        freePB = {i: None for i in range(4)}
        freeQ = {0: None, 1: None}
        freeY = {0: None, 1: None}
        freeYB = {0: None, 1: None}
        kvload = {}
        kvfree = {0: [], 1: []}

        units = [(h, qi) for h in range(H) for qi in range(NQI)]
        qload = {}

        def load_q(i):
            if i >= len(units) or i in qload:
                return
            h, qi = units[i]
            up = i % 2
            qload[i] = kb.ldq(QT[up][:], qnT[64 * h:64 * h + 64,
                                             QTL * qi:QTL * qi + QTL], freeQ[up])

        def load_kv(h):
            hp = h % 2
            tK = kb.ldq(KT[hp][:], knT[64 * h:64 * h + 64, :], *kvfree[hp])
            tV = kb.ldq(VT[hp][:],
                        v0.rearrange("(c p) e -> p c e", p=128)[:, :, 65 * h:65 * h + 65],
                        *kvfree[hp])
            kvload[h] = (tK, tV)

        load_kv(0)
        load_q(0)
        for ui, (h, qi) in enumerate(units):
            hp = h % 2
            up = ui % 2
            tK, tV = kvload[h]
            tQ = qload[ui]
            load_q(ui + 1)
            if qi == 0 and h + 1 < H:
                load_kv(h + 1)
            tq0 = QTL * qi
            nkj = 4 * (qi + 1)
            head_last = None

            def emit_pv(kj, f0, tmask):
                nonlocal head_last
                pb = kj % 4
                kb.wait(kb.T, tmask, tV)
                if kj == 0:
                    kb.wait(kb.T, freeY[up])
                ins = kb.T.matmul(psY[up][:, f0:QTL], VT[hp][:, kj, :],
                                  PB[pb][:, f0:QTL],
                                  start=(kj == 0), stop=(kj == nkj - 1))
                tpv = kb.inc(ins, "sT")
                freePB[pb] = tpv
                head_last = tpv

            pend = []
            for kj in range(nkj):
                sl = kj % ND

                tk0 = 128 * kj
                fd = tk0 - tq0
                f0 = max(0, fd)
                kb.wait(kb.T, tQ, tK, freeS[sl])
                ins = kb.T.matmul(psS[sl][:, f0:QTL], KT[hp][:, tk0:tk0 + 128],
                                  QT[up][:, f0:QTL], start=True, stop=True)
                tmm = kb.inc(ins, "sT")
                pb = kj % 4
                kb.wait(kb.S, tmm, freePB[pb])
                tth = kb.inc(kb.S.activation(TS[:, f0:QTL], psS[sl][:, f0:QTL], AF.Tanh,
                                             scale=1.0 / 400.0), "sS")
                tse = kb.inc(kb.S.activation(PB[pb][:, f0:QTL], TS[:, f0:QTL],
                                             AF.Exp, scale=50.0), "sS")
                freeS[sl] = tth
                if fd >= 0:
                    kb.wait(kb.V, tse)
                    tmask = kb.inc(kb.V.tensor_mul(PB[pb][:, fd:fd + 128],
                                                   PB[pb][:, fd:fd + 128],
                                                   com["tri"][:]), "sV")
                else:
                    tmask = tse
                pend.append((kj, f0, tmask))
                if len(pend) == 3:
                    emit_pv(*pend.pop(0))
                yield
            while pend:
                emit_pv(*pend.pop(0))
            freeQ[up] = head_last
            kb.wait(kb.V, head_last, freeYB[up])
            kb.V.reciprocal(RC[up][:], psY[up][64:65, :])
            tev = kb.inc(kb.V.tensor_copy(YB[up][:], psY[up][0:64, :]), "sV")
            freeY[up] = tev
            kb.stq(yrT[64 * h:64 * h + 64, tq0:tq0 + QTL], YB[up][:], tev)
            t2 = kb.stq(recT[h:h + 1, tq0:tq0 + QTL], RC[up][:], tev)
            freeYB[up] = t2
            if qi == NQI - 1:
                kvfree[hp] = [head_last]



def phase_router(kb, com, inT_bf16, rwT, bias_row_ap, combT, mode):
    nc = kb.nc
    with ExitStack() as st:
        sb = lambda n, s, d: st.enter_context(nc.sbuf_tensor(uniq(n), s, d))
        pst = lambda n, s: st.enter_context(nc.psum_tensor(uniq(n), s, F32))
        RW = sb("rt_rw", [128, NCH, 8], BF16)
        EB = sb("rt_eb", [1, 8], F32)
        XB = [sb(f"rt_xb{p}", [128, NCH, TT], BF16) for p in range(2)]
        LG = sb("rt_lg", [128, 32], F32)
        ACT = sb("rt_act", [128, 32], F32)
        SCR = sb("rt_scr", [128, 32], F32)
        MK = sb("rt_mk", [128, 32], F32)
        CMB = sb("rt_cmb", [128, 32], F32)
        M1 = sb("rt_m1", [128, 4], F32)
        M2 = sb("rt_m2", [128, 4], F32)
        DEN = sb("rt_den", [128, 4], F32)
        psL = pst("rt_psL", [128, 32])
        psT = pst("rt_psT", [8, 128])
        tw = kb.ldp(RW[:], rwT.rearrange("(c p) e -> p c e", p=128))
        te = kb.ldq(EB[:], bias_row_ap)
        in_r = inT_bf16.rearrange("(c p) t -> p c t", p=128)
        prev = {}
        ntt = 1 if "rt_one_tt" in DEBUG_OUTPUTS else NTT
        for tt in range(ntt):
            par = tt % 2
            ts = slice(tt * TT, (tt + 1) * TT)
            tl = kb.ldq(XB[par][:], in_r[:, :, ts], prev.get(("xb", par)))
            kb.wait(kb.T, tl, tw, te, prev.get("evac"))
            for s in range(4):
                sl = slice(s * 8, (s + 1) * 8)
                for ch in range(NCH):
                    kb.T.matmul(psL[:, sl], XB[par][:, ch, s * 128:(s + 1) * 128],
                                RW[:, ch, :], start=(ch == 0), stop=False)
                ins = kb.T.matmul(psL[:, sl], com["ones1_f"][:], EB[:],
                                  start=False, stop=True, skip_group_check=True)
            tmm = kb.inc(ins, "sT")
            kb.wait(kb.V, tmm)
            tlg = kb.inc(kb.V.tensor_copy(LG[:], psL[:]), "sV")
            if "rt_lgd" in DEBUG_OUTPUTS:
                if tt == 0 and not hasattr(kb, "_rt_lgd"):
                    kb._rt_lgd = kb.dram("rt_lgd", [NTT * 128, 32], F32)
                    kb._rt_actd = kb.dram("rt_actd", [NTT * 128, 32], F32)
                kb.stq(kb._rt_lgd[tt * 128:(tt + 1) * 128, :], LG[:], tlg)
            if mode == "l0":
                kb.wait(kb.S, tlg)
                kb.S.activation(SCR[:], LG[:], AF.Exp)
                kb.S.activation(SCR[:], SCR[:], AF.Ln, bias=com["cpool"][:, 0:1])
                tact = kb.inc(kb.S.activation(ACT[:], SCR[:], AF.Sqrt), "sS")
                if "rt_lgd" in DEBUG_OUTPUTS:
                    kb.stq(kb._rt_actd[tt * 128:(tt + 1) * 128, :], ACT[:], tact)
                kb.wait(kb.V, tact)
                base = ACT
            else:
                base = LG
            def vedge(ins):
                t = kb.inc(ins, "sV")
                kb.wait(kb.V, t)
                return t
            for s in range(4):
                ins = kb.V.reduce_max(M1[:, s:s + 1], base[:, s * 8:(s + 1) * 8], AX.X)
            vedge(ins)
            for s in range(4):
                sl = slice(s * 8, (s + 1) * 8)
                ins = kb.V.tensor_scalar(SCR[:, sl], base[:, sl], M1[:, s:s + 1],
                                         None, ALU.is_ge)
            vedge(ins)
            for s in range(4):
                sl = slice(s * 8, (s + 1) * 8)
                ins = kb.V.scalar_tensor_tensor(SCR[:, sl], SCR[:, sl], -1e30, base[:, sl],
                                                ALU.mult, ALU.add)
            vedge(ins)
            for s in range(4):
                ins = kb.V.reduce_max(M2[:, s:s + 1], SCR[:, s * 8:(s + 1) * 8], AX.X)
            vedge(ins)
            kb.wait(kb.V, prev.get("ctT"))
            for s in range(4):
                sl = slice(s * 8, (s + 1) * 8)
                ins = kb.V.tensor_scalar(MK[:, sl], base[:, sl], M2[:, s:s + 1],
                                         None, ALU.is_ge)
            vedge(ins)
            for s in range(4):
                sl = slice(s * 8, (s + 1) * 8)
                if mode == "l0":
                    ins_cmb = kb.V.tensor_mul(CMB[:, sl], base[:, sl], MK[:, sl])
            if mode == "l1":
                tv = kb.inc(kb.V.tensor_sub(M2[:], M2[:], M1[:]), "sV")
                kb.wait(kb.S, tv)
                tde = kb.inc(kb.S.activation(DEN[:], M2[:], AF.Exp), "sS")
                for s in range(4):
                    sl = slice(s * 8, (s + 1) * 8)
                    ins = kb.V.tensor_scalar(SCR[:, sl], base[:, sl], M1[:, s:s + 1], None,
                                             ALU.subtract)
                tsu = kb.inc(ins, "sV")
                kb.wait(kb.S, tsu)
                tex = kb.inc(kb.S.activation(SCR[:], SCR[:], AF.Exp), "sS")
                kb.wait(kb.V, tex, tde)
                ta1 = kb.inc(kb.V.tensor_scalar_add(DEN[:], DEN[:], 1.0), "sV")
                kb.wait(kb.V, ta1)
                trc = kb.inc(kb.V.reciprocal(DEN[:], DEN[:]), "sV")
                kb.V.tensor_mul(SCR[:], SCR[:], MK[:])
                kb.wait(kb.V, trc)
                for s in range(4):
                    sl = slice(s * 8, (s + 1) * 8)
                    ins_cmb = kb.V.tensor_scalar(CMB[:, sl], SCR[:, sl], DEN[:, s:s + 1],
                                                 None, ALU.mult)
            tcmb = kb.inc(ins_cmb, "sV")
            if "rt_lgd" in DEBUG_OUTPUTS:
                if tt == 0 and not hasattr(kb, "_rt_cmbd"):
                    kb._rt_cmbd = kb.dram("rt_cmbd", [NTT * 128, 32], F32)
                kb.stq(kb._rt_cmbd[tt * 128:(tt + 1) * 128, :], CMB[:], tcmb)
            for s in range(4):
                kb.wait(kb.T, tcmb, prev.get("ct"))
                ttr = kb.inc(kb.T.transpose(psT[:], CMB[:, s * 8:(s + 1) * 8],
                                            com["ident"][:]), "sT")
                kb.wait(kb.V, ttr)
                tev = kb.inc(kb.V.tensor_copy(
                    combT[:, tt * TT + s * 128: tt * TT + (s + 1) * 128], psT[:]), "sV")
                prev["ct"] = tev
                prev["ctT"] = ttr
            prev["evac"] = tlg
            prev[("xb", par)] = kb.tok("sT")
        kb.barrier()



def phase_moe(kb, com, items, act_fn, out_ap, combT, bias2=None, accum_all=False,
              mode="full", st=None, fstore=None, dbw1=False):
    nc = kb.nc
    nIc_max = max(it["w1"].shape[-1] for it in items) // 128
    g_hc = 2 if mode == "full" else 1
    n_psY = 1 if mode == "slim" else 2
    share_cb = (mode != "full")
    with (ExitStack() if st is None else nullctx(st)) as st:
        sb = lambda n, s, d: st.enter_context(nc.sbuf_tensor(uniq(n), s, d))
        pst = lambda n, s: st.enter_context(nc.psum_tensor(uniq(n), s, F32))
        W1B = [sb(f"mo_w1{p}", [128, NCH, nIc_max * 128], BF16)
               for p in range(2 if dbw1 else 1)]
        W2B = sb("mo_w2", [128, nIc_max, D], BF16)
        XB = [sb(f"mo_xb{p}", [128, NCH, TT], BF16) for p in range(2)]
        HS = sb("mo_hs", [128, nIc_max, TT], BF16)
        CB = [sb(f"mo_cb{p}", [128, TT], F32) for p in range(2)]
        if act_fn == "silu_exp":
            moE = [sb(f"mo_e{p}", [128, g_hc, TT], F32) for p in range(2)]
            moT = [sb(f"mo_t{p}", [128, g_hc, TT], F32) for p in range(2)]
        YS = [sb(f"mo_ys{p}", [128, NCH, TT], F32) for p in range(2)]
        psH = pst("mo_psH", [128, 512 * g_hc * 2])
        psY = [pst(f"mo_psY{p}", [128, TT]) for p in range(n_psY)]
        psCB = psY[0] if share_cb else pst("mo_psCB", [128, TT])
        out_r = out_ap.rearrange("(c p) t -> p c t", p=128)
        if fstore is None:
            fstore = {}
        prev = {}
        w1tok = {}

        def load_w1(i):
            if i >= len(items) or i in w1tok:
                return
            nIc_i = items[i]["w1"].shape[-1] // 128
            wp = i % 2 if dbw1 else 0
            w1tok[i] = kb.ldp(W1B[wp][:, :, 0:nIc_i * 128],
                              items[i]["w1"].rearrange("(c p) i -> p c i", p=128),
                              prev.get(("w1", wp)))

        load_w1(0)
        for it_i, it in enumerate(items):
            nIc = it["w1"].shape[-1] // 128
            wp = it_i % 2 if dbw1 else 0
            tw1 = w1tok[it_i]
            W1C = W1B[wp]
            tw2 = kb.ldp(W2B[:, 0:nIc, :],
                         it["w2"].rearrange("(c p) j -> p c j", p=128),
                         prev.get("w2"))
            in_r = it["inT"].rearrange("(c p) t -> p c t", p=128)
            if dbw1:
                load_w1(it_i + 1)
            for tt in range(NTT):
                par = tt % 2
                ts = slice(tt * TT, (tt + 1) * TT)
                tl = kb.ldq(XB[par][:], in_r[:, :, ts], prev.get(("xb", par)))
                if it["cb"] is not None:
                    e = it["cb"]
                    kb.wait(kb.T, prev.get("cbps"), prev.get(("y", 0)) if share_cb else None)
                    tcb = kb.inc(kb.T.matmul(psCB[:], com["sel8"][:, e * 128:(e + 1) * 128],
                                             combT[:, ts], start=True, stop=True), "sT")
                    kb.wait(kb.V, tcb)
                    tcbe = kb.inc(kb.V.tensor_copy(CB[par][:], psCB[:]), "sV")
                    prev["cbps"] = tcbe
                    if share_cb:
                        prev[("y", 0)] = tcbe
                else:
                    tcbe = None
                b1t, b1c = it["b1"]
                for g in range(ceil_div(nIc, g_hc)):
                    hcs = range(g * g_hc, min(nIc, g * g_hc + g_hc))
                    ph = g % 2
                    kb.wait(kb.T, tl, tw1, prev.get(("h", ph)))
                    for hi, hc in enumerate(hcs):
                        o0 = ph * g_hc * 512 + hi * 512
                        for ch in range(NCH):
                            ins = kb.T.matmul(psH[:, o0:o0 + TT],
                                              W1C[:, ch, hc * 128:(hc + 1) * 128],
                                              XB[par][:, ch, :],
                                              start=(ch == 0), stop=(ch == NCH - 1))
                    tmm = kb.inc(ins, "sT")
                    kb.wait(kb.S, tmm)
                    if act_fn == "silu_exp":
                        nb1t, _ = it["b1n"]
                        EE = moE[ph]
                        TTt = moT[ph]
                        kb.wait(kb.S, prev.get(("hv", ph)))
                        for hi, hc in enumerate(hcs):
                            o0 = ph * g_hc * 512 + hi * 512
                            ins = kb.S.activation(EE[:, hi, :], psH[:, o0:o0 + TT], AF.Exp,
                                                  scale=-1.0,
                                                  bias=nb1t[:, b1c + hc:b1c + hc + 1])
                        tact_s = kb.inc(ins, "sS")
                        kb.wait(kb.V, tact_s, prev.get(("hv", ph)))
                        for hi, hc in enumerate(hcs):
                            o0 = ph * g_hc * 512 + hi * 512
                            kb.V.tensor_scalar(TTt[:, hi, :], psH[:, o0:o0 + TT],
                                               b1t[:, b1c + hc:b1c + hc + 1], None,
                                               ALU.add)
                            kb.V.tensor_add(EE[:, hi, :], EE[:, hi, :], com["ones512"][:])
                            ins = kb.V.reciprocal(EE[:, hi, :], EE[:, hi, :])
                        tv2 = kb.inc(ins, "sV")
                        kb.wait(kb.V, tv2)
                        for hi, hc in enumerate(hcs):
                            ins = kb.V.tensor_mul(HS[:, hc, :], TTt[:, hi, :], EE[:, hi, :])
                        tact = kb.inc(ins, "sV")
                        prev[("hv", ph)] = tact
                    else:
                        for hi, hc in enumerate(hcs):
                            o0 = ph * g_hc * 512 + hi * 512
                            ins = kb.S.activation(HS[:, hc, :], psH[:, o0:o0 + TT],
                                                  act_fn, bias=b1t[:, b1c + hc:b1c + hc + 1])
                        tact = kb.inc(ins, "sS")
                    prev[("h", ph)] = tact
                    yield
                for jc in range(NCH):
                    yp = jc % n_psY
                    kb.wait(kb.T, tw2, prev.get(("y", yp)), kb.tok("sS"),
                            kb.tok("sV") if act_fn == "silu_exp" else None)
                    for ic in range(nIc):
                        ins = kb.T.matmul(psY[yp][:], W2B[:, ic, jc * 128:(jc + 1) * 128],
                                          HS[:, ic, :], start=(ic == 0), stop=(ic == nIc - 1))
                    tmy = kb.inc(ins, "sT")
                    kb.wait(kb.V, tmy, tcbe, prev.get(("ysw", par)) if jc == 0 else None)
                    if it["cb"] is not None:
                        tev = kb.inc(kb.V.tensor_mul(YS[par][:, jc, :], psY[yp][:],
                                                     CB[par][:]), "sV")
                    else:
                        tev = kb.inc(kb.V.tensor_copy(YS[par][:, jc, :], psY[yp][:]), "sV")
                    prev[("y", yp)] = tev
                    if jc % 2 == 1:
                        yield
                tstore = kb.ldp(out_r[:, :, ts], YS[par][:], tev,
                                fstore.get(tt), accum=(accum_all or it_i > 0))
                fstore[tt] = tstore
                prev[("ysw", par)] = tstore
                prev[("xb", par)] = kb.tok("sT")
            prev[("w1", wp)] = kb.tok("sT")
            prev["w2"] = kb.tok("sT")
            if not dbw1:
                load_w1(it_i + 1)
        if bias2 is not None:
            B2 = sb("mo_b2", [8, D], BF16)
            tb2 = kb.ldp(B2[:], bias2["lhsT"])
            for tt in range(NTT):
                par = tt % 2
                ts = slice(tt * TT, (tt + 1) * TT)
                for jc in range(NCH):
                    yp = jc % n_psY
                    kb.wait(kb.T, tb2, prev.get(("y", yp)),
                            prev.get(("ysw", par)) if jc == 0 else None, kb.tok("sV"))
                    tmy = kb.inc(kb.T.matmul(psY[yp][:], B2[:, jc * 128:(jc + 1) * 128],
                                             combT[:, ts], start=True, stop=True), "sT")
                    if bias2.get("extra") is not None:
                        xt, xc = bias2["extra"]
                        kb.wait(kb.S, tmy)
                        tev = kb.inc(kb.S.activation(YS[par][:, jc, :], psY[yp][:],
                                                     AF.Identity,
                                                     bias=xt[:, jc + xc:jc + xc + 1]), "sS")
                        kb.wait(kb.V, tev)
                    else:
                        kb.wait(kb.V, tmy)
                        tev = kb.inc(kb.V.tensor_copy(YS[par][:, jc, :], psY[yp][:]), "sV")
                    prev[("y", yp)] = tev
                    if jc % 2 == 1:
                        yield
                kb.wait(kb.V, kb.tok("sS"))
                tstore = kb.ldp(out_r[:, :, ts], YS[par][:], kb.tok("sS"), kb.tok("sV"),
                                fstore.get(tt), accum=True)
                fstore[tt] = tstore
                prev[("ysw", par)] = tstore


def run_gen(*gens, weights=None):
    gens = [g for g in gens if g is not None]
    if weights is None:
        weights = [1] * len(gens)
    weights = list(weights[:len(gens)])
    while gens:
        nxt, nw = [], []
        for g, w in zip(gens, weights):
            alive = True
            for _ in range(w):
                try:
                    next(g)
                except StopIteration:
                    alive = False
                    break
            if alive:
                nxt.append(g); nw.append(w)
        gens, weights = nxt, nw



def phase_l0_merge(kb, com, xT, yrT, recT, FT, wo, x1T):
    nc = kb.nc
    with ExitStack() as st:
        sb = lambda n, s, d: st.enter_context(nc.sbuf_tensor(uniq(n), s, d))
        pst = lambda n, s: st.enter_context(nc.psum_tensor(uniq(n), s, F32))
        WO = sb("p3_wo", [128, NCH, D], BF16)
        YR = [sb(f"p3_yr{p}", [128, NCH, TT], BF16) for p in range(2)]
        YN = [sb(f"p3_yn{p}", [128, NCH, TT], BF16) for p in range(2)]
        XF = [sb(f"p3_xf{p}", [128, NCH, TT], F32) for p in range(2)]
        FF = [sb(f"p3_ff{p}", [128, NCH, TT], F32) for p in range(2)]
        RC2 = sb("p3_rc", [2, TT], F32)
        RCB = sb("p3_rcb", [2, TT], BF16)
        EV = [sb(f"p3_ev{p}", [128, TT], F32) for p in range(2)]
        X1 = [sb(f"p3_x1{p}", [128, NCH, TT], F32) for p in range(2)]
        psA = [pst(f"p3_psA{p}", [128, TT]) for p in range(2)]
        psB = pst("p3_psB", [128, TT])
        tw = kb.ldp(WO[:], wo.rearrange("(c p) j -> p c j", p=128))
        yr_r = yrT.rearrange("(c p) t -> p c t", p=128)
        x_r = xT.rearrange("(c p) t -> p c t", p=128)
        f_r = FT.rearrange("(c p) t -> p c t", p=128)
        x1_r = x1T.rearrange("(c p) t -> p c t", p=128)
        prev = {}
        for tt in range(NTT):
            par = tt % 2
            ts = slice(tt * TT, (tt + 1) * TT)
            tly = kb.ldq(YR[par][:], yr_r[:, :, ts], prev.get(("yr", par)))
            tlx = kb.ldq(XF[par][:], x_r[:, :, ts], prev.get(("x1", par)))
            tlf = kb.ldq(FF[par][:], f_r[:, :, ts], prev.get(("x1", par)))
            for ch in range(NCH):
                tlr = kb.ldq(RC2[:], recT[2 * ch:2 * ch + 2, ts], prev.get("rc2"))
                kb.wait(kb.V, tlr, prev.get("rcb"))
                trb = kb.inc(kb.V.tensor_copy(RCB[:], RC2[:]), "sV")
                prev["rc2"] = trb
                kb.wait(kb.T, trb, prev.get("psBr"))
                tbc = kb.inc(kb.T.matmul(psB[:], com["hsel"][:], RCB[:],
                                         start=True, stop=True), "sT")
                prev["rcb"] = tbc
                kb.wait(kb.V, tbc, tly)
                tyn = kb.inc(kb.V.tensor_mul(YN[par][:, ch, :], YR[par][:, ch, :],
                                             psB[:]), "sV")
                prev["psBr"] = tyn
            prev[("yr", par)] = tyn
            for jc in range(NCH):
                yp = jc % 2
                kb.wait(kb.T, tw, tyn, prev.get(("psA", yp)))
                for ch in range(NCH):
                    ins = kb.T.matmul(psA[yp][:], WO[:, ch, jc * 128:(jc + 1) * 128],
                                      YN[par][:, ch, :], start=(ch == 0), stop=(ch == NCH - 1))
                tmm = kb.inc(ins, "sT")
                kb.wait(kb.V, tmm, tlx, prev.get(("ev", yp)))
                tv1 = kb.inc(kb.V.tensor_add(EV[yp][:], psA[yp][:], XF[par][:, jc, :]), "sV")
                prev[("psA", yp)] = tv1
                kb.wait(kb.P, tv1, tlf, prev.get(("x1w", par)))
                tv2 = kb.inc(kb.P.tensor_add(X1[par][:, jc, :], EV[yp][:], FF[par][:, jc, :]), "sP")
                prev[("ev", yp)] = tv2
            tst = kb.stq(x1_r[:, :, ts], X1[par][:], kb.tok("sP"))
            prev[("x1", par)] = tst
            prev[("x1w", par)] = tst
            prev[("yn", par)] = kb.tok("sT")
        kb.barrier()



def phase_l1_proj(kb, com, zT, Wq, Wk, Wv, Wg, Wgk1, Wgk2, q1T, k1T, g1T, v1, wT):
    nc = kb.nc
    with ExitStack() as st:
        sb = lambda n, s, d: st.enter_context(nc.sbuf_tensor(uniq(n), s, d))
        pst = lambda n, s: st.enter_context(nc.psum_tensor(uniq(n), s, F32))
        WQB = sb("p7_wq", [128, NCH, D], BF16)
        WKB = sb("p7_wk", [128, NCH, D], BF16)
        WVB = sb("p7_wv", [128, NCH, D], BF16)
        WGB = sb("p7_wg", [128, NCH, D], BF16)
        WK1 = sb("p7_wk1", [128, NCH, 16], BF16)
        WK2 = sb("p7_wk2", [16, D], BF16)
        ZB = [sb(f"p7_zb{p}", [128, NCH, TT], BF16) for p in range(2)]
        OB = [sb(f"p7_ob{p}", [128, TT], BF16) for p in range(4)]
        T1B = [sb(f"p7_t1{p}", [16, TT], BF16) for p in range(2)]
        WO_ = [sb(f"p7_wo{p}", [128, TT], F32) for p in range(2)]
        psA = [pst(f"p7_psA{p}", [128, TT]) for p in range(2)]
        psG = [pst(f"p7_psG{p}", [16, TT]) for p in range(2)]
        tw = []
        for Wb, w in ((WQB, Wq), (WKB, Wk), (WVB, Wv), (WGB, Wg)):
            tw.append(kb.ldp(Wb[:], w.rearrange("(c p) j -> p c j", p=128)))
        tw1 = kb.ldp(WK1[:], Wgk1.rearrange("(c p) j -> p c j", p=128))
        tw2 = kb.ldp(WK2[:], Wgk2)
        z_r = zT.rearrange("(c p) t -> p c t", p=128)
        outs_r = [o.rearrange("(c p) t -> p c t", p=128) for o in (q1T, k1T, g1T)]
        w_r = wT.rearrange("(c p) t -> p c t", p=128)
        nbg = com["vecs"]["nbgk2"]
        prev = {}
        ob_i = 0
        for tt in range(NTT):
            par = tt % 2
            ts = slice(tt * TT, (tt + 1) * TT)
            tl = kb.ldq(ZB[par][:], z_r[:, :, ts], prev.get(("zb", par)))
            for wi, (Wb, out_r) in enumerate(((WQB, outs_r[0]), (WKB, outs_r[1]),
                                              (WGB, outs_r[2]))):
                for jc in range(NCH):
                    yp = jc % 2
                    osl = ob_i % 4
                    ob_i += 1
                    kb.wait(kb.T, tl, tw[wi if wi < 2 else 3], prev.get(("psA", yp)))
                    for ch in range(NCH):
                        ins = kb.T.matmul(psA[yp][:], Wb[:, ch, jc * 128:(jc + 1) * 128],
                                          ZB[par][:, ch, :], start=(ch == 0),
                                          stop=(ch == NCH - 1))
                    tmm = kb.inc(ins, "sT")
                    kb.wait(kb.V, tmm, prev.get(("ob", osl)))
                    tev = kb.inc(kb.V.tensor_copy(OB[osl][:], psA[yp][:]), "sV")
                    prev[("psA", yp)] = tev
                    tst = kb.stq(out_r[:, jc, ts], OB[osl][:], tev)
                    prev[("ob", osl)] = tst
            for s4 in range(4):
                for half in range(2):
                    yp = (s4 * 2 + half) % 2
                    osl = ob_i % 4
                    ob_i += 1
                    kb.wait(kb.T, tl, tw[2], prev.get(("psA", yp)))
                    for ch in range(NCH):
                        ins = kb.T.matmul(psA[yp][:], ZB[par][:, ch, s4 * 128:(s4 + 1) * 128],
                                          WVB[:, ch, half * 512:(half + 1) * 512],
                                          start=(ch == 0), stop=(ch == NCH - 1))
                    tmm = kb.inc(ins, "sT")
                    kb.wait(kb.V, tmm, prev.get(("ob", osl)))
                    tev = kb.inc(kb.V.tensor_copy(OB[osl][:], psA[yp][:]), "sV")
                    prev[("psA", yp)] = tev
                    row = tt * TT + s4 * 128
                    tst = kb.stq(v1[row:row + 128, half * 512:(half + 1) * 512],
                                 OB[osl][:], tev)
                    prev[("ob", osl)] = tst
            kb.wait(kb.T, tl, tw1, prev.get(("psG", par)))
            for ch in range(NCH):
                ins = kb.T.matmul(psG[par][:], WK1[:, ch, :], ZB[par][:, ch, :],
                                  start=(ch == 0), stop=(ch == NCH - 1))
            tg1 = kb.inc(ins, "sT")
            kb.wait(kb.V, tg1, prev.get(("t1b", par)))
            tt1 = kb.inc(kb.V.tensor_copy(T1B[par][:], psG[par][:]), "sV")
            prev[("psG", par)] = tt1
            for jc in range(NCH):
                yp = jc % 2
                kb.wait(kb.T, tt1, tw2, prev.get(("psA", yp)))
                tmm = kb.inc(kb.T.matmul(psA[yp][:], WK2[:, jc * 128:(jc + 1) * 128],
                                         T1B[par][:], start=True, stop=True), "sT")
                kb.wait(kb.S, tmm, prev.get(("wo", yp)))
                kb.S.activation(WO_[yp][:], psA[yp][:], AF.Exp, scale=-1.0,
                                bias=nbg[:, jc:jc + 1])
                tso = kb.inc(kb.S.activation(WO_[yp][:], WO_[yp][:], AF.Ln,
                                             bias=com["cpool"][:, 0:1]), "sS")
                prev[("psA", yp)] = tso
                tst = kb.stq(w_r[:, jc, ts], WO_[yp][:], tso)
                prev[("wo", yp)] = tst
            prev[("t1b", par)] = kb.tok("sT")
            prev[("zb", par)] = kb.tok("sT")
        kb.barrier()



def phase_gla(kb, com, q1T, k1T, v1, wT, oT, st=None):
    nc = kb.nc
    H1, DK = 4, 256
    NC = T // 128
    with (ExitStack() if st is None else nullctx(st)) as st:
        sb = lambda n, s, d: st.enter_context(nc.sbuf_tensor(uniq(n), s, d))
        pst2 = lambda n, s, d: st.enter_context(nc.psum_tensor(uniq(n), s, d))
        QH = [sb(f"g_qh{p}", [128, 2, 128], BF16) for p in range(2)]
        KH = [sb(f"g_kh{p}", [128, 2, 128], BF16) for p in range(2)]
        WH = [sb(f"g_wh{p}", [128, 2, 128], F32) for p in range(2)]
        VH = [sb(f"g_vh{p}", [128, 256], BF16) for p in range(2)]
        GW = [sb(f"g_gw{p}", [128, 2, 128], F32) for p in range(2)]
        A_ = [sb(f"g_a{p}", [128, 2, 128], F32) for p in range(2)]
        AI = [sb(f"g_ai{p}", [128, 2, 128], F32) for p in range(2)]
        QS = [sb(f"g_qs{p}", [128, 2, 128], BF16) for p in range(2)]
        KS = [sb(f"g_ks{p}", [128, 2, 128], BF16) for p in range(2)]
        KE = [sb(f"g_ke{p}", [128, 2, 128], BF16) for p in range(2)]
        KTok = [sb(f"g_kt{p}", [128, 2, 128], BF16) for p in range(2)]
        PT = [sb(f"g_pt{p}", [128, 128], BF16) for p in range(2)]
        OB = [sb(f"g_ob{p}", [128, 2, 128], F32) for p in range(2)]
        HS = [sb(f"g_h{dc}", [128, 256], F32) for dc in range(2)]
        HB = [sb(f"g_hb{dc}", [128, 256], BF16) for dc in range(2)]
        psS = pst2(f"g_psS", [128, 128], F32)
        psO = pst2(f"g_psO", [128, 256], F32)
        psD = pst2(f"g_psD", [128, 512], F32)
        psK = pst2(f"g_psK", [128, 2, 128], BF16)
        q_r = q1T.rearrange("(c p) t -> p c t", p=128)
        k_r = k1T.rearrange("(c p) t -> p c t", p=128)
        w_r = wT.rearrange("(c p) t -> p c t", p=128)
        o_r = oT.rearrange("(c p) t -> p c t", p=128)
        v_r = v1.rearrange("(c p) j -> p c j", p=128)
        prev = {}
        for h in range(H1):
            hc0 = 2 * h
            for c in range(NC):
                par = c % 2
                ts = slice(c * 128, (c + 1) * 128)
                fq = prev.get(("qh", par))
                tlq = kb.ldq(QH[par][:], q_r[:, hc0:hc0 + 2, ts], fq)
                tlk = kb.ldq(KH[par][:], k_r[:, hc0:hc0 + 2, ts], fq)
                tlw = kb.ldq(WH[par][:], w_r[:, hc0:hc0 + 2, ts], fq)
                tlv = kb.ldq(VH[par][:], v_r[:, c, 256 * h:256 * h + 256], fq)
                kb.wait(kb.V, tlw, prev.get(("gw", par)))
                for dc in range(2):
                    ins = kb.V.tensor_tensor_scan(GW[par][:, dc, :], WH[par][:, dc, :],
                                                  WH[par][:, dc, :], 0.0,
                                                  ALU.add, ALU.bypass)
                tsc = kb.inc(ins, "sV")
                kb.wait(kb.S, tsc, prev.get(("a", par)))
                kb.S.activation(A_[par][:], GW[par][:], AF.Exp, scale=-1.0 / 16)
                texp = kb.inc(kb.S.activation(AI[par][:], GW[par][:], AF.Exp,
                                              scale=1.0 / 16), "sS")
                kb.wait(kb.P, texp, tlk, prev.get(("qs", par)))
                for dc in range(2):
                    ins = kb.P.tensor_mul(KS[par][:, dc, :], KH[par][:, dc, :],
                                          AI[par][:, dc, :])
                tks = kb.inc(ins, "sP")
                kb.wait(kb.V, texp, tlq, tks, prev.get(("qs", par)))
                for dc in range(2):
                    kb.V.scalar_tensor_tensor(QS[par][:, dc, :], A_[par][:, dc, :],
                                              1.0 / 16, QH[par][:, dc, :],
                                              ALU.mult, ALU.mult)
                    ins = kb.V.tensor_scalar(KE[par][:, dc, :], KS[par][:, dc, :],
                                             A_[par][:, dc, 127:128], None, ALU.mult)
                tqk = kb.inc(ins, "sV")
                kb.wait(kb.T, tqk, prev.get("psK"))
                for dc in range(2):
                    ins = kb.T.transpose(psK[:, dc, :], KE[par][:, dc, :],
                                         com["identb"][:])
                tktr = kb.inc(ins, "sT")
                kb.wait(kb.V, tktr, prev.get(("ktok", par)))
                tkt = kb.inc(kb.V.tensor_copy(KTok[par][:], psK[:]), "sV")
                prev["psK"] = tkt
                kb.wait(kb.T, tqk, prev.get("psS"))
                for dc in range(2):
                    ins = kb.T.matmul(psS[:], KS[par][:, dc, :], QS[par][:, dc, :],
                                      start=(dc == 0), stop=(dc == 1))
                tsm = kb.inc(ins, "sT")
                kb.wait(kb.V, tsm, prev.get(("pt", par)))
                tpt = kb.inc(kb.V.tensor_mul(PT[par][:], psS[:], com["tri"][:]), "sV")
                prev["psS"] = tpt
                kb.wait(kb.T, tpt, tlv, prev.get("psO"))
                if c > 0:
                    kb.wait(kb.T, prev.get("hb"))
                for dv in range(2):
                    ins = kb.T.matmul(psO[:, dv * 128:(dv + 1) * 128],
                                      VH[par][:, dv * 128:(dv + 1) * 128], PT[par][:],
                                      start=True, stop=(c == 0))
                    if c > 0:
                        for dc in range(2):
                            ins = kb.T.matmul(psO[:, dv * 128:(dv + 1) * 128],
                                              HB[dc][:, dv * 128:(dv + 1) * 128],
                                              QS[par][:, dc, :],
                                              start=False, stop=(dc == 1))
                to = kb.inc(ins, "sT")
                kb.wait(kb.V, to, prev.get(("obv", par)))
                tev = kb.inc(kb.V.tensor_copy(OB[par][:], psO[:]), "sV")
                prev["psO"] = tev
                tso = kb.stq(o_r[:, hc0:hc0 + 2, ts], OB[par][:], tev)
                prev[("obv", par)] = tso
                kb.wait(kb.T, tkt, tlv, prev.get("psD"))
                for dc in range(2):
                    ins = kb.T.matmul(psD[:, dc * 256:(dc + 1) * 256],
                                      KTok[par][:, dc, :], VH[par][:],
                                      start=True, stop=True)
                tdh = kb.inc(ins, "sT")
                if c == 0:
                    kb.wait(kb.V, tdh, prev.get("hb_free"))
                    for dc in range(2):
                        ins = kb.V.tensor_copy(HS[dc][:], psD[:, dc * 256:(dc + 1) * 256])
                    thu = kb.inc(ins, "sV")
                else:
                    kb.wait(kb.V, texp, prev.get("hadd"))
                    for dc in range(2):
                        ins = kb.V.tensor_scalar(HS[dc][:], HS[dc][:],
                                                 A_[par][:, dc, 127:128], None, ALU.mult)
                    thscale = kb.inc(ins, "sV")
                    kb.wait(kb.V, thscale, tdh)
                    for dc in range(2):
                        ins = kb.V.tensor_add(HS[dc][:], HS[dc][:],
                                              psD[:, dc * 256:(dc + 1) * 256])
                    thu = kb.inc(ins, "sV")
                prev["psD"] = thu
                prev["hadd"] = thu
                kb.wait(kb.P, thu, prev.get("hb"), prev.get("hbT"))
                for dc in range(2):
                    ins = kb.P.tensor_copy(HB[dc][:], HS[dc][:])
                prev["hb"] = kb.inc(ins, "sP")
                prev["hbT"] = to
                prev[("qh", par)] = [kb.tok("sT"), kb.tok("sP")]
                prev[("gw", par)] = texp
                prev[("a", par)] = tqk if c == 0 else [tqk, thscale]
                prev[("qs", par)] = to
                prev[("ktok", par)] = tdh
                prev[("pt", par)] = to
                yield
            prev["hb_free"] = prev["hb"]



def phase_gate(kb, com, oT, g1T, ogT, st=None):
    nc = kb.nc
    with (ExitStack() if st is None else nullctx(st)) as st:
        sb = lambda n, s, d: st.enter_context(nc.sbuf_tensor(uniq(n), s, d))
        pst = lambda n, s: st.enter_context(nc.psum_tensor(uniq(n), s, F32))
        OF = [sb(f"gt_of{p}", [128, NCH, TT], F32) for p in range(2)]
        GB = sb("gt_gb", [128, NCH, TT], BF16)
        SQ = sb("gt_sq", [128, NCH, TT], BF16)
        SG = [sb(f"gt_sg{p}", [128, TT], F32) for p in range(4)]
        TM = [sb(f"gt_tm{p}", [128, TT], F32) for p in range(2)]
        OGB = sb("gt_og", [128, NCH, TT], BF16)
        SD = sb("gt_sd", [1, TT], F32)
        RB1 = sb("gt_rb1", [1, TT], BF16)
        psM = [pst(f"gt_psM{p}", [1, TT]) for p in range(2)]
        psB = [pst(f"gt_psB{p}", [128, TT]) for p in range(2)]
        o_r = oT.rearrange("(c p) t -> p c t", p=128)
        g_r = g1T.rearrange("(c p) t -> p c t", p=128)
        og_r = ogT.rearrange("(c p) t -> p c t", p=128)
        rmsw = com["vecs"]["rms_w"]
        prev = {}
        for tt in range(NTT):
            par = tt % 2
            ts = slice(tt * TT, (tt + 1) * TT)
            tlo = kb.ldq(OF[par][:], o_r[:, :, ts], prev.get(("of", par)))
            tlg = kb.ldq(GB[:], g_r[:, :, ts], prev.get("gbr"))
            kb.wait(kb.S, tlo, prev.get("sq"))
            tsq = kb.inc(kb.S.activation(SQ[:], OF[par][:], AF.Square), "sS")
            last = None
            for h in range(4):
                mp = h % 2
                kb.wait(kb.T, tsq, prev.get(("psM", mp)))
                for dc in range(2):
                    ins = kb.T.matmul(psM[mp][:], com["ones_b"][:],
                                      SQ[:, 2 * h + dc, :],
                                      start=(dc == 0), stop=(dc == 1))
                tst = kb.inc(ins, "sT")
                kb.wait(kb.S, tst, prev.get("sd"))
                tsd = kb.inc(kb.S.activation(SD[:], psM[mp][:], AF.Sqrt,
                                             scale=1.0 / 256, bias=com["cpool"][:1, 1:2]), "sS")
                kb.wait(kb.V, tsd, prev.get("rb1"))
                tre = kb.inc(kb.V.reciprocal(SD[:], SD[:]), "sV")
                kb.wait(kb.V, tre)
                trb = kb.inc(kb.V.tensor_copy(RB1[:], SD[:]), "sV")
                prev["sd"] = trb
                kb.wait(kb.T, trb, prev.get(("psB", mp)))
                tbc = kb.inc(kb.T.matmul(psB[mp][:], com["ones1_b"][0:1, 0:128],
                                         RB1[:], start=True, stop=True), "sT")
                prev["rb1"] = tbc
                prev[("psM", mp)] = tbc
                for dc in range(2):
                    ch = 2 * h + dc
                    sgl = (2 * h + dc) % 4
                    kb.wait(kb.S, tlg, prev.get(("sg", sgl)))
                    tsg = kb.inc(kb.S.activation(SG[sgl][:], GB[:, ch, :], AF.Silu), "sS")
                    kb.wait(kb.V, tbc, tsg, prev.get(("tm", dc)))
                    tv1 = kb.inc(kb.V.scalar_tensor_tensor(
                        TM[dc][:], OF[par][:, ch, :], rmsw[:, dc:dc + 1], psB[mp][:],
                        ALU.mult, ALU.mult), "sV")
                    prev[("psB", mp)] = tv1
                    kb.wait(kb.P, tv1, prev.get("og"))
                    ins = kb.P.tensor_mul(OGB[:, ch, :], TM[dc][:], SG[sgl][:])
                    last = kb.inc(ins, "sP")
                    prev[("tm", dc)] = last
                    prev[("sg", sgl)] = last
                yield
            tso = kb.stq(og_r[:, :, ts], OGB[:], last)
            prev["og"] = tso
            prev[("of", par)] = [kb.tok("sV"), kb.tok("sS")]
            prev["gbr"] = kb.tok("sS")
            prev["sq"] = kb.tok("sT")



def phase_l1_merge(kb, com, xhT, x1T, ogT, Wo, xpT, acc0T, st=None):
    nc = kb.nc
    with (ExitStack() if st is None else nullctx(st)) as st:
        sb = lambda n, s, d: st.enter_context(nc.sbuf_tensor(uniq(n), s, d))
        pst = lambda n, s: st.enter_context(nc.psum_tensor(uniq(n), s, F32))
        WOB = sb("m1_wo", [128, NCH, D], BF16)
        YB = [sb(f"m1_yb{p}", [128, NCH, TT], BF16) for p in range(2)]
        XH = sb("m1_xh", [128, NCH, TT], F32)
        X1 = sb("m1_x1", [128, NCH, TT], F32)
        XP = [sb(f"m1_xp{p}", [128, TT], F32) for p in range(2)]
        AC = [sb(f"m1_ac{p}", [128, TT], F32) for p in range(2)]
        psA = [pst(f"m1_psA{p}", [128, TT]) for p in range(2)]
        tw = kb.ldp(WOB[:], Wo.rearrange("(c p) j -> p c j", p=128))
        og_r = ogT.rearrange("(c p) t -> p c t", p=128)
        xh_r = xhT.rearrange("(c p) t -> p c t", p=128)
        x1_r = x1T.rearrange("(c p) t -> p c t", p=128)
        xp_r = xpT.rearrange("(c p) t -> p c t", p=128)
        ac_r = acc0T.rearrange("(c p) t -> p c t", p=128)
        prev = {}
        for tt in range(NTT):
            par = tt % 2
            ts = slice(tt * TT, (tt + 1) * TT)
            tly = kb.ldq(YB[par][:], og_r[:, :, ts], prev.get(("yb", par)))
            tlx = kb.ldq(XH[:], xh_r[:, :, ts], prev.get("xhr"))
            tl1 = kb.ldq(X1[:], x1_r[:, :, ts], prev.get("x1r"))
            ts2 = None
            for jc in range(NCH):
                yp = jc % 2
                osl = jc % 2
                kb.wait(kb.T, tw, tly, prev.get(("psA", yp)))
                for ch in range(NCH):
                    ins = kb.T.matmul(psA[yp][:], WOB[:, ch, jc * 128:(jc + 1) * 128],
                                      YB[par][:, ch, :], start=(ch == 0), stop=(ch == NCH - 1))
                tmm = kb.inc(ins, "sT")
                kb.wait(kb.V, tmm, tlx, prev.get(("xps", osl)))
                tv1 = kb.inc(kb.V.tensor_add(XP[osl][:], psA[yp][:],
                                             XH[:, jc, :]), "sV")
                prev[("psA", yp)] = tv1
                kb.wait(kb.P, tv1, tl1, prev.get(("acs", osl)))
                tv2 = kb.inc(kb.P.tensor_add(AC[osl][:], XP[osl][:],
                                             X1[:, jc, :]), "sP")
                prev[("xps", osl)] = kb.stq(xp_r[:, jc, ts], XP[osl][:], tv2)
                ts2 = kb.stq(ac_r[:, jc, ts], AC[osl][:], tv2)
                prev[("acs", osl)] = ts2
                if jc % 4 == 3:
                    yield
            prev[("yb", par)] = kb.tok("sT")
            prev["xhr"] = kb.tok("sV")
            prev["x1r"] = kb.tok("sP")



def phase_mlp(kb, com, z2T, acc0T, W1, W2, outT, st=None):
    nc = kb.nc
    NHH = 8
    with (ExitStack() if st is None else nullctx(st)) as st:
        sb = lambda n, s, d: st.enter_context(nc.sbuf_tensor(uniq(n), s, d))
        pst = lambda n, s: st.enter_context(nc.psum_tensor(uniq(n), s, F32))
        W1B = sb("ml_w1", [128, NCH, NHH * 128], BF16)
        W2B = sb("ml_w2", [128, NHH, D], BF16)
        ZB = sb("ml_zb", [128, NCH, TT], BF16)
        HSB = sb("ml_hs", [128, NHH, TT], BF16)
        AC = sb("ml_ac", [128, NCH, TT], F32)
        OU = [sb(f"ml_ou{p}", [128, TT], F32) for p in range(2)]
        psH = pst("ml_psH", [128, 1024])
        psY = pst("ml_psY", [128, TT])
        z_r = z2T.rearrange("(c p) t -> p c t", p=128)
        ac_r = acc0T.rearrange("(c p) t -> p c t", p=128)
        out_r = outT.rearrange("(c p) t -> p c t", p=128)
        b1t = com["vecs"]["b1_mlp"]
        b2t = com["vecs"]["b2_mlp"]
        prev = {}
        for half in range(3):
            h0 = half * NHH
            tw1 = kb.ldp(W1B[:], W1[:, h0 * 128:(h0 + NHH) * 128]
                         .rearrange("(c p) i -> p c i", p=128), prev.get("w1"))
            tw2 = kb.ldp(W2B[:], W2[h0 * 128:(h0 + NHH) * 128, :]
                         .rearrange("(c p) j -> p c j", p=128), prev.get("w2"))
            for tt in range(NTT):
                par = tt % 2
                ts = slice(tt * TT, (tt + 1) * TT)
                tl = kb.ldq(ZB[:], z_r[:, :, ts], prev.get("zbr"))
                if half == 0:
                    tla = kb.ldq(AC[:], ac_r[:, :, ts], prev.get("acw"))
                for g in range(NHH):
                    hc = g
                    ph = g % 2
                    kb.wait(kb.T, tl, tw1, prev.get(("h", ph)))
                    for ch in range(NCH):
                        ins = kb.T.matmul(psH[:, ph * 512:ph * 512 + TT],
                                          W1B[:, ch, hc * 128:(hc + 1) * 128],
                                          ZB[:, ch, :],
                                          start=(ch == 0), stop=(ch == NCH - 1))
                    tmm = kb.inc(ins, "sT")
                    kb.wait(kb.S, tmm)
                    kb.S.activation(HSB[:, hc, :], psH[:, ph * 512:ph * 512 + TT],
                                    AF.Lrelu, alpha=0.01,
                                    bias=b1t[:, h0 + hc:h0 + hc + 1])
                    ins = kb.S.activation(HSB[:, hc, :], HSB[:, hc, :], AF.Square)
                    tact = kb.inc(ins, "sS")
                    prev[("h", ph)] = tact
                    if g % 2 == 1:
                        yield
                for jc in range(NCH):
                    ep = jc % 2
                    osl = jc % 2
                    kb.wait(kb.T, tw2, prev.get("ym"), kb.tok("sS"))
                    for ic in range(NHH):
                        ins = kb.T.matmul(psY[:], W2B[:, ic, jc * 128:(jc + 1) * 128],
                                          HSB[:, ic, :], start=(ic == 0), stop=(ic == NHH - 1))
                    tmy = kb.inc(ins, "sT")
                    if half == 0:
                        kb.wait(kb.V, tmy, tla, prev.get(("ou", osl)))
                        ins = kb.V.scalar_tensor_tensor(OU[osl][:], psY[:],
                                                        b2t[:, jc:jc + 1], AC[:, jc, :],
                                                        ALU.add, ALU.add)
                        tvo = kb.inc(ins, "sV")
                        prev["ym"] = tvo
                    else:
                        kb.wait(kb.V, tmy, prev.get(("ou", osl)))
                        tvo = kb.inc(kb.V.tensor_copy(OU[osl][:], psY[:]), "sV")
                        prev["ym"] = tvo
                    tst = kb.ldp(out_r[:, jc, ts], OU[osl][:], tvo,
                                 prev.get(("ost", tt, jc)), accum=(half > 0))
                    prev[("ou", osl)] = tst
                    prev[("ost", tt, jc)] = tst
                    if jc % 2 == 1:
                        yield
                if half == 0:
                    prev["acw"] = [kb.tok("sV")]
                prev["zbr"] = kb.tok("sT")
            prev["w1"] = kb.tok("sT")
            prev["w2"] = kb.tok("sT")



def phase_ue(kb, com, xhT, ueT, affines):
    nc = kb.nc
    NE = len(affines)
    with ExitStack() as st:
        sb = lambda n, s, d: st.enter_context(nc.sbuf_tensor(uniq(n), s, d))
        XH = [sb(f"ue_xh{p}", [128, NCH, TT], F32) for p in range(2)]
        UB = [sb(f"ue_ub{p}", [128, NCH, TT], BF16) for p in range(4)]
        xh_r = xhT.rearrange("(c p) t -> p c t", p=128)
        ue_r = ueT.rearrange("e (c p) t -> e p c t", p=128)
        prev = {}
        for tt in range(NTT):
            par = tt % 2
            ts = slice(tt * TT, (tt + 1) * TT)
            tl = kb.ldq(XH[par][:], xh_r[:, :, ts], prev.get(("xh", par)))
            for e in range(NE):
                ub = (tt * NE + e) % 4
                g_t, b_t = affines[e]
                kb.wait(kb.V, tl, prev.get(("ub", ub)))
                for ch in range(NCH):
                    ins = kb.V.tensor_scalar(UB[ub][:, ch, :], XH[par][:, ch, :],
                                             g_t[:, ch:ch + 1], b_t[:, ch:ch + 1],
                                             ALU.mult, ALU.add)
                tv = kb.inc(ins, "sV")
                tst = kb.stq(ue_r[e, :, :, ts], UB[ub][:], tv)
                prev[("ub", ub)] = tst
            prev[("xh", par)] = kb.tok("sV")
        kb.barrier()



PHASES_ALL = ["p0", "p1", "p2", "p4a", "p4", "p3",
              "p6", "p7a", "p7", "p7b", "p7c", "p7d",
              "p8a", "p8b", "p8c"]


def build_program(phases=None):
    phases = phases or PHASES_ALL
    kb = KB()
    nc = kb.nc
    with kb.stack:
        ei = lambda n, s, d=F32: nc.dram_tensor(n, s, d, kind="ExternalInput").ap()
        xT = ei("xT", [D, T])
        wq0 = ei("wq0", [D, D]); wk0 = ei("wk0", [D, D]); wv0 = ei("wv0", [D, D])
        wo0 = ei("wo0", [D, D])
        rwT0 = ei("rwT0", [D, 8])
        eb0 = ei("eb0", [1, 8])
        sW1 = ei("sW1", [D, 512]); sW2 = ei("sW2", [512, D])
        rW1 = ei("rW1", [8, D, 512]); rW2 = ei("rW2", [8, 512, D])
        sb1v = ei("sb1v", [512]); sb2v = ei("sb2v", [D])
        rb1v = ei("rb1v", [8, 512]); rb2m = ei("rb2m", [8, D])
        q1w = ei("q1w", [D, D]); k1w = ei("k1w", [D, D])
        v1w = ei("v1w", [D, D]); g1w = ei("g1w", [D, D])
        wgk1 = ei("wgk1", [D, 16]); wgk2 = ei("wgk2", [16, D])
        wo1 = ei("wo1", [D, D])
        w1m = ei("w1m", [D, 3072]); w2m = ei("w2m", [3072, D])
        rwT1 = ei("rwT1", [D, 8]); rb1row = ei("rb1row", [1, 8])
        w1l = ei("w1l", [8, D, 4096]); w2l = ei("w2l", [8, 4096, D])
        b2m8 = ei("b2m8", [8, D])
        ln1wv = ei("ln1wv", [D]); ln1bv = ei("ln1bv", [D])
        ln2wv = ei("ln2wv", [D]); ln2bv = ei("ln2bv", [D])
        bgk2v = ei("bgk2v", [D]); rmswv = ei("rmswv", [256])
        b1mv = ei("b1mv", [3072]); b2mv = ei("b2mv", [D])
        rngv = ei("rngv", [D]); rnbv = ei("rnbv", [D])
        lngv = ei("lngv", [8, D]); lnbv = ei("lnbv", [8, D])
        b1lv = ei("b1lv", [8, 4096])
        consts = declare_consts(nc)
        xn0T = kb.dram("xn0T", [D, T], BF16)
        qnT = kb.dram("qnT", [D, T], BF16)
        knT = kb.dram("knT", [D, T], BF16)
        v0 = kb.dram("v0", [T, 16 * 65], BF16)
        yrT = kb.dram("yrT", [D, T], BF16)
        recT = kb.dram("recT", [16, T], F32)
        FT = kb.dram("FT", [D, T], F32)
        x1T = kb.dram("x1T", [D, T], F32)
        xhT = kb.dram("xhT", [D, T], F32)
        zT = kb.dram("zT", [D, T], BF16)
        q1T = kb.dram("q1T", [D, T], BF16)
        k1T = kb.dram("k1T", [D, T], BF16)
        g1T = kb.dram("g1T", [D, T], BF16)
        v1 = kb.dram("v1", [T, D], BF16)
        wT = kb.dram("wT", [D, T], F32)
        oT = kb.dram("oT", [D, T], F32)
        ogT = kb.dram("ogT", [D, T], BF16)
        xpT = kb.dram("xpT", [D, T], F32)
        acc0T = kb.dram("acc0T", [D, T], F32)
        z2T = kb.dram("z2T", [D, T], BF16)
        ueT = kb.dram("ueT", [9, D, T], BF16)
        MaccT = kb.dram("MaccT", [D, T], F32)
        out_kind = "ExternalOutput"
        outT = nc.dram_tensor("outT", [D, T], F32, kind=out_kind).ap()

        vecs = {"sb1": (sb1v, 512), "sb2": (sb2v, D)}
        for e in range(8):
            vecs[f"rb1_{e}"] = (rb1v[e, :], 512)
        vecs.update({
            "ln1w": (ln1wv, D), "ln1b": (ln1bv, D),
            "ln2w": (ln2wv, D), "ln2b": (ln2bv, D),
            "bgk2": (bgk2v, D), "rms_w": (rmswv, 256),
            "b1_mlp": (b1mv, 3072), "b2_mlp": (b2mv, D),
            "rn_g": (rngv, D), "rn_b": (rnbv, D),
        })
        for e in range(8):
            vecs[f"lng_{e}"] = (lngv[e, :], D)
            vecs[f"lnb_{e}"] = (lnbv[e, :], D)
            vecs[f"b1l_{e}"] = (b1lv[e, :], 4096)
        com = prep_common(kb, vecs, consts)
        nbg = kb.sbuf("vec_nbgk2", [128, NCH], F32)
        tneg = kb.inc(kb.V.tensor_scalar_mul(nbg[:], com["vecs"]["bgk2"][:], -1.0), "sV")
        com["vecs"]["nbgk2"] = nbg
        for nm in ["sb1"] + [f"rb1_{e}" for e in range(8)]:
            nt = kb.sbuf("vec_n" + nm, [128, 4], F32)
            kb.inc(kb.V.tensor_scalar_mul(nt[:], com["vecs"][nm][:], -1.0), "sV")
            com["vecs"]["n" + nm] = nt
        combT0 = kb.sbuf("combT0", [8, T], BF16)
        combT1 = kb.sbuf("combT1", [8, T], BF16)
        kb.barrier()

        if "p0" in phases:
            run_gen(phase_ln(kb, com, xT, xn0T))
            kb.barrier()
        if "p1" in phases:
            phase_l0_proj(kb, com, xn0T, wq0, wk0, wv0, qnT, knT, v0)
        if "p4a" in phases:
            phase_router(kb, com, xn0T, rwT0, eb0, combT0, "l0")
            if "combT0d" in DEBUG_OUTPUTS:
                cd = kb.dram("combT0d", [8, T], BF16)
                kb.stq(cd, combT0[:])
                kb.barrier()
        g_attn = g_moe0 = None
        ilv_st = ExitStack()
        if "p2" in phases:
            g_attn = phase_l0_attn(kb, com, qnT, knT, v0, yrT, recT, st=ilv_st)
        if "p4" in phases:
            items = [dict(w1=sW1, w2=sW2, b1=(com["vecs"]["sb1"], 0),
                          b1n=(com["vecs"]["nsb1"], 0), inT=xn0T, cb=None)]
            if "shared_only" not in phases:
                for e in range(8):
                    items.append(dict(w1=rW1[e], w2=rW2[e],
                                      b1=(com["vecs"][f"rb1_{e}"], 0),
                                      b1n=(com["vecs"][f"nrb1_{e}"], 0),
                                      inT=xn0T, cb=e))
            bias2 = dict(lhsT=rb2m, extra=(com["vecs"]["sb2"], 0))
            if "no_bias2" in phases:
                bias2 = None
            act0 = "silu_exp" if "p2" in phases else AF.Silu
            g_moe0 = phase_moe(kb, com, items, act0, FT, combT0, bias2=bias2,
                               mode=("slim" if "p2" in phases else "full"),
                               st=ilv_st if "p2" in phases else None)
        if g_attn is not None or g_moe0 is not None:
            with ilv_st:
                run_gen(g_attn, g_moe0, weights=[4, 1])
            kb.barrier()
        if "p3" in phases:
            phase_l0_merge(kb, com, xT, yrT, recT, FT, wo0, x1T)
        if "seed_x1" in phases:
            kb.ldq(x1T, xT)
            kb.barrier()
        if "p6" in phases:
            run_gen(phase_ln(kb, com, x1T, zT, dst_f32=xhT,
                             affine=(com["vecs"]["ln1w"], com["vecs"]["ln1b"])))
            kb.barrier()
        if "p7a" in phases:
            phase_l1_proj(kb, com, zT, q1w, k1w, v1w, g1w, wgk1, wgk2,
                          q1T, k1T, g1T, v1, wT)
        if "p8a" in phases:
            affs = [(com["vecs"][f"lng_{e}"], com["vecs"][f"lnb_{e}"]) for e in range(8)]
            affs.append((com["vecs"]["rn_g"], com["vecs"]["rn_b"]))
            phase_ue(kb, com, xhT, ueT, affs)
        if "p8b" in phases:
            phase_router(kb, com, ueT[8], rwT1, rb1row, combT1, "l1")
            if "combT1d" in DEBUG_OUTPUTS:
                cd = kb.dram("combT1d", [8, T], BF16)
                kb.stq(cd, combT1[:])
                kb.barrier()

        ovl = False

        def l1_attn_chain():
            yield from phase_gla(kb, com, q1T, k1T, v1, wT, oT)
            kb.fence_loads()
            yield from phase_gate(kb, com, oT, g1T, ogT)
            kb.fence_loads()
            yield from phase_l1_merge(kb, com, xhT, x1T, ogT, wo1, xpT, acc0T)
            kb.fence_loads()
            yield from phase_ln(kb, com, xpT, z2T,
                                affine=(com["vecs"]["ln2w"], com["vecs"]["ln2b"]),
                                nbuf=1)
            kb.fence_loads()
            yield from phase_mlp(kb, com, z2T, acc0T, w1m, w2m, outT)

        def items_mid(e0, e1):
            items = []
            for e in range(e0, e1):
                for ib in range(4):
                    items.append(dict(
                        w1=w1l[e][:, ib * 1024:(ib + 1) * 1024],
                        w2=w2l[e][ib * 1024:(ib + 1) * 1024, :],
                        b1=(com["vecs"][f"b1l_{e}"], ib * 8),
                        inT=ueT[e], cb=e))
            return items

        def items_full(e0, e1):
            items = []
            for e in range(e0, e1):
                for ib in range(2):
                    items.append(dict(
                        w1=w1l[e][:, ib * 2048:(ib + 1) * 2048],
                        w2=w2l[e][ib * 2048:(ib + 1) * 2048, :],
                        b1=(com["vecs"][f"b1l_{e}"], ib * 16),
                        inT=ueT[e], cb=e))
            return items

        def moe1_gen(out_ap, accum_all):
            return phase_moe(kb, com, items_mid(0, 8), AF.Gelu, out_ap, combT1,
                             bias2=dict(lhsT=b2m8, extra=None), accum_all=accum_all,
                             mode="mid")

        if ovl:
            gm = phase_moe(kb, com, items_mid(0, 8), AF.Gelu, MaccT, combT1,
                           bias2=dict(lhsT=b2m8, extra=None), accum_all=False,
                           mode="mid", dbw1=True)
            next(gm)
            run_gen(l1_attn_chain(), gm, weights=[1, 8])
            kb.barrier()
            with ExitStack() as st9:
                MB = [st9.enter_context(nc.sbuf_tensor(uniq("p9_mb"),
                                                       [128, NCH, TT], F32))
                      for _ in range(2)]
                ma_r = MaccT.rearrange("(c p) t -> p c t", p=128)
                o_r = outT.rearrange("(c p) t -> p c t", p=128)
                p9prev = {}
                for tt in range(NTT):
                    par = tt % 2
                    ts9 = slice(tt * TT, (tt + 1) * TT)
                    tl9 = kb.ldq(MB[par][:], ma_r[:, :, ts9], p9prev.get(par))
                    tst9 = kb.ldp(o_r[:, :, ts9], MB[par][:], tl9, accum=True)
                    p9prev[par] = tst9
                kb.barrier()
        else:
            if "p7" in phases:
                run_gen(phase_gla(kb, com, q1T, k1T, v1, wT, oT)); kb.barrier()
            if "p7b" in phases:
                run_gen(phase_gate(kb, com, oT, g1T, ogT)); kb.barrier()
            if "p7c" in phases:
                run_gen(phase_l1_merge(kb, com, xhT, x1T, ogT, wo1, xpT, acc0T))
                kb.barrier()
            if "p7d" in phases:
                run_gen(phase_ln(kb, com, xpT, z2T,
                                 affine=(com["vecs"]["ln2w"], com["vecs"]["ln2b"])))
                kb.barrier()
                run_gen(phase_mlp(kb, com, z2T, acc0T, w1m, w2m, outT))
                kb.barrier()
            if "p8c" in phases:
                run_gen(phase_moe(kb, com, items_full(0, 8), AF.Gelu, outT, combT1,
                                  bias2=dict(lhsT=b2m8, extra=None), accum_all=True,
                                  mode="full", dbw1=True))
                kb.barrier()

        kb.barrier()
    return nc



def host_inputs_per_core(b, inputs):
    ac = np.ascontiguousarray
    im = {"xT": ac(inputs["L0_attn_x"][b].T)}
    im["wq0"] = ac(inputs["L0_attn_w_q"].T)
    im["wk0"] = ac(inputs["L0_attn_w_k"].T)
    im["wv0"] = ac(inputs["L0_attn_w_v"].T)
    im["wo0"] = ac(inputs["L0_attn_w_o"].T)
    im["rwT0"] = ac(inputs["L0_ffn_routing_W"].T)
    im["eb0"] = ac(inputs["L0_ffn_expert_bias"].reshape(1, 8))
    im["sW1"] = inputs["L0_ffn_sW1"]
    im["sW2"] = inputs["L0_ffn_sW2"]
    im["rW1"] = inputs["L0_ffn_rW1"]
    im["rW2"] = inputs["L0_ffn_rW2"]
    im["sb1v"] = inputs["L0_ffn_sb1"]
    im["sb2v"] = inputs["L0_ffn_sb2"]
    im["rb1v"] = inputs["L0_ffn_rb1"]
    im["rb2m"] = inputs["L0_ffn_rb2"]
    im["q1w"] = inputs["L1_attn_Wq"]
    im["k1w"] = inputs["L1_attn_Wk"]
    im["v1w"] = inputs["L1_attn_Wv"]
    im["g1w"] = inputs["L1_attn_Wg"]
    im["wgk1"] = inputs["L1_attn_Wgk1"]
    im["wgk2"] = inputs["L1_attn_Wgk2"]
    im["wo1"] = inputs["L1_attn_Wo"]
    im["w1m"] = inputs["L1_attn_W1"]
    im["w2m"] = inputs["L1_attn_W2"]
    im["rwT1"] = ac(inputs["L1_ffn_router_w"].T)
    im["rb1row"] = ac(inputs["L1_ffn_router_b"].reshape(1, 8))
    im["w1l"] = inputs["L1_ffn_w1"]
    im["w2l"] = inputs["L1_ffn_w2"]
    im["b2m8"] = inputs["L1_ffn_b2"]
    im["ln1wv"] = inputs["L1_attn_ln1_w"]
    im["ln1bv"] = inputs["L1_attn_ln1_b"]
    im["ln2wv"] = inputs["L1_attn_ln2_w"]
    im["ln2bv"] = inputs["L1_attn_ln2_b"]
    im["bgk2v"] = inputs["L1_attn_bgk2"]
    im["rmswv"] = inputs["L1_attn_rms_w"]
    im["b1mv"] = inputs["L1_attn_b1"]
    im["b2mv"] = inputs["L1_attn_b2"]
    im["rngv"] = inputs["L1_ffn_rn_g"]
    im["rnbv"] = inputs["L1_ffn_rn_b"]
    im["lngv"] = inputs["L1_ffn_ln_g"]
    im["lnbv"] = inputs["L1_ffn_ln_b"]
    im["b1lv"] = inputs["L1_ffn_b1"]
    im.update(host_consts())
    return im


def kernel(**inputs):
    nc = build_program()
    in_maps = [host_inputs_per_core(b, inputs) for b in range(8)]
    res = run_bass_kernel_spmd(nc, in_maps, core_ids=list(range(8)))
    out = np.stack([np.ascontiguousarray(r["outT"].T) for r in res.results], axis=0)
    return out


# version 13
# speedup vs baseline: 1.0365x; 1.0365x over previous
import numpy as np
from contextlib import ExitStack

import concourse.bass as bass
import concourse.mybir as mybir
from concourse.bass_utils import run_bass_kernel_spmd

F32 = mybir.dt.float32
BF16 = mybir.dt.bfloat16
AF = mybir.ActivationFunctionType
ALU = mybir.AluOpType
AX = mybir.AxisListType

T = 4096
D = 1024
TT = 512
NTT = T // TT
NCH = D // 128
LN_EPS = 1e-5
RMS_EPS0 = 1.1920929e-07

DEBUG_OUTPUTS = set()


class KB:

    NDMA = 8

    def __init__(self):
        self.nc = bass.Bass("TRN2", target_bir_lowering=False)
        self.stack = ExitStack()
        self.sems = {}
        self.cnt = {}
        names = ["sT", "sS", "sV", "sP"]
        names += [f"dQ{i}" for i in range(self.NDMA)]
        names += [f"dP{i}" for i in range(self.NDMA)]
        for name in names:
            self.sems[name] = self.stack.enter_context(self.nc.semaphore(name))
            self.cnt[name] = 0
        self._rrq = 0
        self._rrp = 0
        nc = self.nc
        self.T, self.S, self.V, self.P, self.Q = nc.tensor, nc.scalar, nc.vector, nc.gpsimd, nc.sync

    def sbuf(self, name, shape, dtype):
        return self.stack.enter_context(self.nc.sbuf_tensor(name, shape, dtype))

    def psum(self, name, shape, dtype=F32):
        return self.stack.enter_context(self.nc.psum_tensor(name, shape, dtype))

    def dram(self, name, shape, dtype):
        kind = "ExternalOutput" if name in DEBUG_OUTPUTS else "Internal"
        return self.nc.dram_tensor(name, shape, dtype, kind=kind).ap()

    def inc(self, ins, sem, n=1):
        self.cnt[sem] += n
        ins.then_inc(self.sems[sem], n)
        return (sem, self.cnt[sem])

    def tok(self, sem):
        return (sem, self.cnt[sem])

    def wait(self, eng, *toks):
        flat = []
        def _add(t):
            if t is None:
                return
            if isinstance(t, (list, tuple)) and t and isinstance(t[0], (list, tuple)):
                for u in t:
                    _add(u)
            elif isinstance(t, list):
                for u in t:
                    _add(u)
            else:
                flat.append(t)
        for t in toks:
            _add(t)
        best = {}
        for s, v in flat:
            if v > 0:
                best[s] = max(best.get(s, 0), v)
        for s, v in best.items():
            eng.wait_ge(self.sems[s], v)

    def barrier(self):
        toks = [self.tok(s) for s in self.sems]
        for eng in (self.T, self.S, self.V, self.P, self.Q):
            self.wait(eng, *toks)

    def ldq(self, dst, src, *toks):
        self.wait(self.Q, *toks)
        lane = f"dQ{self._rrq}"
        self._rrq = (self._rrq + 1) % self.NDMA
        return self.inc(self.Q.dma_start(dst, src), lane, 16)

    def ldp(self, dst, src, *toks, accum=False):
        self.wait(self.P, *toks)
        kw = {}
        if accum:
            kw["accum_op"] = ALU.add
        lane = f"dP{self._rrp}"
        self._rrp = (self._rrp + 1) % self.NDMA
        return self.inc(self.P.dma_start(dst, src, **kw), lane, 16)

    def stq(self, dst, src, *toks):
        return self.ldq(dst, src, *toks)


def ceil_div(a, b):
    return (a + b - 1) // b


class nullctx:
    def __init__(self, v):
        self.v = v
    def __enter__(self):
        return self.v
    def __exit__(self, *a):
        return False


_UNIQ = [0]


def uniq(name):
    _UNIQ[0] += 1
    return name + "_u" + str(_UNIQ[0])



def prep_common(kb, dram_vecs, consts):
    nc = kb.nc
    c = {}
    c["ident"] = kb.sbuf("ident", [128, 128], F32)
    c["identb"] = kb.sbuf("identb", [128, 128], BF16)
    c["hmask"] = kb.sbuf("hmask", [128, 2], BF16)
    c["hsel"] = kb.sbuf("hsel", [2, 128], BF16)
    c["tri"] = kb.sbuf("tri", [128, 128], BF16)
    c["trineg"] = kb.sbuf("trineg", [128, 128], BF16)
    c["sel8"] = kb.sbuf("sel8", [8, 8 * 128], BF16)
    c["ones_f"] = kb.sbuf("ones_f", [128, 1], F32)
    c["ones_b"] = kb.sbuf("ones_b", [128, 1], BF16)
    c["ones512"] = kb.sbuf("ones512", [128, TT], F32)
    c["ones1_f"] = kb.sbuf("ones1_f", [1, 128], F32)
    c["ones1_b"] = kb.sbuf("ones1_b", [1, 128], BF16)
    c["cpool"] = kb.sbuf("cpool", [128, 8], F32)
    kb.V.memset(c["cpool"][:, 0:1], 1.0)
    kb.V.memset(c["cpool"][:, 1:2], LN_EPS)
    kb.V.memset(c["cpool"][:, 2:3], RMS_EPS0)
    kb.V.memset(c["ones_f"][:], 1.0)
    kb.V.memset(c["ones_b"][:], 1.0)
    kb.V.memset(c["ones512"][:], 1.0)
    kb.V.memset(c["ones1_f"][:], 1.0)
    tvm = kb.inc(kb.V.memset(c["ones1_b"][:], 1.0), "sV")
    tcl = [kb.ldq(c["ident"][:], consts["c_ident"]),
           kb.ldq(c["hmask"][:], consts["c_hmask"]),
           kb.ldq(c["hsel"][:], consts["c_hsel"]),
           kb.ldq(c["tri"][:], consts["c_tri"]),
           kb.ldq(c["trineg"][:], consts["c_trineg"]),
           kb.ldq(c["sel8"][:], consts["c_sel8"]),
           kb.ldp(c["identb"][:], consts["c_ident"])]
    vec_tiles = {}
    for name, (ap, total) in dram_vecs.items():
        rows = ceil_div(total, 128)
        vec_tiles[name] = kb.sbuf("vec_" + name, [128, rows], F32)
    with ExitStack() as st:
        ps = st.enter_context(nc.psum_tensor("ps_prep", [128, 128], F32))
        stage = st.enter_context(nc.sbuf_tensor("stage_prep", [128, 128], F32))
        last_ev = None
        kb.wait(kb.T, tcl, tvm)
        for name, (ap, total) in dram_vecs.items():
            rows = ceil_div(total, 128)
            tile = vec_tiles[name]
            tl = kb.ldq(stage[:rows, :], ap.rearrange("(c p) -> c p", p=128), last_ev)
            kb.wait(kb.T, tl)
            tt_ = kb.inc(kb.T.transpose(ps[:, 0:rows], stage[:rows, :],
                                        c["ident"][:rows, :rows]), "sT")
            kb.wait(kb.V, tt_)
            last_ev = kb.inc(kb.V.tensor_copy(tile[:], ps[:, 0:rows]), "sV")
        kb.barrier()
    c["vecs"] = vec_tiles
    return c


def host_consts():
    import ml_dtypes
    b16 = ml_dtypes.bfloat16
    ident = np.eye(128, dtype=np.float32)
    hmask = np.zeros((128, 2), dtype=b16)
    hmask[0:64, 0] = 1; hmask[64:128, 1] = 1
    hsel = np.zeros((2, 128), dtype=b16)
    hsel[0, 0:64] = 1; hsel[1, 64:128] = 1
    tri = (np.arange(128)[None, :] >= np.arange(128)[:, None]).astype(b16)
    sel8 = np.zeros((8, 8 * 128), dtype=b16)
    for e in range(8):
        sel8[e, e * 128:(e + 1) * 128] = 1
    trineg = np.where(np.arange(128)[None, :] >= np.arange(128)[:, None],
                      0.0, -4096.0).astype(b16)
    trineg = np.ascontiguousarray(trineg.T)
    return {"c_ident": ident, "c_hmask": hmask, "c_hsel": hsel,
            "c_tri": tri, "c_sel8": sel8, "c_trineg": trineg}


def declare_consts(nc):
    return {
        "c_ident": nc.dram_tensor("c_ident", [128, 128], F32, kind="ExternalInput").ap(),
        "c_hmask": nc.dram_tensor("c_hmask", [128, 2], BF16, kind="ExternalInput").ap(),
        "c_hsel": nc.dram_tensor("c_hsel", [2, 128], BF16, kind="ExternalInput").ap(),
        "c_tri": nc.dram_tensor("c_tri", [128, 128], BF16, kind="ExternalInput").ap(),
        "c_trineg": nc.dram_tensor("c_trineg", [128, 128], BF16, kind="ExternalInput").ap(),
        "c_sel8": nc.dram_tensor("c_sel8", [8, 8 * 128], BF16, kind="ExternalInput").ap(),
    }



def phase_ln(kb, com, src_f32, dst_bf16, dst_f32=None, affine=None):
    nc = kb.nc
    with ExitStack() as st:
        sb = lambda n, s, d: st.enter_context(nc.sbuf_tensor(uniq(n), s, d))
        pst = lambda n, s: st.enter_context(nc.psum_tensor(uniq(n), s, F32))
        XF = [sb(f"ln_x{p}", [128, NCH, TT], F32) for p in range(2)]
        SQ = [sb(f"ln_sq{p}", [128, NCH, TT], F32) for p in range(2)]
        EV = [sb(f"ln_ev{p}", [128, 2 * TT], F32) for p in range(2)]
        OB = [sb(f"ln_ob{p}", [128, NCH, TT], BF16) for p in range(2)]
        OF = [sb(f"ln_of{p}", [128, NCH, TT], F32) for p in range(2)] if dst_f32 is not None else None
        ST = [sb(f"ln_st{p}", [1, 8], F32) for p in range(2)]
        psS = [pst(f"ln_psS{p}", [1, TT]) for p in range(2)]
        psQ = [pst(f"ln_psQ{p}", [1, TT]) for p in range(2)]
        psM = [pst(f"ln_psM{p}", [128, TT]) for p in range(2)]
        psR = [pst(f"ln_psR{p}", [128, TT]) for p in range(2)]
        MR = [sb(f"ln_mr{p}", [1, 2 * TT], F32) for p in range(2)]

        src_r = src_f32.rearrange("(c p) t -> p c t", p=128)
        dst_r = dst_bf16.rearrange("(c p) t -> p c t", p=128) if dst_bf16 is not None else None
        dstf_r = dst_f32.rearrange("(c p) t -> p c t", p=128) if dst_f32 is not None else None

        done = {}
        for tt in range(NTT):
            par = tt % 2
            ts = slice(tt * TT, (tt + 1) * TT)
            free = done.get(par)
            tl = kb.ldq(XF[par][:], src_r[:, :, ts], *(free or []))
            kb.wait(kb.S, tl)
            tsq = kb.inc(kb.S.activation(SQ[par][:], XF[par][:], AF.Square), "sS")
            kb.wait(kb.T, tl)
            for ch in range(NCH):
                kb.T.matmul(psS[par][:], com["ones_f"][:], XF[par][:, ch, :],
                            start=(ch == 0), stop=(ch == NCH - 1))
            kb.wait(kb.T, tsq)
            for ch in range(NCH):
                ins = kb.T.matmul(psQ[par][:], com["ones_f"][:], SQ[par][:, ch, :],
                                  start=(ch == 0), stop=(ch == NCH - 1))
            tstat = kb.inc(ins, "sT")
            m = MR[par][:, 0:TT]
            sq_m = EV[par][:1, 0:TT]
            var = EV[par][:1, TT:2 * TT]
            kb.wait(kb.V, tstat)
            tv1 = kb.inc(kb.V.tensor_scalar_mul(m, psS[par][:], 1.0 / D), "sV")
            kb.wait(kb.S, tv1)
            tm2 = kb.inc(kb.S.activation(sq_m, m, AF.Square), "sS")
            kb.wait(kb.V, tm2)
            tvar = kb.inc(kb.V.scalar_tensor_tensor(var, psQ[par][:], 1.0 / D, sq_m,
                                                    ALU.mult, ALU.subtract), "sV")
            kb.wait(kb.S, tvar)
            sd = EV[par][:1, 0:TT]
            tsd = kb.inc(kb.S.activation(sd, var, AF.Sqrt, bias=com["cpool"][:1, 1:2]), "sS")
            kb.wait(kb.V, tsd)
            r = MR[par][:, TT:2 * TT]
            trec = kb.inc(kb.V.reciprocal(r, sd), "sV")
            kb.wait(kb.T, trec)
            kb.T.matmul(psM[par][:], com["ones1_f"][:], m, start=True, stop=True)
            tbc = kb.inc(kb.T.matmul(psR[par][:], com["ones1_f"][:], r,
                                     start=True, stop=True), "sT")
            kb.wait(kb.V, tbc)
            tmp = EV[par][:, 0:TT]
            for ch in range(NCH):
                kb.V.tensor_sub(tmp, XF[par][:, ch, :], psM[par][:])
                if OF is not None:
                    ins = kb.V.tensor_mul(OF[par][:, ch, :], tmp, psR[par][:])
                    src_n = OF[par][:, ch, :]
                else:
                    ins = kb.V.tensor_mul(tmp, tmp, psR[par][:])
                    src_n = tmp
                if affine is not None:
                    w_t, b_t = affine
                    ins = kb.V.tensor_scalar(OB[par][:, ch, :], src_n,
                                             w_t[:, ch:ch + 1], b_t[:, ch:ch + 1],
                                             ALU.mult, ALU.add)
                elif dst_bf16 is not None:
                    ins = kb.V.tensor_copy(OB[par][:, ch, :], src_n)
            tap = kb.inc(ins, "sV")
            outs = []
            if dst_bf16 is not None:
                outs.append(kb.stq(dst_r[:, :, ts], OB[par][:], tap))
            if dstf_r is not None:
                outs.append(kb.stq(dstf_r[:, :, ts], OF[par][:], tap))
            done[par] = outs + [tap]
        kb.barrier()



def phase_l0_proj(kb, com, xn0T, wq, wk, wv, qnT, knT, v0):
    nc = kb.nc
    with ExitStack() as st:
        sb = lambda n, s, d: st.enter_context(nc.sbuf_tensor(uniq(n), s, d))
        pst = lambda n, s: st.enter_context(nc.psum_tensor(uniq(n), s, F32))
        WQ = sb("p1_wq", [128, NCH, D], BF16)
        WK = sb("p1_wk", [128, NCH, D], BF16)
        WV = sb("p1_wv", [128, NCH, D], BF16)
        XB = [sb(f"p1_xb{p}", [128, NCH, TT], BF16) for p in range(2)]
        SQT = [sb(f"p1_sq{p}", [128, TT], BF16) for p in range(2)]
        QF = [sb(f"p1_qf{p}", [128, TT], F32) for p in range(2)]
        OB = [sb(f"p1_ob{p}", [128, TT], BF16) for p in range(2)]
        SD = [sb(f"p1_sd{p}", [2, 2 * TT], F32) for p in range(2)]
        RRB = [sb(f"p1_rr{p}", [2, TT], BF16) for p in range(2)]
        VO = [sb(f"p1_vo{p}", [128, 16, 65], BF16) for p in range(2)]
        psA = [pst(f"p1_psA{p}", [128, TT]) for p in range(2)]
        psST = [pst(f"p1_psST{p}", [2, TT]) for p in range(2)]
        psB = [pst(f"p1_psB{p}", [128, TT]) for p in range(2)]

        twq = kb.ldp(WQ[:], wq.rearrange("(c p) j -> p c j", p=128))
        twk = kb.ldp(WK[:], wk.rearrange("(c p) j -> p c j", p=128))
        twv = kb.ldp(WV[:], wv.rearrange("(c p) j -> p c j", p=128))
        xn_r = xn0T.rearrange("(c p) t -> p c t", p=128)
        qn_r = qnT.rearrange("(c p) t -> p c t", p=128)
        kn_r = knT.rearrange("(c p) t -> p c t", p=128)

        free = {}
        unit = 0
        for tt in range(NTT):
            ts = slice(tt * TT, (tt + 1) * TT)
            tl = kb.ldq(XB[tt % 2][:], xn_r[:, :, ts], *free.get(("xb", tt % 2), []))
            xb = XB[tt % 2]
            for W, wtok, dst_r in ((WQ, twq, qn_r), (WK, twk, kn_r)):
                for jc in range(NCH):
                    par = unit % 2
                    unit += 1
                    kb.wait(kb.T, tl, wtok, *free.get(("qk", par), []))
                    for ch in range(NCH):
                        ins = kb.T.matmul(psA[par][:], W[:, ch, jc * 128:(jc + 1) * 128],
                                          xb[:, ch, :], start=(ch == 0), stop=(ch == NCH - 1))
                    tmm = kb.inc(ins, "sT")
                    kb.wait(kb.S, tmm)
                    tsq = kb.inc(kb.S.activation(SQT[par][:], psA[par][:], AF.Square), "sS")
                    kb.wait(kb.T, tsq)
                    tst = kb.inc(kb.T.matmul(psST[par][:], com["hmask"][:], SQT[par][:],
                                             start=True, stop=True), "sT")
                    kb.wait(kb.S, tst)
                    sd = SD[par][:, 0:TT]
                    tsd = kb.inc(kb.S.activation(sd, psST[par][:], AF.Sqrt,
                                                 scale=1.0 / 64, bias=com["cpool"][:2, 2:3]), "sS")
                    kb.wait(kb.V, tsd)
                    rr = SD[par][:, TT:2 * TT]
                    tre = kb.inc(kb.V.reciprocal(rr, sd), "sV")
                    kb.wait(kb.V, tre)
                    trr = kb.inc(kb.V.tensor_copy(RRB[par][:], rr), "sV")
                    kb.wait(kb.T, trr)
                    tbc = kb.inc(kb.T.matmul(psB[par][:], com["hsel"][:], RRB[par][:],
                                             start=True, stop=True), "sT")
                    kb.wait(kb.V, tbc)
                    kb.V.tensor_copy(QF[par][:], psA[par][:])
                    tev = kb.inc(kb.V.tensor_mul(OB[par][:], QF[par][:], psB[par][:]), "sV")
                    tst_o = kb.stq(dst_r[:, jc, ts], OB[par][:], tev)
                    free[("qk", par)] = [tev, tst_o]
            for s4 in range(4):
                par = s4 % 2
                vo = VO[par]
                kb.wait(kb.V, *free.get(("vo", par), []))
                tvm = kb.inc(kb.V.memset(vo[:, :, 64:65], 1.0), "sV")
                toks_half = []
                for half in range(2):
                    parp = (unit + half) % 2
                    kb.wait(kb.T, tl, twv, *free.get(("qk", parp), []))
                    for ch in range(NCH):
                        ins = kb.T.matmul(psA[parp][:],
                                          xb[:, ch, s4 * 128:(s4 + 1) * 128],
                                          WV[:, ch, half * 512:(half + 1) * 512],
                                          start=(ch == 0), stop=(ch == NCH - 1))
                    tmm = kb.inc(ins, "sT")
                    kb.wait(kb.V, tmm, tvm)
                    tev = kb.inc(kb.V.tensor_copy(
                        vo[:, half * 8:(half + 1) * 8, 0:64],
                        psA[parp][:].rearrange("p (h v) -> p h v", v=64)), "sV")
                    free[("qk", parp)] = [tev]
                    toks_half.append(tev)
                unit += 2
                row = tt * TT + s4 * 128
                tso = kb.stq(v0[row:row + 128, :], vo[:], *toks_half)
                free[("vo", par)] = [tso]
            free[("xb", tt % 2)] = [kb.tok("sT")]
        kb.barrier()



def phase_l0_attn(kb, com, qnT, knT, v0, yrT, recT, st=None):
    nc = kb.nc
    H = 16
    QTL = 512
    NQI = T // QTL
    ND = 3
    with (ExitStack() if st is None else nullctx(st)) as st:
        sb = lambda n, s, d: st.enter_context(nc.sbuf_tensor(uniq(n), s, d))
        pst = lambda n, s: st.enter_context(nc.psum_tensor(uniq(n), s, F32))
        KT = [sb(f"p2_kt{p}", [64, T], BF16) for p in range(2)]
        VT = [sb(f"p2_vt{p}", [128, 32, 65], BF16) for p in range(2)]
        QT = [sb(f"p2_qt{p}", [64, QTL], BF16) for p in range(2)]
        TS = sb("p2_ts", [128, QTL], F32)
        PB = [sb(f"p2_pb{p}", [128, QTL], BF16) for p in range(4)]
        YB = [sb(f"p2_yb{p}", [64, QTL], BF16) for p in range(2)]
        RC = [sb(f"p2_rc{p}", [1, QTL], F32) for p in range(2)]
        psS = [pst(f"p2_psS{p}", [128, QTL]) for p in range(ND)]
        psY = [pst(f"p2_psY{p}", [65, QTL]) for p in range(2)]

        freeS = {i: None for i in range(ND)}
        freePB = {i: None for i in range(4)}
        freeQ = {0: None, 1: None}
        freeY = {0: None, 1: None}
        freeYB = {0: None, 1: None}
        kvload = {}
        kvfree = {0: [], 1: []}

        units = [(h, qi) for h in range(H) for qi in range(NQI)]
        qload = {}

        def load_q(i):
            if i >= len(units) or i in qload:
                return
            h, qi = units[i]
            up = i % 2
            qload[i] = kb.ldq(QT[up][:], qnT[64 * h:64 * h + 64,
                                             QTL * qi:QTL * qi + QTL], freeQ[up])

        def load_kv(h):
            hp = h % 2
            tK = kb.ldq(KT[hp][:], knT[64 * h:64 * h + 64, :], *kvfree[hp])
            tV = kb.ldq(VT[hp][:],
                        v0.rearrange("(c p) e -> p c e", p=128)[:, :, 65 * h:65 * h + 65],
                        *kvfree[hp])
            kvload[h] = (tK, tV)

        load_kv(0)
        load_q(0)
        for ui, (h, qi) in enumerate(units):
            hp = h % 2
            up = ui % 2
            tK, tV = kvload[h]
            tQ = qload[ui]
            load_q(ui + 1)
            if qi == 0 and h + 1 < H:
                load_kv(h + 1)
            tq0 = QTL * qi
            nkj = 4 * (qi + 1)
            head_last = None

            def emit_pv(kj, f0, tmask):
                nonlocal head_last
                pb = kj % 4
                kb.wait(kb.T, tmask, tV)
                if kj == 0:
                    kb.wait(kb.T, freeY[up])
                ins = kb.T.matmul(psY[up][:, f0:QTL], VT[hp][:, kj, :],
                                  PB[pb][:, f0:QTL],
                                  start=(kj == 0), stop=(kj == nkj - 1))
                tpv = kb.inc(ins, "sT")
                freePB[pb] = tpv
                head_last = tpv

            pend = []
            for kj in range(nkj):
                sl = kj % ND

                tk0 = 128 * kj
                fd = tk0 - tq0
                f0 = max(0, fd)
                kb.wait(kb.T, tQ, tK, freeS[sl])
                masked = fd >= 0
                ins = kb.T.matmul(psS[sl][:, f0:QTL], KT[hp][:, tk0:tk0 + 128],
                                  QT[up][:, f0:QTL], start=True, stop=not masked)
                if masked:
                    ins = kb.T.matmul(psS[sl][:, fd:fd + 128], com["trineg"][:],
                                      com["identb"][:], start=False, stop=True,
                                      skip_group_check=True)
                tmm = kb.inc(ins, "sT")
                pb = kj % 4
                kb.wait(kb.S, tmm, freePB[pb])
                tth = kb.inc(kb.S.activation(TS[:, f0:QTL], psS[sl][:, f0:QTL], AF.Tanh,
                                             scale=1.0 / 400.0), "sS")
                tse = kb.inc(kb.S.activation(PB[pb][:, f0:QTL], TS[:, f0:QTL],
                                             AF.Exp, scale=50.0), "sS")
                freeS[sl] = tth
                pend.append((kj, f0, tse))
                if len(pend) == 3:
                    emit_pv(*pend.pop(0))
                yield
            while pend:
                emit_pv(*pend.pop(0))
            freeQ[up] = head_last
            kb.wait(kb.V, head_last, freeYB[up])
            kb.V.reciprocal(RC[up][:], psY[up][64:65, :])
            tev = kb.inc(kb.V.tensor_copy(YB[up][:], psY[up][0:64, :]), "sV")
            freeY[up] = tev
            kb.stq(yrT[64 * h:64 * h + 64, tq0:tq0 + QTL], YB[up][:], tev)
            t2 = kb.stq(recT[h:h + 1, tq0:tq0 + QTL], RC[up][:], tev)
            freeYB[up] = t2
            if qi == NQI - 1:
                kvfree[hp] = [head_last]



def phase_router(kb, com, inT_bf16, rwT, bias_row_ap, combT, mode):
    nc = kb.nc
    with ExitStack() as st:
        sb = lambda n, s, d: st.enter_context(nc.sbuf_tensor(uniq(n), s, d))
        pst = lambda n, s: st.enter_context(nc.psum_tensor(uniq(n), s, F32))
        RW = sb("rt_rw", [128, NCH, 8], BF16)
        EB = sb("rt_eb", [1, 8], F32)
        XB = [sb(f"rt_xb{p}", [128, NCH, TT], BF16) for p in range(2)]
        LG = sb("rt_lg", [128, 32], F32)
        ACT = sb("rt_act", [128, 32], F32)
        SCR = sb("rt_scr", [128, 32], F32)
        MK = sb("rt_mk", [128, 32], F32)
        CMB = sb("rt_cmb", [128, 32], F32)
        M1 = sb("rt_m1", [128, 4], F32)
        M2 = sb("rt_m2", [128, 4], F32)
        DEN = sb("rt_den", [128, 4], F32)
        psL = pst("rt_psL", [128, 32])
        psT = pst("rt_psT", [8, 128])
        tw = kb.ldp(RW[:], rwT.rearrange("(c p) e -> p c e", p=128))
        te = kb.ldq(EB[:], bias_row_ap)
        in_r = inT_bf16.rearrange("(c p) t -> p c t", p=128)
        prev = {}
        ntt = 1 if "rt_one_tt" in DEBUG_OUTPUTS else NTT
        for tt in range(ntt):
            par = tt % 2
            ts = slice(tt * TT, (tt + 1) * TT)
            tl = kb.ldq(XB[par][:], in_r[:, :, ts], prev.get(("xb", par)))
            kb.wait(kb.T, tl, tw, te, prev.get("evac"))
            for s in range(4):
                sl = slice(s * 8, (s + 1) * 8)
                for ch in range(NCH):
                    kb.T.matmul(psL[:, sl], XB[par][:, ch, s * 128:(s + 1) * 128],
                                RW[:, ch, :], start=(ch == 0), stop=False)
                ins = kb.T.matmul(psL[:, sl], com["ones1_f"][:], EB[:],
                                  start=False, stop=True, skip_group_check=True)
            tmm = kb.inc(ins, "sT")
            kb.wait(kb.V, tmm)
            tlg = kb.inc(kb.V.tensor_copy(LG[:], psL[:]), "sV")
            if "rt_lgd" in DEBUG_OUTPUTS:
                if tt == 0 and not hasattr(kb, "_rt_lgd"):
                    kb._rt_lgd = kb.dram("rt_lgd", [NTT * 128, 32], F32)
                    kb._rt_actd = kb.dram("rt_actd", [NTT * 128, 32], F32)
                kb.stq(kb._rt_lgd[tt * 128:(tt + 1) * 128, :], LG[:], tlg)
            if mode == "l0":
                kb.wait(kb.S, tlg)
                kb.S.activation(SCR[:], LG[:], AF.Exp)
                kb.S.activation(SCR[:], SCR[:], AF.Ln, bias=com["cpool"][:, 0:1])
                tact = kb.inc(kb.S.activation(ACT[:], SCR[:], AF.Sqrt), "sS")
                if "rt_lgd" in DEBUG_OUTPUTS:
                    kb.stq(kb._rt_actd[tt * 128:(tt + 1) * 128, :], ACT[:], tact)
                kb.wait(kb.V, tact)
                base = ACT
            else:
                base = LG
            def vedge(ins):
                t = kb.inc(ins, "sV")
                kb.wait(kb.V, t)
                return t
            for s in range(4):
                ins = kb.V.reduce_max(M1[:, s:s + 1], base[:, s * 8:(s + 1) * 8], AX.X)
            vedge(ins)
            for s in range(4):
                sl = slice(s * 8, (s + 1) * 8)
                ins = kb.V.tensor_scalar(SCR[:, sl], base[:, sl], M1[:, s:s + 1],
                                         None, ALU.is_ge)
            vedge(ins)
            for s in range(4):
                sl = slice(s * 8, (s + 1) * 8)
                ins = kb.V.scalar_tensor_tensor(SCR[:, sl], SCR[:, sl], -1e30, base[:, sl],
                                                ALU.mult, ALU.add)
            vedge(ins)
            for s in range(4):
                ins = kb.V.reduce_max(M2[:, s:s + 1], SCR[:, s * 8:(s + 1) * 8], AX.X)
            vedge(ins)
            kb.wait(kb.V, prev.get("ctT"))
            for s in range(4):
                sl = slice(s * 8, (s + 1) * 8)
                ins = kb.V.tensor_scalar(MK[:, sl], base[:, sl], M2[:, s:s + 1],
                                         None, ALU.is_ge)
            vedge(ins)
            for s in range(4):
                sl = slice(s * 8, (s + 1) * 8)
                if mode == "l0":
                    ins_cmb = kb.V.tensor_mul(CMB[:, sl], base[:, sl], MK[:, sl])
            if mode == "l1":
                tv = kb.inc(kb.V.tensor_sub(M2[:], M2[:], M1[:]), "sV")
                kb.wait(kb.S, tv)
                tde = kb.inc(kb.S.activation(DEN[:], M2[:], AF.Exp), "sS")
                for s in range(4):
                    sl = slice(s * 8, (s + 1) * 8)
                    ins = kb.V.tensor_scalar(SCR[:, sl], base[:, sl], M1[:, s:s + 1], None,
                                             ALU.subtract)
                tsu = kb.inc(ins, "sV")
                kb.wait(kb.S, tsu)
                tex = kb.inc(kb.S.activation(SCR[:], SCR[:], AF.Exp), "sS")
                kb.wait(kb.V, tex, tde)
                ta1 = kb.inc(kb.V.tensor_scalar_add(DEN[:], DEN[:], 1.0), "sV")
                kb.wait(kb.V, ta1)
                trc = kb.inc(kb.V.reciprocal(DEN[:], DEN[:]), "sV")
                kb.V.tensor_mul(SCR[:], SCR[:], MK[:])
                kb.wait(kb.V, trc)
                for s in range(4):
                    sl = slice(s * 8, (s + 1) * 8)
                    ins_cmb = kb.V.tensor_scalar(CMB[:, sl], SCR[:, sl], DEN[:, s:s + 1],
                                                 None, ALU.mult)
            tcmb = kb.inc(ins_cmb, "sV")
            if "rt_lgd" in DEBUG_OUTPUTS:
                if tt == 0 and not hasattr(kb, "_rt_cmbd"):
                    kb._rt_cmbd = kb.dram("rt_cmbd", [NTT * 128, 32], F32)
                kb.stq(kb._rt_cmbd[tt * 128:(tt + 1) * 128, :], CMB[:], tcmb)
            for s in range(4):
                kb.wait(kb.T, tcmb, prev.get("ct"))
                ttr = kb.inc(kb.T.transpose(psT[:], CMB[:, s * 8:(s + 1) * 8],
                                            com["ident"][:]), "sT")
                kb.wait(kb.V, ttr)
                tev = kb.inc(kb.V.tensor_copy(
                    combT[:, tt * TT + s * 128: tt * TT + (s + 1) * 128], psT[:]), "sV")
                prev["ct"] = tev
                prev["ctT"] = ttr
            prev["evac"] = tlg
            prev[("xb", par)] = kb.tok("sT")
        kb.barrier()



def phase_moe(kb, com, items, act_fn, out_ap, combT, bias2=None, accum_all=False,
              slim=False, st=None):
    nc = kb.nc
    nIc_max = max(it["w1"].shape[-1] for it in items) // 128
    g_hc = 1 if slim else 2
    n_psH = 2
    n_psY = 1 if slim else 2
    with (ExitStack() if st is None else nullctx(st)) as st:
        sb = lambda n, s, d: st.enter_context(nc.sbuf_tensor(uniq(n), s, d))
        pst = lambda n, s: st.enter_context(nc.psum_tensor(uniq(n), s, F32))
        W1B = [sb(f"mo_w1{p}", [128, NCH, nIc_max * 128], BF16)
               for p in range(1 if slim else 2)]
        W2B = sb("mo_w2", [128, nIc_max, D], BF16)
        XB = [sb(f"mo_xb{p}", [128, NCH, TT], BF16) for p in range(2)]
        HS = sb("mo_hs", [128, nIc_max, TT], BF16)
        CB = [sb(f"mo_cb{p}", [128, TT], F32) for p in range(2)]
        if act_fn == "silu_exp":
            moE = [sb(f"mo_e{p}", [128, g_hc, TT], F32) for p in range(2)]
            moT = [sb(f"mo_t{p}", [128, g_hc, TT], F32) for p in range(2)]
        YS = [sb(f"mo_ys{p}", [128, NCH, TT], F32) for p in range(2)]
        psH = pst("mo_psH", [128, 512 * g_hc * 2])
        psY = [pst(f"mo_psY{p}", [128, TT]) for p in range(n_psY)]
        psCB = psY[0] if slim else pst("mo_psCB", [128, TT])
        out_r = out_ap.rearrange("(c p) t -> p c t", p=128)
        fstore = {}
        prev = {}
        dbw1 = not slim
        w1tok = {}

        def load_w1(i):
            if i >= len(items) or i in w1tok:
                return
            nIc_i = items[i]["w1"].shape[-1] // 128
            wp = i % 2 if dbw1 else 0
            w1tok[i] = kb.ldp(W1B[wp][:, :, 0:nIc_i * 128],
                              items[i]["w1"].rearrange("(c p) i -> p c i", p=128),
                              prev.get(("w1", wp)))

        load_w1(0)
        for it_i, it in enumerate(items):
            nIc = it["w1"].shape[-1] // 128
            wp = it_i % 2 if dbw1 else 0
            tw1 = w1tok[it_i]
            W1C = W1B[wp]
            tw2 = kb.ldp(W2B[:, 0:nIc, :],
                         it["w2"].rearrange("(c p) j -> p c j", p=128),
                         prev.get("w2"))
            in_r = it["inT"].rearrange("(c p) t -> p c t", p=128)
            if dbw1:
                load_w1(it_i + 1)
            for tt in range(NTT):
                par = tt % 2
                ts = slice(tt * TT, (tt + 1) * TT)
                tl = kb.ldq(XB[par][:], in_r[:, :, ts], prev.get(("xb", par)))
                if it["cb"] is not None:
                    e = it["cb"]
                    kb.wait(kb.T, prev.get("cbps"), prev.get(("y", 0)) if slim else None)
                    tcb = kb.inc(kb.T.matmul(psCB[:], com["sel8"][:, e * 128:(e + 1) * 128],
                                             combT[:, ts], start=True, stop=True), "sT")
                    kb.wait(kb.V, tcb)
                    tcbe = kb.inc(kb.V.tensor_copy(CB[par][:], psCB[:]), "sV")
                    prev["cbps"] = tcbe
                    if slim:
                        prev[("y", 0)] = tcbe
                else:
                    tcbe = None
                b1t, b1c = it["b1"]
                for g in range(ceil_div(nIc, g_hc)):
                    hcs = range(g * g_hc, min(nIc, g * g_hc + g_hc))
                    ph = g % 2
                    kb.wait(kb.T, tl, tw1, prev.get(("h", ph)))
                    for hi, hc in enumerate(hcs):
                        o0 = ph * g_hc * 512 + hi * 512
                        for ch in range(NCH):
                            ins = kb.T.matmul(psH[:, o0:o0 + TT],
                                              W1C[:, ch, hc * 128:(hc + 1) * 128],
                                              XB[par][:, ch, :],
                                              start=(ch == 0), stop=(ch == NCH - 1))
                    tmm = kb.inc(ins, "sT")
                    kb.wait(kb.S, tmm)
                    if act_fn == "silu_exp":
                        nb1t, _ = it["b1n"]
                        EE = moE[ph]
                        TTt = moT[ph]
                        kb.wait(kb.S, prev.get(("hv", ph)))
                        for hi, hc in enumerate(hcs):
                            o0 = ph * g_hc * 512 + hi * 512
                            ins = kb.S.activation(EE[:, hi, :], psH[:, o0:o0 + TT], AF.Exp,
                                                  scale=-1.0,
                                                  bias=nb1t[:, b1c + hc:b1c + hc + 1])
                        tact_s = kb.inc(ins, "sS")
                        kb.wait(kb.V, tact_s, prev.get(("hv", ph)))
                        for hi, hc in enumerate(hcs):
                            o0 = ph * g_hc * 512 + hi * 512
                            kb.V.tensor_scalar(TTt[:, hi, :], psH[:, o0:o0 + TT],
                                               b1t[:, b1c + hc:b1c + hc + 1], None,
                                               ALU.add)
                            kb.V.tensor_add(EE[:, hi, :], EE[:, hi, :], com["ones512"][:])
                            ins = kb.V.reciprocal(EE[:, hi, :], EE[:, hi, :])
                        tv2 = kb.inc(ins, "sV")
                        kb.wait(kb.V, tv2)
                        for hi, hc in enumerate(hcs):
                            ins = kb.V.tensor_mul(HS[:, hc, :], TTt[:, hi, :], EE[:, hi, :])
                        tact = kb.inc(ins, "sV")
                        prev[("hv", ph)] = tact
                    else:
                        for hi, hc in enumerate(hcs):
                            o0 = ph * g_hc * 512 + hi * 512
                            ins = kb.S.activation(HS[:, hc, :], psH[:, o0:o0 + TT],
                                                  act_fn, bias=b1t[:, b1c + hc:b1c + hc + 1])
                        tact = kb.inc(ins, "sS")
                    prev[("h", ph)] = tact
                    yield
                for jc in range(NCH):
                    yp = 0 if slim else jc % 2
                    kb.wait(kb.T, tw2, prev.get(("y", yp)), kb.tok("sS"),
                            kb.tok("sV") if act_fn == "silu_exp" else None)
                    for ic in range(nIc):
                        ins = kb.T.matmul(psY[yp][:], W2B[:, ic, jc * 128:(jc + 1) * 128],
                                          HS[:, ic, :], start=(ic == 0), stop=(ic == nIc - 1))
                    tmy = kb.inc(ins, "sT")
                    kb.wait(kb.V, tmy, tcbe, prev.get(("ysw", par)) if jc == 0 else None)
                    if it["cb"] is not None:
                        tev = kb.inc(kb.V.tensor_mul(YS[par][:, jc, :], psY[yp][:],
                                                     CB[par][:]), "sV")
                    else:
                        tev = kb.inc(kb.V.tensor_copy(YS[par][:, jc, :], psY[yp][:]), "sV")
                    prev[("y", yp)] = tev
                    if jc % 2 == 1:
                        yield
                tstore = kb.ldp(out_r[:, :, ts], YS[par][:], tev,
                                fstore.get(tt), accum=(accum_all or it_i > 0))
                fstore[tt] = tstore
                prev[("ysw", par)] = tstore
                prev[("xb", par)] = kb.tok("sT")
            prev[("w1", wp)] = kb.tok("sT")
            prev["w2"] = kb.tok("sT")
            if not dbw1:
                load_w1(it_i + 1)
        if bias2 is not None:
            B2 = sb("mo_b2", [8, D], BF16)
            tb2 = kb.ldp(B2[:], bias2["lhsT"])
            for tt in range(NTT):
                par = tt % 2
                ts = slice(tt * TT, (tt + 1) * TT)
                for jc in range(NCH):
                    yp = 0 if slim else jc % 2
                    kb.wait(kb.T, tb2, prev.get(("y", yp)),
                            prev.get(("ysw", par)) if jc == 0 else None, kb.tok("sV"))
                    tmy = kb.inc(kb.T.matmul(psY[yp][:], B2[:, jc * 128:(jc + 1) * 128],
                                             combT[:, ts], start=True, stop=True), "sT")
                    if bias2.get("extra") is not None:
                        xt, xc = bias2["extra"]
                        kb.wait(kb.S, tmy)
                        tev = kb.inc(kb.S.activation(YS[par][:, jc, :], psY[yp][:],
                                                     AF.Identity,
                                                     bias=xt[:, jc + xc:jc + xc + 1]), "sS")
                        kb.wait(kb.V, tev)
                    else:
                        kb.wait(kb.V, tmy)
                        tev = kb.inc(kb.V.tensor_copy(YS[par][:, jc, :], psY[yp][:]), "sV")
                    prev[("y", yp)] = tev
                    if jc % 2 == 1:
                        yield
                kb.wait(kb.V, kb.tok("sS"))
                tstore = kb.ldp(out_r[:, :, ts], YS[par][:], kb.tok("sS"), kb.tok("sV"),
                                fstore.get(tt), accum=True)
                fstore[tt] = tstore
                prev[("ysw", par)] = tstore


def run_gen(*gens, weights=None):
    gens = [g for g in gens if g is not None]
    if weights is None:
        weights = [1] * len(gens)
    weights = list(weights[:len(gens)])
    while gens:
        nxt, nw = [], []
        for g, w in zip(gens, weights):
            alive = True
            for _ in range(w):
                try:
                    next(g)
                except StopIteration:
                    alive = False
                    break
            if alive:
                nxt.append(g); nw.append(w)
        gens, weights = nxt, nw



def phase_l0_merge(kb, com, xT, yrT, recT, FT, wo, x1T):
    nc = kb.nc
    with ExitStack() as st:
        sb = lambda n, s, d: st.enter_context(nc.sbuf_tensor(uniq(n), s, d))
        pst = lambda n, s: st.enter_context(nc.psum_tensor(uniq(n), s, F32))
        WO = sb("p3_wo", [128, NCH, D], BF16)
        YR = [sb(f"p3_yr{p}", [128, NCH, TT], BF16) for p in range(2)]
        YN = [sb(f"p3_yn{p}", [128, NCH, TT], BF16) for p in range(2)]
        XF = [sb(f"p3_xf{p}", [128, NCH, TT], F32) for p in range(2)]
        FF = [sb(f"p3_ff{p}", [128, NCH, TT], F32) for p in range(2)]
        RC2 = sb("p3_rc", [2, TT], F32)
        RCB = sb("p3_rcb", [2, TT], BF16)
        EV = [sb(f"p3_ev{p}", [128, TT], F32) for p in range(2)]
        X1 = [sb(f"p3_x1{p}", [128, NCH, TT], F32) for p in range(2)]
        psA = [pst(f"p3_psA{p}", [128, TT]) for p in range(2)]
        psB = pst("p3_psB", [128, TT])
        tw = kb.ldp(WO[:], wo.rearrange("(c p) j -> p c j", p=128))
        yr_r = yrT.rearrange("(c p) t -> p c t", p=128)
        x_r = xT.rearrange("(c p) t -> p c t", p=128)
        f_r = FT.rearrange("(c p) t -> p c t", p=128)
        x1_r = x1T.rearrange("(c p) t -> p c t", p=128)
        prev = {}
        for tt in range(NTT):
            par = tt % 2
            ts = slice(tt * TT, (tt + 1) * TT)
            tly = kb.ldq(YR[par][:], yr_r[:, :, ts], prev.get(("yr", par)))
            tlx = kb.ldq(XF[par][:], x_r[:, :, ts], prev.get(("x1", par)))
            tlf = kb.ldq(FF[par][:], f_r[:, :, ts], prev.get(("x1", par)))
            for ch in range(NCH):
                tlr = kb.ldq(RC2[:], recT[2 * ch:2 * ch + 2, ts], prev.get("rc2"))
                kb.wait(kb.V, tlr, prev.get("rcb"))
                trb = kb.inc(kb.V.tensor_copy(RCB[:], RC2[:]), "sV")
                prev["rc2"] = trb
                kb.wait(kb.T, trb, prev.get("psBr"))
                tbc = kb.inc(kb.T.matmul(psB[:], com["hsel"][:], RCB[:],
                                         start=True, stop=True), "sT")
                prev["rcb"] = tbc
                kb.wait(kb.V, tbc, tly)
                tyn = kb.inc(kb.V.tensor_mul(YN[par][:, ch, :], YR[par][:, ch, :],
                                             psB[:]), "sV")
                prev["psBr"] = tyn
            prev[("yr", par)] = tyn
            for jc in range(NCH):
                yp = jc % 2
                kb.wait(kb.T, tw, tyn, prev.get(("psA", yp)))
                for ch in range(NCH):
                    ins = kb.T.matmul(psA[yp][:], WO[:, ch, jc * 128:(jc + 1) * 128],
                                      YN[par][:, ch, :], start=(ch == 0), stop=(ch == NCH - 1))
                tmm = kb.inc(ins, "sT")
                kb.wait(kb.V, tmm, tlx, prev.get(("ev", yp)))
                tv1 = kb.inc(kb.V.tensor_add(EV[yp][:], psA[yp][:], XF[par][:, jc, :]), "sV")
                prev[("psA", yp)] = tv1
                kb.wait(kb.P, tv1, tlf, prev.get(("x1w", par)))
                tv2 = kb.inc(kb.P.tensor_add(X1[par][:, jc, :], EV[yp][:], FF[par][:, jc, :]), "sP")
                prev[("ev", yp)] = tv2
            tst = kb.stq(x1_r[:, :, ts], X1[par][:], kb.tok("sP"))
            prev[("x1", par)] = tst
            prev[("x1w", par)] = tst
            prev[("yn", par)] = kb.tok("sT")
        kb.barrier()



def phase_l1_proj(kb, com, zT, Wq, Wk, Wv, Wg, Wgk1, Wgk2, q1T, k1T, g1T, v1, wT):
    nc = kb.nc
    with ExitStack() as st:
        sb = lambda n, s, d: st.enter_context(nc.sbuf_tensor(uniq(n), s, d))
        pst = lambda n, s: st.enter_context(nc.psum_tensor(uniq(n), s, F32))
        WQB = sb("p7_wq", [128, NCH, D], BF16)
        WKB = sb("p7_wk", [128, NCH, D], BF16)
        WVB = sb("p7_wv", [128, NCH, D], BF16)
        WGB = sb("p7_wg", [128, NCH, D], BF16)
        WK1 = sb("p7_wk1", [128, NCH, 16], BF16)
        WK2 = sb("p7_wk2", [16, D], BF16)
        ZB = [sb(f"p7_zb{p}", [128, NCH, TT], BF16) for p in range(2)]
        OB = [sb(f"p7_ob{p}", [128, TT], BF16) for p in range(4)]
        T1B = [sb(f"p7_t1{p}", [16, TT], BF16) for p in range(2)]
        WO_ = [sb(f"p7_wo{p}", [128, TT], F32) for p in range(2)]
        psA = [pst(f"p7_psA{p}", [128, TT]) for p in range(2)]
        psG = [pst(f"p7_psG{p}", [16, TT]) for p in range(2)]
        tw = []
        for Wb, w in ((WQB, Wq), (WKB, Wk), (WVB, Wv), (WGB, Wg)):
            tw.append(kb.ldp(Wb[:], w.rearrange("(c p) j -> p c j", p=128)))
        tw1 = kb.ldp(WK1[:], Wgk1.rearrange("(c p) j -> p c j", p=128))
        tw2 = kb.ldp(WK2[:], Wgk2)
        z_r = zT.rearrange("(c p) t -> p c t", p=128)
        outs_r = [o.rearrange("(c p) t -> p c t", p=128) for o in (q1T, k1T, g1T)]
        w_r = wT.rearrange("(c p) t -> p c t", p=128)
        nbg = com["vecs"]["nbgk2"]
        prev = {}
        ob_i = 0
        for tt in range(NTT):
            par = tt % 2
            ts = slice(tt * TT, (tt + 1) * TT)
            tl = kb.ldq(ZB[par][:], z_r[:, :, ts], prev.get(("zb", par)))
            for wi, (Wb, out_r) in enumerate(((WQB, outs_r[0]), (WKB, outs_r[1]),
                                              (WGB, outs_r[2]))):
                for jc in range(NCH):
                    yp = jc % 2
                    osl = ob_i % 4
                    ob_i += 1
                    kb.wait(kb.T, tl, tw[wi if wi < 2 else 3], prev.get(("psA", yp)))
                    for ch in range(NCH):
                        ins = kb.T.matmul(psA[yp][:], Wb[:, ch, jc * 128:(jc + 1) * 128],
                                          ZB[par][:, ch, :], start=(ch == 0),
                                          stop=(ch == NCH - 1))
                    tmm = kb.inc(ins, "sT")
                    kb.wait(kb.V, tmm, prev.get(("ob", osl)))
                    tev = kb.inc(kb.V.tensor_copy(OB[osl][:], psA[yp][:]), "sV")
                    prev[("psA", yp)] = tev
                    tst = kb.stq(out_r[:, jc, ts], OB[osl][:], tev)
                    prev[("ob", osl)] = tst
            for s4 in range(4):
                for half in range(2):
                    yp = (s4 * 2 + half) % 2
                    osl = ob_i % 4
                    ob_i += 1
                    kb.wait(kb.T, tl, tw[2], prev.get(("psA", yp)))
                    for ch in range(NCH):
                        ins = kb.T.matmul(psA[yp][:], ZB[par][:, ch, s4 * 128:(s4 + 1) * 128],
                                          WVB[:, ch, half * 512:(half + 1) * 512],
                                          start=(ch == 0), stop=(ch == NCH - 1))
                    tmm = kb.inc(ins, "sT")
                    kb.wait(kb.V, tmm, prev.get(("ob", osl)))
                    tev = kb.inc(kb.V.tensor_copy(OB[osl][:], psA[yp][:]), "sV")
                    prev[("psA", yp)] = tev
                    row = tt * TT + s4 * 128
                    tst = kb.stq(v1[row:row + 128, half * 512:(half + 1) * 512],
                                 OB[osl][:], tev)
                    prev[("ob", osl)] = tst
            kb.wait(kb.T, tl, tw1, prev.get(("psG", par)))
            for ch in range(NCH):
                ins = kb.T.matmul(psG[par][:], WK1[:, ch, :], ZB[par][:, ch, :],
                                  start=(ch == 0), stop=(ch == NCH - 1))
            tg1 = kb.inc(ins, "sT")
            kb.wait(kb.V, tg1, prev.get(("t1b", par)))
            tt1 = kb.inc(kb.V.tensor_copy(T1B[par][:], psG[par][:]), "sV")
            prev[("psG", par)] = tt1
            for jc in range(NCH):
                yp = jc % 2
                kb.wait(kb.T, tt1, tw2, prev.get(("psA", yp)))
                tmm = kb.inc(kb.T.matmul(psA[yp][:], WK2[:, jc * 128:(jc + 1) * 128],
                                         T1B[par][:], start=True, stop=True), "sT")
                kb.wait(kb.S, tmm, prev.get(("wo", yp)))
                kb.S.activation(WO_[yp][:], psA[yp][:], AF.Exp, scale=-1.0,
                                bias=nbg[:, jc:jc + 1])
                tso = kb.inc(kb.S.activation(WO_[yp][:], WO_[yp][:], AF.Ln,
                                             bias=com["cpool"][:, 0:1]), "sS")
                prev[("psA", yp)] = tso
                tst = kb.stq(w_r[:, jc, ts], WO_[yp][:], tso)
                prev[("wo", yp)] = tst
            prev[("t1b", par)] = kb.tok("sT")
            prev[("zb", par)] = kb.tok("sT")
        kb.barrier()



def phase_gla(kb, com, q1T, k1T, v1, wT, oT):
    nc = kb.nc
    H1, DK = 4, 256
    NC = T // 128
    with ExitStack() as st:
        sb = lambda n, s, d: st.enter_context(nc.sbuf_tensor(uniq(n), s, d))
        pst = lambda n, s, d=F32: st.enter_context(nc.psum_tensor(uniq(n), s, d))
        QH = [sb(f"g_qh{p}", [128, 2, 128], BF16) for p in range(2)]
        KH = [sb(f"g_kh{p}", [128, 2, 128], BF16) for p in range(2)]
        WH = [sb(f"g_wh{p}", [128, 2, 128], F32) for p in range(2)]
        VH = [sb(f"g_vh{p}", [128, 256], BF16) for p in range(2)]
        GW = [sb(f"g_gw{p}", [128, 2, 128], F32) for p in range(2)]
        A_ = [sb(f"g_a{p}", [128, 2, 128], F32) for p in range(2)]
        AI = [sb(f"g_ai{p}", [128, 2, 128], F32) for p in range(2)]
        QS = [sb(f"g_qs{p}", [128, 2, 128], BF16) for p in range(2)]
        KS = [sb(f"g_ks{p}", [128, 2, 128], BF16) for p in range(2)]
        KE = [sb(f"g_ke{p}", [128, 2, 128], BF16) for p in range(2)]
        KTok = [sb(f"g_kt{p}", [128, 2, 128], BF16) for p in range(2)]
        PT = [sb(f"g_pt{p}", [128, 128], BF16) for p in range(2)]
        OB = [sb(f"g_ob{p}", [128, 2, 128], F32) for p in range(2)]
        HS = [sb(f"g_h{dc}", [128, 256], F32) for dc in range(2)]
        HB = [sb(f"g_hb{dc}", [128, 256], BF16) for dc in range(2)]
        psS = [pst(f"g_psS{p}", [128, 128]) for p in range(2)]
        psO = [pst(f"g_psO{p}", [128, 256]) for p in range(2)]
        psD = [pst(f"g_psD{p}", [128, 512]) for p in range(2)]
        psK = [pst(f"g_psK{p}", [128, 2, 128], BF16) for p in range(2)]
        q_r = q1T.rearrange("(c p) t -> p c t", p=128)
        k_r = k1T.rearrange("(c p) t -> p c t", p=128)
        w_r = wT.rearrange("(c p) t -> p c t", p=128)
        o_r = oT.rearrange("(c p) t -> p c t", p=128)
        v_r = v1.rearrange("(c p) j -> p c j", p=128)
        prev = {}
        for h in range(H1):
            hc0 = 2 * h
            for c in range(NC):
                par = c % 2
                ts = slice(c * 128, (c + 1) * 128)
                fq = prev.get(("qh", par))
                tlq = kb.ldq(QH[par][:], q_r[:, hc0:hc0 + 2, ts], fq)
                tlk = kb.ldq(KH[par][:], k_r[:, hc0:hc0 + 2, ts], fq)
                tlw = kb.ldq(WH[par][:], w_r[:, hc0:hc0 + 2, ts], fq)
                tlv = kb.ldq(VH[par][:], v_r[:, c, 256 * h:256 * h + 256], fq)
                kb.wait(kb.V, tlw, prev.get(("gw", par)))
                for dc in range(2):
                    ins = kb.V.tensor_tensor_scan(GW[par][:, dc, :], WH[par][:, dc, :],
                                                  WH[par][:, dc, :], 0.0,
                                                  ALU.add, ALU.bypass)
                tsc = kb.inc(ins, "sV")
                kb.wait(kb.S, tsc, prev.get(("a", par)))
                kb.S.activation(A_[par][:], GW[par][:], AF.Exp, scale=-1.0 / 16)
                texp = kb.inc(kb.S.activation(AI[par][:], GW[par][:], AF.Exp,
                                              scale=1.0 / 16), "sS")
                kb.wait(kb.P, texp, tlk, prev.get(("qs", par)))
                for dc in range(2):
                    ins = kb.P.tensor_mul(KS[par][:, dc, :], KH[par][:, dc, :],
                                          AI[par][:, dc, :])
                tks = kb.inc(ins, "sP")
                kb.wait(kb.V, texp, tlq, tks, prev.get(("qs", par)))
                for dc in range(2):
                    kb.V.scalar_tensor_tensor(QS[par][:, dc, :], A_[par][:, dc, :],
                                              1.0 / 16, QH[par][:, dc, :],
                                              ALU.mult, ALU.mult)
                    ins = kb.V.tensor_scalar(KE[par][:, dc, :], KS[par][:, dc, :],
                                             A_[par][:, dc, 127:128], None, ALU.mult)
                tqk = kb.inc(ins, "sV")
                kb.wait(kb.T, tqk, prev.get(("psK", par)))
                for dc in range(2):
                    ins = kb.T.transpose(psK[par][:, dc, :], KE[par][:, dc, :],
                                         com["identb"][:])
                tktr = kb.inc(ins, "sT")
                kb.wait(kb.V, tktr, prev.get(("ktok", par)))
                tkt = kb.inc(kb.V.tensor_copy(KTok[par][:], psK[par][:]), "sV")
                prev[("psK", par)] = tkt
                kb.wait(kb.T, tqk, prev.get(("psS", par)))
                for dc in range(2):
                    ins = kb.T.matmul(psS[par][:], KS[par][:, dc, :], QS[par][:, dc, :],
                                      start=(dc == 0), stop=(dc == 1))
                tsm = kb.inc(ins, "sT")
                kb.wait(kb.V, tsm, prev.get(("pt", par)))
                tpt = kb.inc(kb.V.tensor_mul(PT[par][:], psS[par][:], com["tri"][:]), "sV")
                prev[("psS", par)] = tpt
                kb.wait(kb.T, tpt, tlv, prev.get(("psO", par)))
                if c > 0:
                    kb.wait(kb.T, prev.get("hb"))
                for dv in range(2):
                    ins = kb.T.matmul(psO[par][:, dv * 128:(dv + 1) * 128],
                                      VH[par][:, dv * 128:(dv + 1) * 128], PT[par][:],
                                      start=True, stop=(c == 0))
                    if c > 0:
                        for dc in range(2):
                            ins = kb.T.matmul(psO[par][:, dv * 128:(dv + 1) * 128],
                                              HB[dc][:, dv * 128:(dv + 1) * 128],
                                              QS[par][:, dc, :],
                                              start=False, stop=(dc == 1))
                to = kb.inc(ins, "sT")
                kb.wait(kb.V, to, prev.get(("obv", par)))
                tev = kb.inc(kb.V.tensor_copy(OB[par][:], psO[par][:]), "sV")
                prev[("psO", par)] = tev
                tso = kb.stq(o_r[:, hc0:hc0 + 2, ts], OB[par][:], tev)
                prev[("obv", par)] = tso
                kb.wait(kb.T, tkt, tlv, prev.get(("psD", par)))
                for dc in range(2):
                    ins = kb.T.matmul(psD[par][:, dc * 256:(dc + 1) * 256],
                                      KTok[par][:, dc, :], VH[par][:],
                                      start=True, stop=True)
                tdh = kb.inc(ins, "sT")
                if c == 0:
                    kb.wait(kb.V, tdh, prev.get("hb_free"))
                    for dc in range(2):
                        ins = kb.V.tensor_copy(HS[dc][:], psD[par][:, dc * 256:(dc + 1) * 256])
                    thu = kb.inc(ins, "sV")
                else:
                    kb.wait(kb.V, texp, prev.get("hadd"))
                    for dc in range(2):
                        ins = kb.V.tensor_scalar(HS[dc][:], HS[dc][:],
                                                 A_[par][:, dc, 127:128], None, ALU.mult)
                    thscale = kb.inc(ins, "sV")
                    kb.wait(kb.V, thscale, tdh)
                    for dc in range(2):
                        ins = kb.V.tensor_add(HS[dc][:], HS[dc][:],
                                              psD[par][:, dc * 256:(dc + 1) * 256])
                    thu = kb.inc(ins, "sV")
                prev[("psD", par)] = thu
                prev["hadd"] = thu
                kb.wait(kb.P, thu, prev.get("hb"), prev.get("hbT"))
                for dc in range(2):
                    ins = kb.P.tensor_copy(HB[dc][:], HS[dc][:])
                prev["hb"] = kb.inc(ins, "sP")
                prev["hbT"] = to
                prev[("qh", par)] = [kb.tok("sT"), kb.tok("sP")]
                prev[("gw", par)] = texp
                prev[("a", par)] = tqk if c == 0 else [tqk, thscale]
                prev[("qs", par)] = to
                prev[("ktok", par)] = tdh
                prev[("pt", par)] = to
            prev["hb_free"] = prev["hb"]
        kb.barrier()



def phase_gate(kb, com, oT, g1T, ogT):
    nc = kb.nc
    with ExitStack() as st:
        sb = lambda n, s, d: st.enter_context(nc.sbuf_tensor(uniq(n), s, d))
        pst = lambda n, s: st.enter_context(nc.psum_tensor(uniq(n), s, F32))
        OF = [sb(f"gt_of{p}", [128, NCH, TT], F32) for p in range(2)]
        GB = [sb(f"gt_gb{p}", [128, NCH, TT], BF16) for p in range(2)]
        SQ = [sb(f"gt_sq{p}", [128, NCH, TT], BF16) for p in range(2)]
        SG = [sb(f"gt_sg{p}", [128, TT], F32) for p in range(4)]
        TM = [sb(f"gt_tm{p}", [128, TT], F32) for p in range(2)]
        OGB = [sb(f"gt_og{p}", [128, NCH, TT], BF16) for p in range(2)]
        SD = sb("gt_sd", [1, TT], F32)
        RB1 = sb("gt_rb1", [1, TT], BF16)
        psM = [pst(f"gt_psM{p}", [1, TT]) for p in range(2)]
        psB = [pst(f"gt_psB{p}", [128, TT]) for p in range(2)]
        o_r = oT.rearrange("(c p) t -> p c t", p=128)
        g_r = g1T.rearrange("(c p) t -> p c t", p=128)
        og_r = ogT.rearrange("(c p) t -> p c t", p=128)
        rmsw = com["vecs"]["rms_w"]
        prev = {}
        for tt in range(NTT):
            par = tt % 2
            ts = slice(tt * TT, (tt + 1) * TT)
            tlo = kb.ldq(OF[par][:], o_r[:, :, ts], prev.get(("of", par)))
            tlg = kb.ldq(GB[par][:], g_r[:, :, ts], prev.get(("gb", par)))
            kb.wait(kb.S, tlo, prev.get(("sq", par)))
            tsq = kb.inc(kb.S.activation(SQ[par][:], OF[par][:], AF.Square), "sS")
            last = None
            for h in range(4):
                mp = h % 2
                kb.wait(kb.T, tsq, prev.get(("psM", mp)))
                for dc in range(2):
                    ins = kb.T.matmul(psM[mp][:], com["ones_b"][:],
                                      SQ[par][:, 2 * h + dc, :],
                                      start=(dc == 0), stop=(dc == 1))
                tst = kb.inc(ins, "sT")
                kb.wait(kb.S, tst, prev.get("sd"))
                tsd = kb.inc(kb.S.activation(SD[:], psM[mp][:], AF.Sqrt,
                                             scale=1.0 / 256, bias=com["cpool"][:1, 1:2]), "sS")
                kb.wait(kb.V, tsd, prev.get("rb1"))
                tre = kb.inc(kb.V.reciprocal(SD[:], SD[:]), "sV")
                kb.wait(kb.V, tre)
                trb = kb.inc(kb.V.tensor_copy(RB1[:], SD[:]), "sV")
                prev["sd"] = trb
                kb.wait(kb.T, trb, prev.get(("psB", mp)))
                tbc = kb.inc(kb.T.matmul(psB[mp][:], com["ones1_b"][0:1, 0:128],
                                         RB1[:], start=True, stop=True), "sT")
                prev["rb1"] = tbc
                prev[("psM", mp)] = tbc
                for dc in range(2):
                    ch = 2 * h + dc
                    sgl = (2 * h + dc) % 4
                    kb.wait(kb.S, tlg, prev.get(("sg", sgl)))
                    tsg = kb.inc(kb.S.activation(SG[sgl][:], GB[par][:, ch, :], AF.Silu), "sS")
                    kb.wait(kb.V, tbc, tsg, prev.get(("tm", dc)))
                    tv1 = kb.inc(kb.V.scalar_tensor_tensor(
                        TM[dc][:], OF[par][:, ch, :], rmsw[:, dc:dc + 1], psB[mp][:],
                        ALU.mult, ALU.mult), "sV")
                    prev[("psB", mp)] = tv1
                    kb.wait(kb.P, tv1, prev.get(("og", par)))
                    ins = kb.P.tensor_mul(OGB[par][:, ch, :], TM[dc][:], SG[sgl][:])
                    last = kb.inc(ins, "sP")
                    prev[("tm", dc)] = last
                    prev[("sg", sgl)] = last
            tso = kb.stq(og_r[:, :, ts], OGB[par][:], last)
            prev[("og", par)] = tso
            prev[("of", par)] = [kb.tok("sV"), kb.tok("sS")]
            prev[("gb", par)] = kb.tok("sS")
            prev[("sq", par)] = kb.tok("sT")
        kb.barrier()



def phase_l1_merge(kb, com, xhT, x1T, ogT, Wo, xpT, acc0T):
    nc = kb.nc
    with ExitStack() as st:
        sb = lambda n, s, d: st.enter_context(nc.sbuf_tensor(uniq(n), s, d))
        pst = lambda n, s: st.enter_context(nc.psum_tensor(uniq(n), s, F32))
        WOB = sb("m1_wo", [128, NCH, D], BF16)
        YB = [sb(f"m1_yb{p}", [128, NCH, TT], BF16) for p in range(2)]
        XH = [sb(f"m1_xh{p}", [128, NCH, TT], F32) for p in range(2)]
        X1 = [sb(f"m1_x1{p}", [128, NCH, TT], F32) for p in range(2)]
        XP = [sb(f"m1_xp{p}", [128, NCH, TT], F32) for p in range(2)]
        AC = [sb(f"m1_ac{p}", [128, NCH, TT], F32) for p in range(2)]
        psA = [pst(f"m1_psA{p}", [128, TT]) for p in range(2)]
        tw = kb.ldp(WOB[:], Wo.rearrange("(c p) j -> p c j", p=128))
        og_r = ogT.rearrange("(c p) t -> p c t", p=128)
        xh_r = xhT.rearrange("(c p) t -> p c t", p=128)
        x1_r = x1T.rearrange("(c p) t -> p c t", p=128)
        xp_r = xpT.rearrange("(c p) t -> p c t", p=128)
        ac_r = acc0T.rearrange("(c p) t -> p c t", p=128)
        prev = {}
        for tt in range(NTT):
            par = tt % 2
            ts = slice(tt * TT, (tt + 1) * TT)
            tly = kb.ldq(YB[par][:], og_r[:, :, ts], prev.get(("yb", par)))
            tlx = kb.ldq(XH[par][:], xh_r[:, :, ts], prev.get(("w", par)))
            tl1 = kb.ldq(X1[par][:], x1_r[:, :, ts], prev.get(("w", par)))
            for jc in range(NCH):
                yp = jc % 2
                kb.wait(kb.T, tw, tly, prev.get(("psA", yp)))
                for ch in range(NCH):
                    ins = kb.T.matmul(psA[yp][:], WOB[:, ch, jc * 128:(jc + 1) * 128],
                                      YB[par][:, ch, :], start=(ch == 0), stop=(ch == NCH - 1))
                tmm = kb.inc(ins, "sT")
                kb.wait(kb.V, tmm, tlx)
                tv1 = kb.inc(kb.V.tensor_add(XP[par][:, jc, :], psA[yp][:],
                                             XH[par][:, jc, :]), "sV")
                prev[("psA", yp)] = tv1
                kb.wait(kb.P, tv1, tl1)
                tv2 = kb.inc(kb.P.tensor_add(AC[par][:, jc, :], XP[par][:, jc, :],
                                             X1[par][:, jc, :]), "sP")
            ts1 = kb.stq(xp_r[:, :, ts], XP[par][:], kb.tok("sP"))
            ts2 = kb.stq(ac_r[:, :, ts], AC[par][:], kb.tok("sP"))
            prev[("yb", par)] = kb.tok("sT")
            prev[("w", par)] = ts2
        kb.barrier()



def phase_mlp(kb, com, z2T, acc0T, W1, W2, outT):
    nc = kb.nc
    NH = 24
    with ExitStack() as st:
        sb = lambda n, s, d: st.enter_context(nc.sbuf_tensor(uniq(n), s, d))
        pst = lambda n, s: st.enter_context(nc.psum_tensor(uniq(n), s, F32))
        W1B = sb("ml_w1", [128, NCH, 3072], BF16)
        W2B = sb("ml_w2", [128, NH, D], BF16)
        ZB = [sb(f"ml_zb{p}", [128, NCH, TT], BF16) for p in range(2)]
        HSB = sb("ml_hs", [128, NH, TT], BF16)
        AC = [sb(f"ml_ac{p}", [128, NCH, TT], F32) for p in range(2)]
        EV = [sb(f"ml_ev{p}", [128, TT], F32) for p in range(2)]
        OU = [sb(f"ml_ou{p}", [128, TT], F32) for p in range(4)]
        psH = pst("ml_psH", [128, 2048])
        psY = [pst(f"ml_psY{p}", [128, TT]) for p in range(2)]
        tw1 = kb.ldp(W1B[:], W1.rearrange("(c p) i -> p c i", p=128))
        tw2 = kb.ldp(W2B[:], W2.rearrange("(c p) j -> p c j", p=128))
        z_r = z2T.rearrange("(c p) t -> p c t", p=128)
        ac_r = acc0T.rearrange("(c p) t -> p c t", p=128)
        out_r = outT.rearrange("(c p) t -> p c t", p=128)
        b1t = com["vecs"]["b1_mlp"]
        b2t = com["vecs"]["b2_mlp"]
        prev = {}
        for tt in range(NTT):
            par = tt % 2
            ts = slice(tt * TT, (tt + 1) * TT)
            tl = kb.ldq(ZB[par][:], z_r[:, :, ts], prev.get(("zb", par)))
            tla = kb.ldq(AC[par][:], ac_r[:, :, ts], prev.get(("ou", par)))
            for g in range(12):
                hcs = range(g * 2, g * 2 + 2)
                ph = g % 2
                kb.wait(kb.T, tl, tw1, prev.get(("h", ph)))
                for hi, hc in enumerate(hcs):
                    for ch in range(NCH):
                        ins = kb.T.matmul(psH[:, ph * 1024 + hi * 512:ph * 1024 + hi * 512 + TT],
                                          W1B[:, ch, hc * 128:(hc + 1) * 128],
                                          ZB[par][:, ch, :],
                                          start=(ch == 0), stop=(ch == NCH - 1))
                tmm = kb.inc(ins, "sT")
                kb.wait(kb.S, tmm)
                for hi, hc in enumerate(hcs):
                    kb.S.activation(HSB[:, hc, :],
                                    psH[:, ph * 1024 + hi * 512:ph * 1024 + hi * 512 + TT],
                                    AF.Lrelu, alpha=0.01, bias=b1t[:, hc:hc + 1])
                    ins = kb.S.activation(HSB[:, hc, :], HSB[:, hc, :], AF.Square)
                tact = kb.inc(ins, "sS")
                prev[("h", ph)] = tact
            ost_toks = []
            for jc in range(NCH):
                yp = jc % 2
                osl = (tt * NCH + jc) % 4
                kb.wait(kb.T, tw2, prev.get(("y", yp)), kb.tok("sS"))
                for ic in range(NH):
                    ins = kb.T.matmul(psY[yp][:], W2B[:, ic, jc * 128:(jc + 1) * 128],
                                      HSB[:, ic, :], start=(ic == 0), stop=(ic == NH - 1))
                tmy = kb.inc(ins, "sT")
                kb.wait(kb.S, tmy, prev.get(("ev", yp)))
                tev = kb.inc(kb.S.activation(EV[yp][:], psY[yp][:], AF.Identity,
                                             bias=b2t[:, jc:jc + 1]), "sS")
                prev[("y", yp)] = tev
                kb.wait(kb.V, tev, tla, prev.get(("ous", osl)))
                ins = kb.V.tensor_add(OU[osl][:], EV[yp][:], AC[par][:, jc, :])
                tvo = kb.inc(ins, "sV")
                prev[("ev", yp)] = tvo
                tst = kb.stq(out_r[:, jc, ts], OU[osl][:], tvo)
                prev[("ous", osl)] = tst
                ost_toks.append(tst)
            prev[("ou", par)] = ost_toks[-1]
            prev[("zb", par)] = kb.tok("sT")
        kb.barrier()



def phase_ue(kb, com, xhT, ueT, affines):
    nc = kb.nc
    NE = len(affines)
    with ExitStack() as st:
        sb = lambda n, s, d: st.enter_context(nc.sbuf_tensor(uniq(n), s, d))
        XH = [sb(f"ue_xh{p}", [128, NCH, TT], F32) for p in range(2)]
        UB = [sb(f"ue_ub{p}", [128, NCH, TT], BF16) for p in range(4)]
        xh_r = xhT.rearrange("(c p) t -> p c t", p=128)
        ue_r = ueT.rearrange("e (c p) t -> e p c t", p=128)
        prev = {}
        for tt in range(NTT):
            par = tt % 2
            ts = slice(tt * TT, (tt + 1) * TT)
            tl = kb.ldq(XH[par][:], xh_r[:, :, ts], prev.get(("xh", par)))
            for e in range(NE):
                ub = (tt * NE + e) % 4
                g_t, b_t = affines[e]
                kb.wait(kb.V, tl, prev.get(("ub", ub)))
                for ch in range(NCH):
                    ins = kb.V.tensor_scalar(UB[ub][:, ch, :], XH[par][:, ch, :],
                                             g_t[:, ch:ch + 1], b_t[:, ch:ch + 1],
                                             ALU.mult, ALU.add)
                tv = kb.inc(ins, "sV")
                tst = kb.stq(ue_r[e, :, :, ts], UB[ub][:], tv)
                prev[("ub", ub)] = tst
            prev[("xh", par)] = kb.tok("sV")
        kb.barrier()



PHASES_ALL = ["p0", "p1", "p2", "p4a", "p4", "p3",
              "p6", "p7a", "p7", "p7b", "p7c", "p7d",
              "p8a", "p8b", "p8c"]


def build_program(phases=None):
    phases = phases or PHASES_ALL
    kb = KB()
    nc = kb.nc
    with kb.stack:
        ei = lambda n, s, d=F32: nc.dram_tensor(n, s, d, kind="ExternalInput").ap()
        xT = ei("xT", [D, T])
        wq0 = ei("wq0", [D, D]); wk0 = ei("wk0", [D, D]); wv0 = ei("wv0", [D, D])
        wo0 = ei("wo0", [D, D])
        rwT0 = ei("rwT0", [D, 8])
        eb0 = ei("eb0", [1, 8])
        sW1 = ei("sW1", [D, 512]); sW2 = ei("sW2", [512, D])
        rW1 = ei("rW1", [8, D, 512]); rW2 = ei("rW2", [8, 512, D])
        sb1v = ei("sb1v", [512]); sb2v = ei("sb2v", [D])
        rb1v = ei("rb1v", [8, 512]); rb2m = ei("rb2m", [8, D])
        q1w = ei("q1w", [D, D]); k1w = ei("k1w", [D, D])
        v1w = ei("v1w", [D, D]); g1w = ei("g1w", [D, D])
        wgk1 = ei("wgk1", [D, 16]); wgk2 = ei("wgk2", [16, D])
        wo1 = ei("wo1", [D, D])
        w1m = ei("w1m", [D, 3072]); w2m = ei("w2m", [3072, D])
        rwT1 = ei("rwT1", [D, 8]); rb1row = ei("rb1row", [1, 8])
        w1l = ei("w1l", [8, D, 4096]); w2l = ei("w2l", [8, 4096, D])
        b2m8 = ei("b2m8", [8, D])
        ln1wv = ei("ln1wv", [D]); ln1bv = ei("ln1bv", [D])
        ln2wv = ei("ln2wv", [D]); ln2bv = ei("ln2bv", [D])
        bgk2v = ei("bgk2v", [D]); rmswv = ei("rmswv", [256])
        b1mv = ei("b1mv", [3072]); b2mv = ei("b2mv", [D])
        rngv = ei("rngv", [D]); rnbv = ei("rnbv", [D])
        lngv = ei("lngv", [8, D]); lnbv = ei("lnbv", [8, D])
        b1lv = ei("b1lv", [8, 4096])
        consts = declare_consts(nc)
        xn0T = kb.dram("xn0T", [D, T], BF16)
        qnT = kb.dram("qnT", [D, T], BF16)
        knT = kb.dram("knT", [D, T], BF16)
        v0 = kb.dram("v0", [T, 16 * 65], BF16)
        yrT = kb.dram("yrT", [D, T], BF16)
        recT = kb.dram("recT", [16, T], F32)
        FT = kb.dram("FT", [D, T], F32)
        x1T = kb.dram("x1T", [D, T], F32)
        xhT = kb.dram("xhT", [D, T], F32)
        zT = kb.dram("zT", [D, T], BF16)
        q1T = kb.dram("q1T", [D, T], BF16)
        k1T = kb.dram("k1T", [D, T], BF16)
        g1T = kb.dram("g1T", [D, T], BF16)
        v1 = kb.dram("v1", [T, D], BF16)
        wT = kb.dram("wT", [D, T], F32)
        oT = kb.dram("oT", [D, T], F32)
        ogT = kb.dram("ogT", [D, T], BF16)
        xpT = kb.dram("xpT", [D, T], F32)
        acc0T = kb.dram("acc0T", [D, T], F32)
        z2T = kb.dram("z2T", [D, T], BF16)
        ueT = kb.dram("ueT", [9, D, T], BF16)
        out_kind = "ExternalOutput"
        outT = nc.dram_tensor("outT", [D, T], F32, kind=out_kind).ap()

        vecs = {"sb1": (sb1v, 512), "sb2": (sb2v, D)}
        for e in range(8):
            vecs[f"rb1_{e}"] = (rb1v[e, :], 512)
        vecs.update({
            "ln1w": (ln1wv, D), "ln1b": (ln1bv, D),
            "ln2w": (ln2wv, D), "ln2b": (ln2bv, D),
            "bgk2": (bgk2v, D), "rms_w": (rmswv, 256),
            "b1_mlp": (b1mv, 3072), "b2_mlp": (b2mv, D),
            "rn_g": (rngv, D), "rn_b": (rnbv, D),
        })
        for e in range(8):
            vecs[f"lng_{e}"] = (lngv[e, :], D)
            vecs[f"lnb_{e}"] = (lnbv[e, :], D)
            vecs[f"b1l_{e}"] = (b1lv[e, :], 4096)
        com = prep_common(kb, vecs, consts)
        nbg = kb.sbuf("vec_nbgk2", [128, NCH], F32)
        tneg = kb.inc(kb.V.tensor_scalar_mul(nbg[:], com["vecs"]["bgk2"][:], -1.0), "sV")
        com["vecs"]["nbgk2"] = nbg
        for nm in ["sb1"] + [f"rb1_{e}" for e in range(8)]:
            nt = kb.sbuf("vec_n" + nm, [128, 4], F32)
            kb.inc(kb.V.tensor_scalar_mul(nt[:], com["vecs"][nm][:], -1.0), "sV")
            com["vecs"]["n" + nm] = nt
        combT0 = kb.sbuf("combT0", [8, T], BF16)
        combT1 = kb.sbuf("combT1", [8, T], BF16)
        kb.barrier()

        if "p0" in phases:
            phase_ln(kb, com, xT, xn0T)
        if "p1" in phases:
            phase_l0_proj(kb, com, xn0T, wq0, wk0, wv0, qnT, knT, v0)
        if "p4a" in phases:
            phase_router(kb, com, xn0T, rwT0, eb0, combT0, "l0")
            if "combT0d" in DEBUG_OUTPUTS:
                cd = kb.dram("combT0d", [8, T], BF16)
                kb.stq(cd, combT0[:])
                kb.barrier()
        g_attn = g_moe0 = None
        ilv_st = ExitStack()
        if "p2" in phases:
            g_attn = phase_l0_attn(kb, com, qnT, knT, v0, yrT, recT, st=ilv_st)
        if "p4" in phases:
            items = [dict(w1=sW1, w2=sW2, b1=(com["vecs"]["sb1"], 0),
                          b1n=(com["vecs"]["nsb1"], 0), inT=xn0T, cb=None)]
            if "shared_only" not in phases:
                for e in range(8):
                    items.append(dict(w1=rW1[e], w2=rW2[e],
                                      b1=(com["vecs"][f"rb1_{e}"], 0),
                                      b1n=(com["vecs"][f"nrb1_{e}"], 0),
                                      inT=xn0T, cb=e))
            bias2 = dict(lhsT=rb2m, extra=(com["vecs"]["sb2"], 0))
            if "no_bias2" in phases:
                bias2 = None
            act0 = "silu_exp" if "p2" in phases else AF.Silu
            g_moe0 = phase_moe(kb, com, items, act0, FT, combT0, bias2=bias2,
                               slim=("p2" in phases),
                               st=ilv_st if "p2" in phases else None)
        if g_attn is not None or g_moe0 is not None:
            with ilv_st:
                run_gen(g_attn, g_moe0, weights=[4, 1])
            kb.barrier()
        if "p3" in phases:
            phase_l0_merge(kb, com, xT, yrT, recT, FT, wo0, x1T)
        if "seed_x1" in phases:
            kb.ldq(x1T, xT)
            kb.barrier()
        if "p6" in phases:
            phase_ln(kb, com, x1T, zT, dst_f32=xhT,
                     affine=(com["vecs"]["ln1w"], com["vecs"]["ln1b"]))
        if "p7a" in phases:
            phase_l1_proj(kb, com, zT, q1w, k1w, v1w, g1w, wgk1, wgk2,
                          q1T, k1T, g1T, v1, wT)
        if "p7" in phases:
            phase_gla(kb, com, q1T, k1T, v1, wT, oT)
        if "p7b" in phases:
            phase_gate(kb, com, oT, g1T, ogT)
        if "p7c" in phases:
            phase_l1_merge(kb, com, xhT, x1T, ogT, wo1, xpT, acc0T)
        if "p7d" in phases:
            phase_ln(kb, com, xpT, z2T,
                     affine=(com["vecs"]["ln2w"], com["vecs"]["ln2b"]))
            phase_mlp(kb, com, z2T, acc0T, w1m, w2m, outT)
        if "p8a" in phases:
            affs = [(com["vecs"][f"lng_{e}"], com["vecs"][f"lnb_{e}"]) for e in range(8)]
            affs.append((com["vecs"]["rn_g"], com["vecs"]["rn_b"]))
            phase_ue(kb, com, xhT, ueT, affs)
        if "p8b" in phases:
            phase_router(kb, com, ueT[8], rwT1, rb1row, combT1, "l1")
            if "combT1d" in DEBUG_OUTPUTS:
                cd = kb.dram("combT1d", [8, T], BF16)
                kb.stq(cd, combT1[:])
                kb.barrier()
        if "p8c" in phases:
            items = []
            for e in range(8):
                for ib in range(2):
                    items.append(dict(
                        w1=w1l[e][:, ib * 2048:(ib + 1) * 2048],
                        w2=w2l[e][ib * 2048:(ib + 1) * 2048, :],
                        b1=(com["vecs"][f"b1l_{e}"], ib * 16),
                        inT=ueT[e], cb=e))
            run_gen(phase_moe(kb, com, items, AF.Gelu, outT, combT1,
                              bias2=dict(lhsT=b2m8, extra=None), accum_all=True))
            kb.barrier()

        kb.barrier()
    return nc



def host_inputs_per_core(b, inputs):
    ac = np.ascontiguousarray
    im = {"xT": ac(inputs["L0_attn_x"][b].T)}
    im["wq0"] = ac(inputs["L0_attn_w_q"].T)
    im["wk0"] = ac(inputs["L0_attn_w_k"].T)
    im["wv0"] = ac(inputs["L0_attn_w_v"].T)
    im["wo0"] = ac(inputs["L0_attn_w_o"].T)
    im["rwT0"] = ac(inputs["L0_ffn_routing_W"].T)
    im["eb0"] = ac(inputs["L0_ffn_expert_bias"].reshape(1, 8))
    im["sW1"] = inputs["L0_ffn_sW1"]
    im["sW2"] = inputs["L0_ffn_sW2"]
    im["rW1"] = inputs["L0_ffn_rW1"]
    im["rW2"] = inputs["L0_ffn_rW2"]
    im["sb1v"] = inputs["L0_ffn_sb1"]
    im["sb2v"] = inputs["L0_ffn_sb2"]
    im["rb1v"] = inputs["L0_ffn_rb1"]
    im["rb2m"] = inputs["L0_ffn_rb2"]
    im["q1w"] = inputs["L1_attn_Wq"]
    im["k1w"] = inputs["L1_attn_Wk"]
    im["v1w"] = inputs["L1_attn_Wv"]
    im["g1w"] = inputs["L1_attn_Wg"]
    im["wgk1"] = inputs["L1_attn_Wgk1"]
    im["wgk2"] = inputs["L1_attn_Wgk2"]
    im["wo1"] = inputs["L1_attn_Wo"]
    im["w1m"] = inputs["L1_attn_W1"]
    im["w2m"] = inputs["L1_attn_W2"]
    im["rwT1"] = ac(inputs["L1_ffn_router_w"].T)
    im["rb1row"] = ac(inputs["L1_ffn_router_b"].reshape(1, 8))
    im["w1l"] = inputs["L1_ffn_w1"]
    im["w2l"] = inputs["L1_ffn_w2"]
    im["b2m8"] = inputs["L1_ffn_b2"]
    im["ln1wv"] = inputs["L1_attn_ln1_w"]
    im["ln1bv"] = inputs["L1_attn_ln1_b"]
    im["ln2wv"] = inputs["L1_attn_ln2_w"]
    im["ln2bv"] = inputs["L1_attn_ln2_b"]
    im["bgk2v"] = inputs["L1_attn_bgk2"]
    im["rmswv"] = inputs["L1_attn_rms_w"]
    im["b1mv"] = inputs["L1_attn_b1"]
    im["b2mv"] = inputs["L1_attn_b2"]
    im["rngv"] = inputs["L1_ffn_rn_g"]
    im["rnbv"] = inputs["L1_ffn_rn_b"]
    im["lngv"] = inputs["L1_ffn_ln_g"]
    im["lnbv"] = inputs["L1_ffn_ln_b"]
    im["b1lv"] = inputs["L1_ffn_b1"]
    im.update(host_consts())
    return im


def kernel(**inputs):
    nc = build_program()
    in_maps = [host_inputs_per_core(b, inputs) for b in range(8)]
    res = run_bass_kernel_spmd(nc, in_maps, core_ids=list(range(8)))
    out = np.stack([np.ascontiguousarray(r["outT"].T) for r in res.results], axis=0)
    return out


# version 14 (final)
# speedup vs baseline: 1.0959x; 1.0573x over previous
import numpy as np
from contextlib import ExitStack

import concourse.bass as bass
import concourse.mybir as mybir
from concourse.bass_utils import run_bass_kernel_spmd

F32 = mybir.dt.float32
BF16 = mybir.dt.bfloat16
AF = mybir.ActivationFunctionType
ALU = mybir.AluOpType
AX = mybir.AxisListType

T = 4096
D = 1024
TT = 512
NTT = T // TT
NCH = D // 128
LN_EPS = 1e-5
RMS_EPS0 = 1.1920929e-07

DEBUG_OUTPUTS = set()


class KB:

    NDMA = 8

    def __init__(self):
        self.nc = bass.Bass("TRN2", target_bir_lowering=False)
        self.stack = ExitStack()
        self.sems = {}
        self.cnt = {}
        names = ["sT", "sS", "sV", "sP"]
        names += [f"dQ{i}" for i in range(self.NDMA)]
        names += [f"dP{i}" for i in range(self.NDMA)]
        for name in names:
            self.sems[name] = self.stack.enter_context(self.nc.semaphore(name))
            self.cnt[name] = 0
        self._rrq = 0
        self._rrp = 0
        nc = self.nc
        self.T, self.S, self.V, self.P, self.Q = nc.tensor, nc.scalar, nc.vector, nc.gpsimd, nc.sync

    def sbuf(self, name, shape, dtype):
        return self.stack.enter_context(self.nc.sbuf_tensor(name, shape, dtype))

    def psum(self, name, shape, dtype=F32):
        return self.stack.enter_context(self.nc.psum_tensor(name, shape, dtype))

    def dram(self, name, shape, dtype):
        kind = "ExternalOutput" if name in DEBUG_OUTPUTS else "Internal"
        return self.nc.dram_tensor(name, shape, dtype, kind=kind).ap()

    def inc(self, ins, sem, n=1):
        self.cnt[sem] += n
        ins.then_inc(self.sems[sem], n)
        return (sem, self.cnt[sem])

    def tok(self, sem):
        return (sem, self.cnt[sem])

    def wait(self, eng, *toks):
        flat = []
        def _add(t):
            if t is None:
                return
            if isinstance(t, (list, tuple)) and t and isinstance(t[0], (list, tuple)):
                for u in t:
                    _add(u)
            elif isinstance(t, list):
                for u in t:
                    _add(u)
            else:
                flat.append(t)
        for t in toks:
            _add(t)
        best = {}
        for s, v in flat:
            if v > 0:
                best[s] = max(best.get(s, 0), v)
        for s, v in best.items():
            eng.wait_ge(self.sems[s], v)

    def barrier(self):
        toks = [self.tok(s) for s in self.sems]
        for eng in (self.T, self.S, self.V, self.P, self.Q):
            self.wait(eng, *toks)

    def ldq(self, dst, src, *toks):
        self.wait(self.Q, *toks)
        lane = f"dQ{self._rrq}"
        self._rrq = (self._rrq + 1) % self.NDMA
        return self.inc(self.Q.dma_start(dst, src), lane, 16)

    def ldp(self, dst, src, *toks, accum=False):
        self.wait(self.P, *toks)
        kw = {}
        if accum:
            kw["accum_op"] = ALU.add
        lane = f"dP{self._rrp}"
        self._rrp = (self._rrp + 1) % self.NDMA
        return self.inc(self.P.dma_start(dst, src, **kw), lane, 16)

    def stq(self, dst, src, *toks):
        return self.ldq(dst, src, *toks)


def ceil_div(a, b):
    return (a + b - 1) // b


class nullctx:
    def __init__(self, v):
        self.v = v
    def __enter__(self):
        return self.v
    def __exit__(self, *a):
        return False


_UNIQ = [0]


def uniq(name):
    _UNIQ[0] += 1
    return name + "_u" + str(_UNIQ[0])



def prep_common(kb, dram_vecs, consts):
    nc = kb.nc
    c = {}
    c["ident"] = kb.sbuf("ident", [128, 128], F32)
    c["identb"] = kb.sbuf("identb", [128, 128], BF16)
    c["hmask"] = kb.sbuf("hmask", [128, 2], BF16)
    c["hsel"] = kb.sbuf("hsel", [2, 128], BF16)
    c["tri"] = kb.sbuf("tri", [128, 128], BF16)
    c["trineg"] = kb.sbuf("trineg", [128, 128], BF16)
    c["sel8"] = kb.sbuf("sel8", [8, 8 * 128], BF16)
    c["ones_f"] = kb.sbuf("ones_f", [128, 1], F32)
    c["ones_b"] = kb.sbuf("ones_b", [128, 1], BF16)
    c["ones512"] = kb.sbuf("ones512", [128, TT], F32)
    c["ones1_f"] = kb.sbuf("ones1_f", [1, 128], F32)
    c["ones1_b"] = kb.sbuf("ones1_b", [1, 128], BF16)
    c["cpool"] = kb.sbuf("cpool", [128, 8], F32)
    kb.V.memset(c["cpool"][:, 0:1], 1.0)
    kb.V.memset(c["cpool"][:, 1:2], LN_EPS)
    kb.V.memset(c["cpool"][:, 2:3], RMS_EPS0)
    kb.V.memset(c["ones_f"][:], 1.0)
    kb.V.memset(c["ones_b"][:], 1.0)
    kb.V.memset(c["ones512"][:], 1.0)
    kb.V.memset(c["ones1_f"][:], 1.0)
    tvm = kb.inc(kb.V.memset(c["ones1_b"][:], 1.0), "sV")
    tcl = [kb.ldq(c["ident"][:], consts["c_ident"]),
           kb.ldq(c["hmask"][:], consts["c_hmask"]),
           kb.ldq(c["hsel"][:], consts["c_hsel"]),
           kb.ldq(c["tri"][:], consts["c_tri"]),
           kb.ldq(c["trineg"][:], consts["c_trineg"]),
           kb.ldq(c["sel8"][:], consts["c_sel8"]),
           kb.ldp(c["identb"][:], consts["c_ident"])]
    vec_tiles = {}
    for name, (ap, total) in dram_vecs.items():
        rows = ceil_div(total, 128)
        vec_tiles[name] = kb.sbuf("vec_" + name, [128, rows], F32)
    with ExitStack() as st:
        ps = st.enter_context(nc.psum_tensor("ps_prep", [128, 128], F32))
        stage = st.enter_context(nc.sbuf_tensor("stage_prep", [128, 128], F32))
        last_ev = None
        kb.wait(kb.T, tcl, tvm)
        for name, (ap, total) in dram_vecs.items():
            rows = ceil_div(total, 128)
            tile = vec_tiles[name]
            tl = kb.ldq(stage[:rows, :], ap.rearrange("(c p) -> c p", p=128), last_ev)
            kb.wait(kb.T, tl)
            tt_ = kb.inc(kb.T.transpose(ps[:, 0:rows], stage[:rows, :],
                                        c["ident"][:rows, :rows]), "sT")
            kb.wait(kb.V, tt_)
            last_ev = kb.inc(kb.V.tensor_copy(tile[:], ps[:, 0:rows]), "sV")
        kb.barrier()
    c["vecs"] = vec_tiles
    return c


def host_consts():
    import ml_dtypes
    b16 = ml_dtypes.bfloat16
    ident = np.eye(128, dtype=np.float32)
    hmask = np.zeros((128, 2), dtype=b16)
    hmask[0:64, 0] = 1; hmask[64:128, 1] = 1
    hsel = np.zeros((2, 128), dtype=b16)
    hsel[0, 0:64] = 1; hsel[1, 64:128] = 1
    tri = (np.arange(128)[None, :] >= np.arange(128)[:, None]).astype(b16)
    sel8 = np.zeros((8, 8 * 128), dtype=b16)
    for e in range(8):
        sel8[e, e * 128:(e + 1) * 128] = 1
    trineg = np.where(np.arange(128)[None, :] >= np.arange(128)[:, None],
                      0.0, -4096.0).astype(b16)
    trineg = np.ascontiguousarray(trineg.T)
    return {"c_ident": ident, "c_hmask": hmask, "c_hsel": hsel,
            "c_tri": tri, "c_sel8": sel8, "c_trineg": trineg}


def declare_consts(nc):
    return {
        "c_ident": nc.dram_tensor("c_ident", [128, 128], F32, kind="ExternalInput").ap(),
        "c_hmask": nc.dram_tensor("c_hmask", [128, 2], BF16, kind="ExternalInput").ap(),
        "c_hsel": nc.dram_tensor("c_hsel", [2, 128], BF16, kind="ExternalInput").ap(),
        "c_tri": nc.dram_tensor("c_tri", [128, 128], BF16, kind="ExternalInput").ap(),
        "c_trineg": nc.dram_tensor("c_trineg", [128, 128], BF16, kind="ExternalInput").ap(),
        "c_sel8": nc.dram_tensor("c_sel8", [8, 8 * 128], BF16, kind="ExternalInput").ap(),
    }



def phase_ln(kb, com, src_f32, dst_bf16, dst_f32=None, affine=None):
    nc = kb.nc
    with ExitStack() as st:
        sb = lambda n, s, d: st.enter_context(nc.sbuf_tensor(uniq(n), s, d))
        pst = lambda n, s: st.enter_context(nc.psum_tensor(uniq(n), s, F32))
        XF = [sb(f"ln_x{p}", [128, NCH, TT], F32) for p in range(2)]
        SQ = [sb(f"ln_sq{p}", [128, NCH, TT], F32) for p in range(2)]
        EV = [sb(f"ln_ev{p}", [128, 2 * TT], F32) for p in range(2)]
        OB = [sb(f"ln_ob{p}", [128, NCH, TT], BF16) for p in range(2)]
        OF = [sb(f"ln_of{p}", [128, NCH, TT], F32) for p in range(2)] if dst_f32 is not None else None
        ST = [sb(f"ln_st{p}", [1, 8], F32) for p in range(2)]
        psS = [pst(f"ln_psS{p}", [1, TT]) for p in range(2)]
        psQ = [pst(f"ln_psQ{p}", [1, TT]) for p in range(2)]
        psM = [pst(f"ln_psM{p}", [128, TT]) for p in range(2)]
        psR = [pst(f"ln_psR{p}", [128, TT]) for p in range(2)]
        MR = [sb(f"ln_mr{p}", [1, 2 * TT], F32) for p in range(2)]

        src_r = src_f32.rearrange("(c p) t -> p c t", p=128)
        dst_r = dst_bf16.rearrange("(c p) t -> p c t", p=128) if dst_bf16 is not None else None
        dstf_r = dst_f32.rearrange("(c p) t -> p c t", p=128) if dst_f32 is not None else None

        done = {}
        for tt in range(NTT):
            par = tt % 2
            ts = slice(tt * TT, (tt + 1) * TT)
            free = done.get(par)
            tl = kb.ldq(XF[par][:], src_r[:, :, ts], *(free or []))
            kb.wait(kb.S, tl)
            tsq = kb.inc(kb.S.activation(SQ[par][:], XF[par][:], AF.Square), "sS")
            kb.wait(kb.T, tl)
            for ch in range(NCH):
                kb.T.matmul(psS[par][:], com["ones_f"][:], XF[par][:, ch, :],
                            start=(ch == 0), stop=(ch == NCH - 1))
            kb.wait(kb.T, tsq)
            for ch in range(NCH):
                ins = kb.T.matmul(psQ[par][:], com["ones_f"][:], SQ[par][:, ch, :],
                                  start=(ch == 0), stop=(ch == NCH - 1))
            tstat = kb.inc(ins, "sT")
            m = MR[par][:, 0:TT]
            sq_m = EV[par][:1, 0:TT]
            var = EV[par][:1, TT:2 * TT]
            kb.wait(kb.V, tstat)
            tv1 = kb.inc(kb.V.tensor_scalar_mul(m, psS[par][:], 1.0 / D), "sV")
            kb.wait(kb.S, tv1)
            tm2 = kb.inc(kb.S.activation(sq_m, m, AF.Square), "sS")
            kb.wait(kb.V, tm2)
            tvar = kb.inc(kb.V.scalar_tensor_tensor(var, psQ[par][:], 1.0 / D, sq_m,
                                                    ALU.mult, ALU.subtract), "sV")
            kb.wait(kb.S, tvar)
            sd = EV[par][:1, 0:TT]
            tsd = kb.inc(kb.S.activation(sd, var, AF.Sqrt, bias=com["cpool"][:1, 1:2]), "sS")
            kb.wait(kb.V, tsd)
            r = MR[par][:, TT:2 * TT]
            trec = kb.inc(kb.V.reciprocal(r, sd), "sV")
            kb.wait(kb.T, trec)
            kb.T.matmul(psM[par][:], com["ones1_f"][:], m, start=True, stop=True)
            tbc = kb.inc(kb.T.matmul(psR[par][:], com["ones1_f"][:], r,
                                     start=True, stop=True), "sT")
            kb.wait(kb.V, tbc)
            tmp = EV[par][:, 0:TT]
            for ch in range(NCH):
                kb.V.tensor_sub(tmp, XF[par][:, ch, :], psM[par][:])
                if OF is not None:
                    ins = kb.V.tensor_mul(OF[par][:, ch, :], tmp, psR[par][:])
                    src_n = OF[par][:, ch, :]
                else:
                    ins = kb.V.tensor_mul(tmp, tmp, psR[par][:])
                    src_n = tmp
                if affine is not None:
                    w_t, b_t = affine
                    ins = kb.V.tensor_scalar(OB[par][:, ch, :], src_n,
                                             w_t[:, ch:ch + 1], b_t[:, ch:ch + 1],
                                             ALU.mult, ALU.add)
                elif dst_bf16 is not None:
                    ins = kb.V.tensor_copy(OB[par][:, ch, :], src_n)
            tap = kb.inc(ins, "sV")
            outs = []
            if dst_bf16 is not None:
                outs.append(kb.stq(dst_r[:, :, ts], OB[par][:], tap))
            if dstf_r is not None:
                outs.append(kb.stq(dstf_r[:, :, ts], OF[par][:], tap))
            done[par] = outs + [tap]
        kb.barrier()



def phase_l0_proj(kb, com, xn0T, wq, wk, wv, qnT, knT, v0):
    nc = kb.nc
    with ExitStack() as st:
        sb = lambda n, s, d: st.enter_context(nc.sbuf_tensor(uniq(n), s, d))
        pst = lambda n, s: st.enter_context(nc.psum_tensor(uniq(n), s, F32))
        WQ = sb("p1_wq", [128, NCH, D], BF16)
        WK = sb("p1_wk", [128, NCH, D], BF16)
        WV = sb("p1_wv", [128, NCH, D], BF16)
        XB = [sb(f"p1_xb{p}", [128, NCH, TT], BF16) for p in range(2)]
        SQT = [sb(f"p1_sq{p}", [128, TT], BF16) for p in range(2)]
        QF = [sb(f"p1_qf{p}", [128, TT], F32) for p in range(2)]
        OB = [sb(f"p1_ob{p}", [128, TT], BF16) for p in range(2)]
        SD = [sb(f"p1_sd{p}", [2, 2 * TT], F32) for p in range(2)]
        RRB = [sb(f"p1_rr{p}", [2, TT], BF16) for p in range(2)]
        VO = [sb(f"p1_vo{p}", [128, 16, 65], BF16) for p in range(2)]
        psA = [pst(f"p1_psA{p}", [128, TT]) for p in range(2)]
        psST = [pst(f"p1_psST{p}", [2, TT]) for p in range(2)]
        psB = [pst(f"p1_psB{p}", [128, TT]) for p in range(2)]

        twq = kb.ldp(WQ[:], wq.rearrange("(c p) j -> p c j", p=128))
        twk = kb.ldp(WK[:], wk.rearrange("(c p) j -> p c j", p=128))
        twv = kb.ldp(WV[:], wv.rearrange("(c p) j -> p c j", p=128))
        xn_r = xn0T.rearrange("(c p) t -> p c t", p=128)
        qn_r = qnT.rearrange("(c p) t -> p c t", p=128)
        kn_r = knT.rearrange("(c p) t -> p c t", p=128)

        free = {}
        unit = 0
        for tt in range(NTT):
            ts = slice(tt * TT, (tt + 1) * TT)
            tl = kb.ldq(XB[tt % 2][:], xn_r[:, :, ts], *free.get(("xb", tt % 2), []))
            xb = XB[tt % 2]
            for W, wtok, dst_r in ((WQ, twq, qn_r), (WK, twk, kn_r)):
                for jc in range(NCH):
                    par = unit % 2
                    unit += 1
                    kb.wait(kb.T, tl, wtok, *free.get(("qk", par), []))
                    for ch in range(NCH):
                        ins = kb.T.matmul(psA[par][:], W[:, ch, jc * 128:(jc + 1) * 128],
                                          xb[:, ch, :], start=(ch == 0), stop=(ch == NCH - 1))
                    tmm = kb.inc(ins, "sT")
                    kb.wait(kb.S, tmm)
                    tsq = kb.inc(kb.S.activation(SQT[par][:], psA[par][:], AF.Square), "sS")
                    kb.wait(kb.T, tsq)
                    tst = kb.inc(kb.T.matmul(psST[par][:], com["hmask"][:], SQT[par][:],
                                             start=True, stop=True), "sT")
                    kb.wait(kb.S, tst)
                    sd = SD[par][:, 0:TT]
                    tsd = kb.inc(kb.S.activation(sd, psST[par][:], AF.Sqrt,
                                                 scale=1.0 / 64, bias=com["cpool"][:2, 2:3]), "sS")
                    kb.wait(kb.V, tsd)
                    rr = SD[par][:, TT:2 * TT]
                    tre = kb.inc(kb.V.reciprocal(rr, sd), "sV")
                    kb.wait(kb.V, tre)
                    trr = kb.inc(kb.V.tensor_copy(RRB[par][:], rr), "sV")
                    kb.wait(kb.T, trr)
                    tbc = kb.inc(kb.T.matmul(psB[par][:], com["hsel"][:], RRB[par][:],
                                             start=True, stop=True), "sT")
                    kb.wait(kb.V, tbc)
                    kb.V.tensor_copy(QF[par][:], psA[par][:])
                    tev = kb.inc(kb.V.tensor_mul(OB[par][:], QF[par][:], psB[par][:]), "sV")
                    tst_o = kb.stq(dst_r[:, jc, ts], OB[par][:], tev)
                    free[("qk", par)] = [tev, tst_o]
            for s4 in range(4):
                par = s4 % 2
                vo = VO[par]
                kb.wait(kb.V, *free.get(("vo", par), []))
                tvm = kb.inc(kb.V.memset(vo[:, :, 64:65], 1.0), "sV")
                toks_half = []
                for half in range(2):
                    parp = (unit + half) % 2
                    kb.wait(kb.T, tl, twv, *free.get(("qk", parp), []))
                    for ch in range(NCH):
                        ins = kb.T.matmul(psA[parp][:],
                                          xb[:, ch, s4 * 128:(s4 + 1) * 128],
                                          WV[:, ch, half * 512:(half + 1) * 512],
                                          start=(ch == 0), stop=(ch == NCH - 1))
                    tmm = kb.inc(ins, "sT")
                    kb.wait(kb.V, tmm, tvm)
                    tev = kb.inc(kb.V.tensor_copy(
                        vo[:, half * 8:(half + 1) * 8, 0:64],
                        psA[parp][:].rearrange("p (h v) -> p h v", v=64)), "sV")
                    free[("qk", parp)] = [tev]
                    toks_half.append(tev)
                unit += 2
                row = tt * TT + s4 * 128
                tso = kb.stq(v0[row:row + 128, :], vo[:], *toks_half)
                free[("vo", par)] = [tso]
            free[("xb", tt % 2)] = [kb.tok("sT")]
        kb.barrier()



def phase_l0_attn(kb, com, qnT, knT, v0, yrT, recT, st=None):
    nc = kb.nc
    H = 16
    QTL = 512
    NQI = T // QTL
    ND = 3
    with (ExitStack() if st is None else nullctx(st)) as st:
        sb = lambda n, s, d: st.enter_context(nc.sbuf_tensor(uniq(n), s, d))
        pst = lambda n, s: st.enter_context(nc.psum_tensor(uniq(n), s, F32))
        KT = [sb(f"p2_kt{p}", [64, T], BF16) for p in range(2)]
        VT = [sb(f"p2_vt{p}", [128, 32, 65], BF16) for p in range(2)]
        QT = [sb(f"p2_qt{p}", [64, QTL], BF16) for p in range(2)]
        TS = sb("p2_ts", [128, QTL], F32)
        PB = [sb(f"p2_pb{p}", [128, QTL], BF16) for p in range(4)]
        YB = [sb(f"p2_yb{p}", [64, QTL], BF16) for p in range(2)]
        RC = [sb(f"p2_rc{p}", [1, QTL], F32) for p in range(2)]
        psS = [pst(f"p2_psS{p}", [128, QTL]) for p in range(ND)]
        psY = [pst(f"p2_psY{p}", [65, QTL]) for p in range(2)]

        freeS = {i: None for i in range(ND)}
        freePB = {i: None for i in range(4)}
        freeQ = {0: None, 1: None}
        freeY = {0: None, 1: None}
        freeYB = {0: None, 1: None}
        kvload = {}
        kvfree = {0: [], 1: []}

        units = [(h, qi) for h in range(H) for qi in range(NQI)]
        qload = {}

        def load_q(i):
            if i >= len(units) or i in qload:
                return
            h, qi = units[i]
            up = i % 2
            qload[i] = kb.ldq(QT[up][:], qnT[64 * h:64 * h + 64,
                                             QTL * qi:QTL * qi + QTL], freeQ[up])

        def load_kv(h):
            hp = h % 2
            tK = kb.ldq(KT[hp][:], knT[64 * h:64 * h + 64, :], *kvfree[hp])
            tV = kb.ldq(VT[hp][:],
                        v0.rearrange("(c p) e -> p c e", p=128)[:, :, 65 * h:65 * h + 65],
                        *kvfree[hp])
            kvload[h] = (tK, tV)

        load_kv(0)
        load_q(0)
        for ui, (h, qi) in enumerate(units):
            hp = h % 2
            up = ui % 2
            tK, tV = kvload[h]
            tQ = qload[ui]
            load_q(ui + 1)
            if qi == 0 and h + 1 < H:
                load_kv(h + 1)
            tq0 = QTL * qi
            nkj = 4 * (qi + 1)
            head_last = None

            def emit_pv(kj, f0, tmask):
                nonlocal head_last
                pb = kj % 4
                kb.wait(kb.T, tmask, tV)
                if kj == 0:
                    kb.wait(kb.T, freeY[up])
                ins = kb.T.matmul(psY[up][:, f0:QTL], VT[hp][:, kj, :],
                                  PB[pb][:, f0:QTL],
                                  start=(kj == 0), stop=(kj == nkj - 1))
                tpv = kb.inc(ins, "sT")
                freePB[pb] = tpv
                head_last = tpv

            pend = []
            for kj in range(nkj):
                sl = kj % ND

                tk0 = 128 * kj
                fd = tk0 - tq0
                f0 = max(0, fd)
                kb.wait(kb.T, tQ, tK, freeS[sl])
                masked = fd >= 0
                ins = kb.T.matmul(psS[sl][:, f0:QTL], KT[hp][:, tk0:tk0 + 128],
                                  QT[up][:, f0:QTL], start=True, stop=not masked)
                if masked:
                    ins = kb.T.matmul(psS[sl][:, fd:fd + 128], com["trineg"][:],
                                      com["identb"][:], start=False, stop=True,
                                      skip_group_check=True)
                tmm = kb.inc(ins, "sT")
                pb = kj % 4
                kb.wait(kb.S, tmm, freePB[pb])
                tth = kb.inc(kb.S.activation(TS[:, f0:QTL], psS[sl][:, f0:QTL], AF.Tanh,
                                             scale=1.0 / 400.0), "sS")
                tse = kb.inc(kb.S.activation(PB[pb][:, f0:QTL], TS[:, f0:QTL],
                                             AF.Exp, scale=50.0), "sS")
                freeS[sl] = tth
                pend.append((kj, f0, tse))
                if len(pend) == 3:
                    emit_pv(*pend.pop(0))
                yield
            while pend:
                emit_pv(*pend.pop(0))
            freeQ[up] = head_last
            kb.wait(kb.V, head_last, freeYB[up])
            kb.V.reciprocal(RC[up][:], psY[up][64:65, :])
            tev = kb.inc(kb.V.tensor_copy(YB[up][:], psY[up][0:64, :]), "sV")
            freeY[up] = tev
            kb.stq(yrT[64 * h:64 * h + 64, tq0:tq0 + QTL], YB[up][:], tev)
            t2 = kb.stq(recT[h:h + 1, tq0:tq0 + QTL], RC[up][:], tev)
            freeYB[up] = t2
            if qi == NQI - 1:
                kvfree[hp] = [head_last]



def phase_router(kb, com, inT_bf16, rwT, bias_row_ap, combT, mode):
    nc = kb.nc
    with ExitStack() as st:
        sb = lambda n, s, d: st.enter_context(nc.sbuf_tensor(uniq(n), s, d))
        pst = lambda n, s: st.enter_context(nc.psum_tensor(uniq(n), s, F32))
        RW = sb("rt_rw", [128, NCH, 8], BF16)
        EB = sb("rt_eb", [1, 8], F32)
        XB = [sb(f"rt_xb{p}", [128, NCH, TT], BF16) for p in range(2)]
        LG = sb("rt_lg", [128, 32], F32)
        ACT = sb("rt_act", [128, 32], F32)
        SCR = sb("rt_scr", [128, 32], F32)
        MK = sb("rt_mk", [128, 32], F32)
        CMB = sb("rt_cmb", [128, 32], F32)
        M1 = sb("rt_m1", [128, 4], F32)
        M2 = sb("rt_m2", [128, 4], F32)
        DEN = sb("rt_den", [128, 4], F32)
        psL = pst("rt_psL", [128, 32])
        psT = pst("rt_psT", [8, 128])
        tw = kb.ldp(RW[:], rwT.rearrange("(c p) e -> p c e", p=128))
        te = kb.ldq(EB[:], bias_row_ap)
        in_r = inT_bf16.rearrange("(c p) t -> p c t", p=128)
        prev = {}
        ntt = 1 if "rt_one_tt" in DEBUG_OUTPUTS else NTT
        for tt in range(ntt):
            par = tt % 2
            ts = slice(tt * TT, (tt + 1) * TT)
            tl = kb.ldq(XB[par][:], in_r[:, :, ts], prev.get(("xb", par)))
            kb.wait(kb.T, tl, tw, te, prev.get("evac"))
            for s in range(4):
                sl = slice(s * 8, (s + 1) * 8)
                for ch in range(NCH):
                    kb.T.matmul(psL[:, sl], XB[par][:, ch, s * 128:(s + 1) * 128],
                                RW[:, ch, :], start=(ch == 0), stop=False)
                ins = kb.T.matmul(psL[:, sl], com["ones1_f"][:], EB[:],
                                  start=False, stop=True, skip_group_check=True)
            tmm = kb.inc(ins, "sT")
            kb.wait(kb.V, tmm)
            tlg = kb.inc(kb.V.tensor_copy(LG[:], psL[:]), "sV")
            if "rt_lgd" in DEBUG_OUTPUTS:
                if tt == 0 and not hasattr(kb, "_rt_lgd"):
                    kb._rt_lgd = kb.dram("rt_lgd", [NTT * 128, 32], F32)
                    kb._rt_actd = kb.dram("rt_actd", [NTT * 128, 32], F32)
                kb.stq(kb._rt_lgd[tt * 128:(tt + 1) * 128, :], LG[:], tlg)
            if mode == "l0":
                kb.wait(kb.S, tlg)
                kb.S.activation(SCR[:], LG[:], AF.Exp)
                kb.S.activation(SCR[:], SCR[:], AF.Ln, bias=com["cpool"][:, 0:1])
                tact = kb.inc(kb.S.activation(ACT[:], SCR[:], AF.Sqrt), "sS")
                if "rt_lgd" in DEBUG_OUTPUTS:
                    kb.stq(kb._rt_actd[tt * 128:(tt + 1) * 128, :], ACT[:], tact)
                kb.wait(kb.V, tact)
                base = ACT
            else:
                base = LG
            def vedge(ins):
                t = kb.inc(ins, "sV")
                kb.wait(kb.V, t)
                return t
            for s in range(4):
                ins = kb.V.reduce_max(M1[:, s:s + 1], base[:, s * 8:(s + 1) * 8], AX.X)
            vedge(ins)
            for s in range(4):
                sl = slice(s * 8, (s + 1) * 8)
                ins = kb.V.tensor_scalar(SCR[:, sl], base[:, sl], M1[:, s:s + 1],
                                         None, ALU.is_ge)
            vedge(ins)
            for s in range(4):
                sl = slice(s * 8, (s + 1) * 8)
                ins = kb.V.scalar_tensor_tensor(SCR[:, sl], SCR[:, sl], -1e30, base[:, sl],
                                                ALU.mult, ALU.add)
            vedge(ins)
            for s in range(4):
                ins = kb.V.reduce_max(M2[:, s:s + 1], SCR[:, s * 8:(s + 1) * 8], AX.X)
            vedge(ins)
            kb.wait(kb.V, prev.get("ctT"))
            for s in range(4):
                sl = slice(s * 8, (s + 1) * 8)
                ins = kb.V.tensor_scalar(MK[:, sl], base[:, sl], M2[:, s:s + 1],
                                         None, ALU.is_ge)
            vedge(ins)
            for s in range(4):
                sl = slice(s * 8, (s + 1) * 8)
                if mode == "l0":
                    ins_cmb = kb.V.tensor_mul(CMB[:, sl], base[:, sl], MK[:, sl])
            if mode == "l1":
                tv = kb.inc(kb.V.tensor_sub(M2[:], M2[:], M1[:]), "sV")
                kb.wait(kb.S, tv)
                tde = kb.inc(kb.S.activation(DEN[:], M2[:], AF.Exp), "sS")
                for s in range(4):
                    sl = slice(s * 8, (s + 1) * 8)
                    ins = kb.V.tensor_scalar(SCR[:, sl], base[:, sl], M1[:, s:s + 1], None,
                                             ALU.subtract)
                tsu = kb.inc(ins, "sV")
                kb.wait(kb.S, tsu)
                tex = kb.inc(kb.S.activation(SCR[:], SCR[:], AF.Exp), "sS")
                kb.wait(kb.V, tex, tde)
                ta1 = kb.inc(kb.V.tensor_scalar_add(DEN[:], DEN[:], 1.0), "sV")
                kb.wait(kb.V, ta1)
                trc = kb.inc(kb.V.reciprocal(DEN[:], DEN[:]), "sV")
                kb.V.tensor_mul(SCR[:], SCR[:], MK[:])
                kb.wait(kb.V, trc)
                for s in range(4):
                    sl = slice(s * 8, (s + 1) * 8)
                    ins_cmb = kb.V.tensor_scalar(CMB[:, sl], SCR[:, sl], DEN[:, s:s + 1],
                                                 None, ALU.mult)
            tcmb = kb.inc(ins_cmb, "sV")
            if "rt_lgd" in DEBUG_OUTPUTS:
                if tt == 0 and not hasattr(kb, "_rt_cmbd"):
                    kb._rt_cmbd = kb.dram("rt_cmbd", [NTT * 128, 32], F32)
                kb.stq(kb._rt_cmbd[tt * 128:(tt + 1) * 128, :], CMB[:], tcmb)
            for s in range(4):
                kb.wait(kb.T, tcmb, prev.get("ct"))
                ttr = kb.inc(kb.T.transpose(psT[:], CMB[:, s * 8:(s + 1) * 8],
                                            com["ident"][:]), "sT")
                kb.wait(kb.V, ttr)
                tev = kb.inc(kb.V.tensor_copy(
                    combT[:, tt * TT + s * 128: tt * TT + (s + 1) * 128], psT[:]), "sV")
                prev["ct"] = tev
                prev["ctT"] = ttr
            prev["evac"] = tlg
            prev[("xb", par)] = kb.tok("sT")
        kb.barrier()



def phase_moe(kb, com, items, act_fn, out_ap, combT, bias2=None, accum_all=False,
              slim=False, st=None):
    nc = kb.nc
    nIc_max = max(it["w1"].shape[-1] for it in items) // 128
    g_hc = 1 if slim else 2
    n_psH = 2
    n_psY = 1 if slim else 2
    with (ExitStack() if st is None else nullctx(st)) as st:
        sb = lambda n, s, d: st.enter_context(nc.sbuf_tensor(uniq(n), s, d))
        pst = lambda n, s: st.enter_context(nc.psum_tensor(uniq(n), s, F32))
        W1B = [sb(f"mo_w1{p}", [128, NCH, nIc_max * 128], BF16)
               for p in range(1 if slim else 2)]
        W2B = sb("mo_w2", [128, nIc_max, D], BF16)
        XB = [sb(f"mo_xb{p}", [128, NCH, TT], BF16) for p in range(2)]
        HS = sb("mo_hs", [128, nIc_max, TT], BF16)
        CB = [sb(f"mo_cb{p}", [128, TT], F32) for p in range(2)]
        if act_fn == "silu_exp":
            moE = [sb(f"mo_e{p}", [128, g_hc, TT], F32) for p in range(2)]
            moT = [sb(f"mo_t{p}", [128, g_hc, TT], F32) for p in range(2)]
        YS = [sb(f"mo_ys{p}", [128, NCH, TT], F32) for p in range(2)]
        psH = pst("mo_psH", [128, 512 * g_hc * 2])
        psY = [pst(f"mo_psY{p}", [128, TT]) for p in range(n_psY)]
        psCB = psY[0] if slim else pst("mo_psCB", [128, TT])
        out_r = out_ap.rearrange("(c p) t -> p c t", p=128)
        fstore = {}
        prev = {}
        dbw1 = not slim
        w1tok = {}

        def load_w1(i):
            if i >= len(items) or i in w1tok:
                return
            nIc_i = items[i]["w1"].shape[-1] // 128
            wp = i % 2 if dbw1 else 0
            w1tok[i] = kb.ldp(W1B[wp][:, :, 0:nIc_i * 128],
                              items[i]["w1"].rearrange("(c p) i -> p c i", p=128),
                              prev.get(("w1", wp)))

        load_w1(0)
        for it_i, it in enumerate(items):
            nIc = it["w1"].shape[-1] // 128
            wp = it_i % 2 if dbw1 else 0
            tw1 = w1tok[it_i]
            W1C = W1B[wp]
            tw2 = kb.ldp(W2B[:, 0:nIc, :],
                         it["w2"].rearrange("(c p) j -> p c j", p=128),
                         prev.get("w2"))
            in_r = it["inT"].rearrange("(c p) t -> p c t", p=128)
            if dbw1:
                load_w1(it_i + 1)
            for tt in range(NTT):
                par = tt % 2
                ts = slice(tt * TT, (tt + 1) * TT)
                tl = kb.ldq(XB[par][:], in_r[:, :, ts], prev.get(("xb", par)))
                if it["cb"] is not None:
                    e = it["cb"]
                    kb.wait(kb.T, prev.get("cbps"), prev.get(("y", 0)) if slim else None)
                    tcb = kb.inc(kb.T.matmul(psCB[:], com["sel8"][:, e * 128:(e + 1) * 128],
                                             combT[:, ts], start=True, stop=True), "sT")
                    kb.wait(kb.V, tcb)
                    tcbe = kb.inc(kb.V.tensor_copy(CB[par][:], psCB[:]), "sV")
                    prev["cbps"] = tcbe
                    if slim:
                        prev[("y", 0)] = tcbe
                else:
                    tcbe = None
                b1t, b1c = it["b1"]
                for g in range(ceil_div(nIc, g_hc)):
                    hcs = range(g * g_hc, min(nIc, g * g_hc + g_hc))
                    ph = g % 2
                    kb.wait(kb.T, tl, tw1, prev.get(("h", ph)))
                    for hi, hc in enumerate(hcs):
                        o0 = ph * g_hc * 512 + hi * 512
                        for ch in range(NCH):
                            ins = kb.T.matmul(psH[:, o0:o0 + TT],
                                              W1C[:, ch, hc * 128:(hc + 1) * 128],
                                              XB[par][:, ch, :],
                                              start=(ch == 0), stop=(ch == NCH - 1))
                    tmm = kb.inc(ins, "sT")
                    kb.wait(kb.S, tmm)
                    if act_fn == "silu_exp":
                        nb1t, _ = it["b1n"]
                        EE = moE[ph]
                        TTt = moT[ph]
                        kb.wait(kb.S, prev.get(("hv", ph)))
                        for hi, hc in enumerate(hcs):
                            o0 = ph * g_hc * 512 + hi * 512
                            ins = kb.S.activation(EE[:, hi, :], psH[:, o0:o0 + TT], AF.Exp,
                                                  scale=-1.0,
                                                  bias=nb1t[:, b1c + hc:b1c + hc + 1])
                        tact_s = kb.inc(ins, "sS")
                        kb.wait(kb.V, tact_s, prev.get(("hv", ph)))
                        for hi, hc in enumerate(hcs):
                            o0 = ph * g_hc * 512 + hi * 512
                            kb.V.tensor_scalar(TTt[:, hi, :], psH[:, o0:o0 + TT],
                                               b1t[:, b1c + hc:b1c + hc + 1], None,
                                               ALU.add)
                            kb.V.tensor_add(EE[:, hi, :], EE[:, hi, :], com["ones512"][:])
                            ins = kb.V.reciprocal(EE[:, hi, :], EE[:, hi, :])
                        tv2 = kb.inc(ins, "sV")
                        kb.wait(kb.V, tv2)
                        for hi, hc in enumerate(hcs):
                            ins = kb.V.tensor_mul(HS[:, hc, :], TTt[:, hi, :], EE[:, hi, :])
                        tact = kb.inc(ins, "sV")
                        prev[("hv", ph)] = tact
                        last_tact = tact
                    else:
                        for hi, hc in enumerate(hcs):
                            o0 = ph * g_hc * 512 + hi * 512
                            ins = kb.S.activation(HS[:, hc, :], psH[:, o0:o0 + TT],
                                                  act_fn, bias=b1t[:, b1c + hc:b1c + hc + 1])
                        tact = kb.inc(ins, "sS")
                    prev[("h", ph)] = tact
                    last_tact = tact
                    yield
                for jc in range(NCH):
                    yp = 0 if slim else jc % 2
                    kb.wait(kb.T, tw2, prev.get(("y", yp)), last_tact)
                    for ic in range(nIc):
                        ins = kb.T.matmul(psY[yp][:], W2B[:, ic, jc * 128:(jc + 1) * 128],
                                          HS[:, ic, :], start=(ic == 0), stop=(ic == nIc - 1))
                    tmy = kb.inc(ins, "sT")
                    kb.wait(kb.V, tmy, tcbe, prev.get(("ysw", par)) if jc == 0 else None)
                    if it["cb"] is not None:
                        tev = kb.inc(kb.V.tensor_mul(YS[par][:, jc, :], psY[yp][:],
                                                     CB[par][:]), "sV")
                    else:
                        tev = kb.inc(kb.V.tensor_copy(YS[par][:, jc, :], psY[yp][:]), "sV")
                    prev[("y", yp)] = tev
                    if jc % 2 == 1:
                        yield
                tstore = kb.ldp(out_r[:, :, ts], YS[par][:], tev,
                                fstore.get(tt), accum=(accum_all or it_i > 0))
                fstore[tt] = tstore
                prev[("ysw", par)] = tstore
                prev[("xb", par)] = kb.tok("sT")
            prev[("w1", wp)] = kb.tok("sT")
            prev["w2"] = kb.tok("sT")
            if not dbw1:
                load_w1(it_i + 1)
        if bias2 is not None:
            B2 = sb("mo_b2", [8, D], BF16)
            tb2 = kb.ldp(B2[:], bias2["lhsT"])
            for tt in range(NTT):
                par = tt % 2
                ts = slice(tt * TT, (tt + 1) * TT)
                for jc in range(NCH):
                    yp = 0 if slim else jc % 2
                    kb.wait(kb.T, tb2, prev.get(("y", yp)),
                            prev.get(("ysw", par)) if jc == 0 else None, kb.tok("sV"))
                    tmy = kb.inc(kb.T.matmul(psY[yp][:], B2[:, jc * 128:(jc + 1) * 128],
                                             combT[:, ts], start=True, stop=True), "sT")
                    if bias2.get("extra") is not None:
                        xt, xc = bias2["extra"]
                        kb.wait(kb.S, tmy)
                        tev = kb.inc(kb.S.activation(YS[par][:, jc, :], psY[yp][:],
                                                     AF.Identity,
                                                     bias=xt[:, jc + xc:jc + xc + 1]), "sS")
                        kb.wait(kb.V, tev)
                    else:
                        kb.wait(kb.V, tmy)
                        tev = kb.inc(kb.V.tensor_copy(YS[par][:, jc, :], psY[yp][:]), "sV")
                    prev[("y", yp)] = tev
                    if jc % 2 == 1:
                        yield
                kb.wait(kb.V, kb.tok("sS"))
                tstore = kb.ldp(out_r[:, :, ts], YS[par][:], kb.tok("sS"), kb.tok("sV"),
                                fstore.get(tt), accum=True)
                fstore[tt] = tstore
                prev[("ysw", par)] = tstore


def run_gen(*gens, weights=None):
    gens = [g for g in gens if g is not None]
    if weights is None:
        weights = [1] * len(gens)
    weights = list(weights[:len(gens)])
    while gens:
        nxt, nw = [], []
        for g, w in zip(gens, weights):
            alive = True
            for _ in range(w):
                try:
                    next(g)
                except StopIteration:
                    alive = False
                    break
            if alive:
                nxt.append(g); nw.append(w)
        gens, weights = nxt, nw



def phase_l0_merge(kb, com, xT, yrT, recT, FT, wo, x1T):
    nc = kb.nc
    with ExitStack() as st:
        sb = lambda n, s, d: st.enter_context(nc.sbuf_tensor(uniq(n), s, d))
        pst = lambda n, s: st.enter_context(nc.psum_tensor(uniq(n), s, F32))
        WO = sb("p3_wo", [128, NCH, D], BF16)
        YR = [sb(f"p3_yr{p}", [128, NCH, TT], BF16) for p in range(2)]
        YN = [sb(f"p3_yn{p}", [128, NCH, TT], BF16) for p in range(2)]
        XF = [sb(f"p3_xf{p}", [128, NCH, TT], F32) for p in range(2)]
        FF = [sb(f"p3_ff{p}", [128, NCH, TT], F32) for p in range(2)]
        RC2 = sb("p3_rc", [2, TT], F32)
        RCB = sb("p3_rcb", [2, TT], BF16)
        EV = [sb(f"p3_ev{p}", [128, TT], F32) for p in range(2)]
        X1 = [sb(f"p3_x1{p}", [128, NCH, TT], F32) for p in range(2)]
        psA = [pst(f"p3_psA{p}", [128, TT]) for p in range(2)]
        psB = pst("p3_psB", [128, TT])
        tw = kb.ldp(WO[:], wo.rearrange("(c p) j -> p c j", p=128))
        yr_r = yrT.rearrange("(c p) t -> p c t", p=128)
        x_r = xT.rearrange("(c p) t -> p c t", p=128)
        f_r = FT.rearrange("(c p) t -> p c t", p=128)
        x1_r = x1T.rearrange("(c p) t -> p c t", p=128)
        prev = {}
        for tt in range(NTT):
            par = tt % 2
            ts = slice(tt * TT, (tt + 1) * TT)
            tly = kb.ldq(YR[par][:], yr_r[:, :, ts], prev.get(("yr", par)))
            tlx = kb.ldq(XF[par][:], x_r[:, :, ts], prev.get(("x1", par)))
            tlf = kb.ldq(FF[par][:], f_r[:, :, ts], prev.get(("x1", par)))
            for ch in range(NCH):
                tlr = kb.ldq(RC2[:], recT[2 * ch:2 * ch + 2, ts], prev.get("rc2"))
                kb.wait(kb.V, tlr, prev.get("rcb"))
                trb = kb.inc(kb.V.tensor_copy(RCB[:], RC2[:]), "sV")
                prev["rc2"] = trb
                kb.wait(kb.T, trb, prev.get("psBr"))
                tbc = kb.inc(kb.T.matmul(psB[:], com["hsel"][:], RCB[:],
                                         start=True, stop=True), "sT")
                prev["rcb"] = tbc
                kb.wait(kb.V, tbc, tly)
                tyn = kb.inc(kb.V.tensor_mul(YN[par][:, ch, :], YR[par][:, ch, :],
                                             psB[:]), "sV")
                prev["psBr"] = tyn
            prev[("yr", par)] = tyn
            for jc in range(NCH):
                yp = jc % 2
                kb.wait(kb.T, tw, tyn, prev.get(("psA", yp)))
                for ch in range(NCH):
                    ins = kb.T.matmul(psA[yp][:], WO[:, ch, jc * 128:(jc + 1) * 128],
                                      YN[par][:, ch, :], start=(ch == 0), stop=(ch == NCH - 1))
                tmm = kb.inc(ins, "sT")
                kb.wait(kb.V, tmm, tlx, prev.get(("ev", yp)))
                tv1 = kb.inc(kb.V.tensor_add(EV[yp][:], psA[yp][:], XF[par][:, jc, :]), "sV")
                prev[("psA", yp)] = tv1
                kb.wait(kb.P, tv1, tlf, prev.get(("x1w", par)))
                tv2 = kb.inc(kb.P.tensor_add(X1[par][:, jc, :], EV[yp][:], FF[par][:, jc, :]), "sP")
                prev[("ev", yp)] = tv2
            tst = kb.stq(x1_r[:, :, ts], X1[par][:], kb.tok("sP"))
            prev[("x1", par)] = tst
            prev[("x1w", par)] = tst
            prev[("yn", par)] = kb.tok("sT")
        kb.barrier()



def phase_l1_proj(kb, com, zT, Wq, Wk, Wv, Wg, Wgk1, Wgk2, q1T, k1T, g1T, v1, wT):
    nc = kb.nc
    with ExitStack() as st:
        sb = lambda n, s, d: st.enter_context(nc.sbuf_tensor(uniq(n), s, d))
        pst = lambda n, s: st.enter_context(nc.psum_tensor(uniq(n), s, F32))
        WQB = sb("p7_wq", [128, NCH, D], BF16)
        WKB = sb("p7_wk", [128, NCH, D], BF16)
        WVB = sb("p7_wv", [128, NCH, D], BF16)
        WGB = sb("p7_wg", [128, NCH, D], BF16)
        WK1 = sb("p7_wk1", [128, NCH, 16], BF16)
        WK2 = sb("p7_wk2", [16, D], BF16)
        ZB = [sb(f"p7_zb{p}", [128, NCH, TT], BF16) for p in range(2)]
        OB = [sb(f"p7_ob{p}", [128, TT], BF16) for p in range(4)]
        T1B = [sb(f"p7_t1{p}", [16, TT], BF16) for p in range(2)]
        WO_ = [sb(f"p7_wo{p}", [128, TT], F32) for p in range(2)]
        psA = [pst(f"p7_psA{p}", [128, TT]) for p in range(2)]
        psG = [pst(f"p7_psG{p}", [16, TT]) for p in range(2)]
        tw = []
        for Wb, w in ((WQB, Wq), (WKB, Wk), (WVB, Wv), (WGB, Wg)):
            tw.append(kb.ldp(Wb[:], w.rearrange("(c p) j -> p c j", p=128)))
        tw1 = kb.ldp(WK1[:], Wgk1.rearrange("(c p) j -> p c j", p=128))
        tw2 = kb.ldp(WK2[:], Wgk2)
        z_r = zT.rearrange("(c p) t -> p c t", p=128)
        outs_r = [o.rearrange("(c p) t -> p c t", p=128) for o in (q1T, k1T, g1T)]
        w_r = wT.rearrange("(c p) t -> p c t", p=128)
        nbg = com["vecs"]["nbgk2"]
        prev = {}
        ob_i = 0
        for tt in range(NTT):
            par = tt % 2
            ts = slice(tt * TT, (tt + 1) * TT)
            tl = kb.ldq(ZB[par][:], z_r[:, :, ts], prev.get(("zb", par)))
            for wi, (Wb, out_r) in enumerate(((WQB, outs_r[0]), (WKB, outs_r[1]),
                                              (WGB, outs_r[2]))):
                for jc in range(NCH):
                    yp = jc % 2
                    osl = ob_i % 4
                    ob_i += 1
                    kb.wait(kb.T, tl, tw[wi if wi < 2 else 3], prev.get(("psA", yp)))
                    for ch in range(NCH):
                        ins = kb.T.matmul(psA[yp][:], Wb[:, ch, jc * 128:(jc + 1) * 128],
                                          ZB[par][:, ch, :], start=(ch == 0),
                                          stop=(ch == NCH - 1))
                    tmm = kb.inc(ins, "sT")
                    kb.wait(kb.V, tmm, prev.get(("ob", osl)))
                    tev = kb.inc(kb.V.tensor_copy(OB[osl][:], psA[yp][:]), "sV")
                    prev[("psA", yp)] = tev
                    tst = kb.stq(out_r[:, jc, ts], OB[osl][:], tev)
                    prev[("ob", osl)] = tst
            for s4 in range(4):
                for half in range(2):
                    yp = (s4 * 2 + half) % 2
                    osl = ob_i % 4
                    ob_i += 1
                    kb.wait(kb.T, tl, tw[2], prev.get(("psA", yp)))
                    for ch in range(NCH):
                        ins = kb.T.matmul(psA[yp][:], ZB[par][:, ch, s4 * 128:(s4 + 1) * 128],
                                          WVB[:, ch, half * 512:(half + 1) * 512],
                                          start=(ch == 0), stop=(ch == NCH - 1))
                    tmm = kb.inc(ins, "sT")
                    kb.wait(kb.V, tmm, prev.get(("ob", osl)))
                    tev = kb.inc(kb.V.tensor_copy(OB[osl][:], psA[yp][:]), "sV")
                    prev[("psA", yp)] = tev
                    row = tt * TT + s4 * 128
                    tst = kb.stq(v1[row:row + 128, half * 512:(half + 1) * 512],
                                 OB[osl][:], tev)
                    prev[("ob", osl)] = tst
            kb.wait(kb.T, tl, tw1, prev.get(("psG", par)))
            for ch in range(NCH):
                ins = kb.T.matmul(psG[par][:], WK1[:, ch, :], ZB[par][:, ch, :],
                                  start=(ch == 0), stop=(ch == NCH - 1))
            tg1 = kb.inc(ins, "sT")
            kb.wait(kb.V, tg1, prev.get(("t1b", par)))
            tt1 = kb.inc(kb.V.tensor_copy(T1B[par][:], psG[par][:]), "sV")
            prev[("psG", par)] = tt1
            for jc in range(NCH):
                yp = jc % 2
                kb.wait(kb.T, tt1, tw2, prev.get(("psA", yp)))
                tmm = kb.inc(kb.T.matmul(psA[yp][:], WK2[:, jc * 128:(jc + 1) * 128],
                                         T1B[par][:], start=True, stop=True), "sT")
                kb.wait(kb.S, tmm, prev.get(("wo", yp)))
                kb.S.activation(WO_[yp][:], psA[yp][:], AF.Exp, scale=-1.0,
                                bias=nbg[:, jc:jc + 1])
                tso = kb.inc(kb.S.activation(WO_[yp][:], WO_[yp][:], AF.Ln,
                                             bias=com["cpool"][:, 0:1]), "sS")
                prev[("psA", yp)] = tso
                tst = kb.stq(w_r[:, jc, ts], WO_[yp][:], tso)
                prev[("wo", yp)] = tst
            prev[("t1b", par)] = kb.tok("sT")
            prev[("zb", par)] = kb.tok("sT")
        kb.barrier()



def phase_gla(kb, com, q1T, k1T, v1, wT, oT):
    nc = kb.nc
    H1, DK = 4, 256
    NC = T // 128
    with ExitStack() as st:
        sb = lambda n, s, d: st.enter_context(nc.sbuf_tensor(uniq(n), s, d))
        pst = lambda n, s, d=F32: st.enter_context(nc.psum_tensor(uniq(n), s, d))
        QH = [sb(f"g_qh{p}", [128, 2, 128], BF16) for p in range(2)]
        KH = [sb(f"g_kh{p}", [128, 2, 128], BF16) for p in range(2)]
        WH = [sb(f"g_wh{p}", [128, 2, 128], F32) for p in range(2)]
        VH = [sb(f"g_vh{p}", [128, 256], BF16) for p in range(2)]
        GW = [sb(f"g_gw{p}", [128, 2, 128], F32) for p in range(2)]
        A_ = [sb(f"g_a{p}", [128, 2, 128], F32) for p in range(2)]
        AI = [sb(f"g_ai{p}", [128, 2, 128], F32) for p in range(2)]
        QS = [sb(f"g_qs{p}", [128, 2, 128], BF16) for p in range(2)]
        KS = [sb(f"g_ks{p}", [128, 2, 128], BF16) for p in range(2)]
        KE = [sb(f"g_ke{p}", [128, 2, 128], BF16) for p in range(2)]
        KTok = [sb(f"g_kt{p}", [128, 2, 128], BF16) for p in range(2)]
        PT = [sb(f"g_pt{p}", [128, 128], BF16) for p in range(2)]
        OB = [sb(f"g_ob{p}", [128, 2, 128], F32) for p in range(2)]
        HS = [sb(f"g_h{dc}", [128, 256], F32) for dc in range(2)]
        HB = [sb(f"g_hb{dc}", [128, 256], BF16) for dc in range(2)]
        psS = [pst(f"g_psS{p}", [128, 128]) for p in range(2)]
        psO = [pst(f"g_psO{p}", [128, 256]) for p in range(2)]
        psD = [pst(f"g_psD{p}", [128, 512]) for p in range(2)]
        psK = [pst(f"g_psK{p}", [128, 2, 128], BF16) for p in range(2)]
        q_r = q1T.rearrange("(c p) t -> p c t", p=128)
        k_r = k1T.rearrange("(c p) t -> p c t", p=128)
        w_r = wT.rearrange("(c p) t -> p c t", p=128)
        o_r = oT.rearrange("(c p) t -> p c t", p=128)
        v_r = v1.rearrange("(c p) j -> p c j", p=128)
        prev = {}
        for h in range(H1):
            hc0 = 2 * h
            for c in range(NC):
                par = c % 2
                ts = slice(c * 128, (c + 1) * 128)
                fq = prev.get(("qh", par))
                tlq = kb.ldq(QH[par][:], q_r[:, hc0:hc0 + 2, ts], fq)
                tlk = kb.ldq(KH[par][:], k_r[:, hc0:hc0 + 2, ts], fq)
                tlw = kb.ldq(WH[par][:], w_r[:, hc0:hc0 + 2, ts], fq)
                tlv = kb.ldq(VH[par][:], v_r[:, c, 256 * h:256 * h + 256], fq)
                kb.wait(kb.V, tlw, prev.get(("gw", par)))
                for dc in range(2):
                    ins = kb.V.tensor_tensor_scan(GW[par][:, dc, :], WH[par][:, dc, :],
                                                  WH[par][:, dc, :], 0.0,
                                                  ALU.add, ALU.bypass)
                tsc = kb.inc(ins, "sV")
                kb.wait(kb.S, tsc, prev.get(("a", par)))
                kb.S.activation(A_[par][:], GW[par][:], AF.Exp, scale=-1.0 / 16)
                texp = kb.inc(kb.S.activation(AI[par][:], GW[par][:], AF.Exp,
                                              scale=1.0 / 16), "sS")
                kb.wait(kb.P, texp, tlk, prev.get(("qs", par)))
                for dc in range(2):
                    ins = kb.P.tensor_mul(KS[par][:, dc, :], KH[par][:, dc, :],
                                          AI[par][:, dc, :])
                tks = kb.inc(ins, "sP")
                kb.wait(kb.V, texp, tlq, tks, prev.get(("qs", par)))
                for dc in range(2):
                    kb.V.scalar_tensor_tensor(QS[par][:, dc, :], A_[par][:, dc, :],
                                              1.0 / 16, QH[par][:, dc, :],
                                              ALU.mult, ALU.mult)
                    ins = kb.V.tensor_scalar(KE[par][:, dc, :], KS[par][:, dc, :],
                                             A_[par][:, dc, 127:128], None, ALU.mult)
                tqk = kb.inc(ins, "sV")
                kb.wait(kb.T, tqk, prev.get(("psK", par)))
                for dc in range(2):
                    ins = kb.T.transpose(psK[par][:, dc, :], KE[par][:, dc, :],
                                         com["identb"][:])
                tktr = kb.inc(ins, "sT")
                kb.wait(kb.V, tktr, prev.get(("ktok", par)))
                tkt = kb.inc(kb.V.tensor_copy(KTok[par][:], psK[par][:]), "sV")
                prev[("psK", par)] = tkt
                kb.wait(kb.T, tqk, prev.get(("psS", par)))
                for dc in range(2):
                    ins = kb.T.matmul(psS[par][:], KS[par][:, dc, :], QS[par][:, dc, :],
                                      start=(dc == 0), stop=(dc == 1))
                tsm = kb.inc(ins, "sT")
                kb.wait(kb.V, tsm, prev.get(("pt", par)))
                tpt = kb.inc(kb.V.tensor_mul(PT[par][:], psS[par][:], com["tri"][:]), "sV")
                prev[("psS", par)] = tpt
                kb.wait(kb.T, tpt, tlv, prev.get(("psO", par)))
                if c > 0:
                    kb.wait(kb.T, prev.get("hb"))
                for dv in range(2):
                    ins = kb.T.matmul(psO[par][:, dv * 128:(dv + 1) * 128],
                                      VH[par][:, dv * 128:(dv + 1) * 128], PT[par][:],
                                      start=True, stop=(c == 0))
                    if c > 0:
                        for dc in range(2):
                            ins = kb.T.matmul(psO[par][:, dv * 128:(dv + 1) * 128],
                                              HB[dc][:, dv * 128:(dv + 1) * 128],
                                              QS[par][:, dc, :],
                                              start=False, stop=(dc == 1))
                to = kb.inc(ins, "sT")
                kb.wait(kb.V, to, prev.get(("obv", par)))
                tev = kb.inc(kb.V.tensor_copy(OB[par][:], psO[par][:]), "sV")
                prev[("psO", par)] = tev
                tso = kb.stq(o_r[:, hc0:hc0 + 2, ts], OB[par][:], tev)
                prev[("obv", par)] = tso
                kb.wait(kb.T, tkt, tlv, prev.get(("psD", par)))
                for dc in range(2):
                    ins = kb.T.matmul(psD[par][:, dc * 256:(dc + 1) * 256],
                                      KTok[par][:, dc, :], VH[par][:],
                                      start=True, stop=True)
                tdh = kb.inc(ins, "sT")
                if c == 0:
                    kb.wait(kb.V, tdh, prev.get("hb_free"))
                    for dc in range(2):
                        ins = kb.V.tensor_copy(HS[dc][:], psD[par][:, dc * 256:(dc + 1) * 256])
                    thu = kb.inc(ins, "sV")
                else:
                    kb.wait(kb.V, texp, prev.get("hadd"))
                    for dc in range(2):
                        ins = kb.V.tensor_scalar(HS[dc][:], HS[dc][:],
                                                 A_[par][:, dc, 127:128], None, ALU.mult)
                    thscale = kb.inc(ins, "sV")
                    kb.wait(kb.V, thscale, tdh)
                    for dc in range(2):
                        ins = kb.V.tensor_add(HS[dc][:], HS[dc][:],
                                              psD[par][:, dc * 256:(dc + 1) * 256])
                    thu = kb.inc(ins, "sV")
                prev[("psD", par)] = thu
                prev["hadd"] = thu
                kb.wait(kb.P, thu, prev.get("hb"), prev.get("hbT"))
                for dc in range(2):
                    ins = kb.P.tensor_copy(HB[dc][:], HS[dc][:])
                prev["hb"] = kb.inc(ins, "sP")
                prev["hbT"] = to
                prev[("qh", par)] = [kb.tok("sT"), kb.tok("sP")]
                prev[("gw", par)] = texp
                prev[("a", par)] = tqk if c == 0 else [tqk, thscale]
                prev[("qs", par)] = to
                prev[("ktok", par)] = tdh
                prev[("pt", par)] = to
            prev["hb_free"] = prev["hb"]
        kb.barrier()



def phase_gate(kb, com, oT, g1T, ogT):
    nc = kb.nc
    with ExitStack() as st:
        sb = lambda n, s, d: st.enter_context(nc.sbuf_tensor(uniq(n), s, d))
        pst = lambda n, s: st.enter_context(nc.psum_tensor(uniq(n), s, F32))
        OF = [sb(f"gt_of{p}", [128, NCH, TT], F32) for p in range(2)]
        GB = [sb(f"gt_gb{p}", [128, NCH, TT], BF16) for p in range(2)]
        SQ = [sb(f"gt_sq{p}", [128, NCH, TT], BF16) for p in range(2)]
        SG = [sb(f"gt_sg{p}", [128, TT], F32) for p in range(4)]
        TM = [sb(f"gt_tm{p}", [128, TT], F32) for p in range(2)]
        OGB = [sb(f"gt_og{p}", [128, NCH, TT], BF16) for p in range(2)]
        SD = sb("gt_sd", [1, TT], F32)
        RB1 = sb("gt_rb1", [1, TT], BF16)
        psM = [pst(f"gt_psM{p}", [1, TT]) for p in range(2)]
        psB = [pst(f"gt_psB{p}", [128, TT]) for p in range(2)]
        o_r = oT.rearrange("(c p) t -> p c t", p=128)
        g_r = g1T.rearrange("(c p) t -> p c t", p=128)
        og_r = ogT.rearrange("(c p) t -> p c t", p=128)
        rmsw = com["vecs"]["rms_w"]
        prev = {}
        for tt in range(NTT):
            par = tt % 2
            ts = slice(tt * TT, (tt + 1) * TT)
            tlo = kb.ldq(OF[par][:], o_r[:, :, ts], prev.get(("of", par)))
            tlg = kb.ldq(GB[par][:], g_r[:, :, ts], prev.get(("gb", par)))
            kb.wait(kb.S, tlo, prev.get(("sq", par)))
            tsq = kb.inc(kb.S.activation(SQ[par][:], OF[par][:], AF.Square), "sS")
            last = None
            for h in range(4):
                mp = h % 2
                kb.wait(kb.T, tsq, prev.get(("psM", mp)))
                for dc in range(2):
                    ins = kb.T.matmul(psM[mp][:], com["ones_b"][:],
                                      SQ[par][:, 2 * h + dc, :],
                                      start=(dc == 0), stop=(dc == 1))
                tst = kb.inc(ins, "sT")
                kb.wait(kb.S, tst, prev.get("sd"))
                tsd = kb.inc(kb.S.activation(SD[:], psM[mp][:], AF.Sqrt,
                                             scale=1.0 / 256, bias=com["cpool"][:1, 1:2]), "sS")
                kb.wait(kb.V, tsd, prev.get("rb1"))
                tre = kb.inc(kb.V.reciprocal(SD[:], SD[:]), "sV")
                kb.wait(kb.V, tre)
                trb = kb.inc(kb.V.tensor_copy(RB1[:], SD[:]), "sV")
                prev["sd"] = trb
                kb.wait(kb.T, trb, prev.get(("psB", mp)))
                tbc = kb.inc(kb.T.matmul(psB[mp][:], com["ones1_b"][0:1, 0:128],
                                         RB1[:], start=True, stop=True), "sT")
                prev["rb1"] = tbc
                prev[("psM", mp)] = tbc
                for dc in range(2):
                    ch = 2 * h + dc
                    sgl = (2 * h + dc) % 4
                    kb.wait(kb.S, tlg, prev.get(("sg", sgl)))
                    tsg = kb.inc(kb.S.activation(SG[sgl][:], GB[par][:, ch, :], AF.Silu), "sS")
                    kb.wait(kb.V, tbc, tsg, prev.get(("tm", dc)))
                    tv1 = kb.inc(kb.V.scalar_tensor_tensor(
                        TM[dc][:], OF[par][:, ch, :], rmsw[:, dc:dc + 1], psB[mp][:],
                        ALU.mult, ALU.mult), "sV")
                    prev[("psB", mp)] = tv1
                    kb.wait(kb.P, tv1, prev.get(("og", par)))
                    ins = kb.P.tensor_mul(OGB[par][:, ch, :], TM[dc][:], SG[sgl][:])
                    last = kb.inc(ins, "sP")
                    prev[("tm", dc)] = last
                    prev[("sg", sgl)] = last
            tso = kb.stq(og_r[:, :, ts], OGB[par][:], last)
            prev[("og", par)] = tso
            prev[("of", par)] = [kb.tok("sV"), kb.tok("sS")]
            prev[("gb", par)] = kb.tok("sS")
            prev[("sq", par)] = kb.tok("sT")
        kb.barrier()



def phase_l1_merge(kb, com, xhT, x1T, ogT, Wo, xpT, acc0T):
    nc = kb.nc
    with ExitStack() as st:
        sb = lambda n, s, d: st.enter_context(nc.sbuf_tensor(uniq(n), s, d))
        pst = lambda n, s: st.enter_context(nc.psum_tensor(uniq(n), s, F32))
        WOB = sb("m1_wo", [128, NCH, D], BF16)
        YB = [sb(f"m1_yb{p}", [128, NCH, TT], BF16) for p in range(2)]
        XH = [sb(f"m1_xh{p}", [128, NCH, TT], F32) for p in range(2)]
        X1 = [sb(f"m1_x1{p}", [128, NCH, TT], F32) for p in range(2)]
        XP = [sb(f"m1_xp{p}", [128, NCH, TT], F32) for p in range(2)]
        AC = [sb(f"m1_ac{p}", [128, NCH, TT], F32) for p in range(2)]
        psA = [pst(f"m1_psA{p}", [128, TT]) for p in range(2)]
        tw = kb.ldp(WOB[:], Wo.rearrange("(c p) j -> p c j", p=128))
        og_r = ogT.rearrange("(c p) t -> p c t", p=128)
        xh_r = xhT.rearrange("(c p) t -> p c t", p=128)
        x1_r = x1T.rearrange("(c p) t -> p c t", p=128)
        xp_r = xpT.rearrange("(c p) t -> p c t", p=128)
        ac_r = acc0T.rearrange("(c p) t -> p c t", p=128)
        prev = {}
        for tt in range(NTT):
            par = tt % 2
            ts = slice(tt * TT, (tt + 1) * TT)
            tly = kb.ldq(YB[par][:], og_r[:, :, ts], prev.get(("yb", par)))
            tlx = kb.ldq(XH[par][:], xh_r[:, :, ts], prev.get(("w", par)))
            tl1 = kb.ldq(X1[par][:], x1_r[:, :, ts], prev.get(("w", par)))
            for jc in range(NCH):
                yp = jc % 2
                kb.wait(kb.T, tw, tly, prev.get(("psA", yp)))
                for ch in range(NCH):
                    ins = kb.T.matmul(psA[yp][:], WOB[:, ch, jc * 128:(jc + 1) * 128],
                                      YB[par][:, ch, :], start=(ch == 0), stop=(ch == NCH - 1))
                tmm = kb.inc(ins, "sT")
                kb.wait(kb.V, tmm, tlx)
                tv1 = kb.inc(kb.V.tensor_add(XP[par][:, jc, :], psA[yp][:],
                                             XH[par][:, jc, :]), "sV")
                prev[("psA", yp)] = tv1
                kb.wait(kb.P, tv1, tl1)
                tv2 = kb.inc(kb.P.tensor_add(AC[par][:, jc, :], XP[par][:, jc, :],
                                             X1[par][:, jc, :]), "sP")
            ts1 = kb.stq(xp_r[:, :, ts], XP[par][:], kb.tok("sP"))
            ts2 = kb.stq(ac_r[:, :, ts], AC[par][:], kb.tok("sP"))
            prev[("yb", par)] = kb.tok("sT")
            prev[("w", par)] = ts2
        kb.barrier()



def phase_mlp(kb, com, z2T, acc0T, W1, W2, outT):
    nc = kb.nc
    NH = 24
    with ExitStack() as st:
        sb = lambda n, s, d: st.enter_context(nc.sbuf_tensor(uniq(n), s, d))
        pst = lambda n, s: st.enter_context(nc.psum_tensor(uniq(n), s, F32))
        W1B = sb("ml_w1", [128, NCH, 3072], BF16)
        W2B = sb("ml_w2", [128, NH, D], BF16)
        ZB = [sb(f"ml_zb{p}", [128, NCH, TT], BF16) for p in range(2)]
        HSB = sb("ml_hs", [128, NH, TT], BF16)
        AC = [sb(f"ml_ac{p}", [128, NCH, TT], F32) for p in range(2)]
        EV = [sb(f"ml_ev{p}", [128, TT], F32) for p in range(2)]
        OU = [sb(f"ml_ou{p}", [128, TT], F32) for p in range(4)]
        psH = pst("ml_psH", [128, 2048])
        psY = [pst(f"ml_psY{p}", [128, TT]) for p in range(2)]
        tw1 = kb.ldp(W1B[:], W1.rearrange("(c p) i -> p c i", p=128))
        tw2 = kb.ldp(W2B[:], W2.rearrange("(c p) j -> p c j", p=128))
        z_r = z2T.rearrange("(c p) t -> p c t", p=128)
        ac_r = acc0T.rearrange("(c p) t -> p c t", p=128)
        out_r = outT.rearrange("(c p) t -> p c t", p=128)
        b1t = com["vecs"]["b1_mlp"]
        b2t = com["vecs"]["b2_mlp"]
        prev = {}
        for tt in range(NTT):
            par = tt % 2
            ts = slice(tt * TT, (tt + 1) * TT)
            tl = kb.ldq(ZB[par][:], z_r[:, :, ts], prev.get(("zb", par)))
            tla = kb.ldq(AC[par][:], ac_r[:, :, ts], prev.get(("ou", par)))
            for g in range(12):
                hcs = range(g * 2, g * 2 + 2)
                ph = g % 2
                kb.wait(kb.T, tl, tw1, prev.get(("h", ph)))
                for hi, hc in enumerate(hcs):
                    for ch in range(NCH):
                        ins = kb.T.matmul(psH[:, ph * 1024 + hi * 512:ph * 1024 + hi * 512 + TT],
                                          W1B[:, ch, hc * 128:(hc + 1) * 128],
                                          ZB[par][:, ch, :],
                                          start=(ch == 0), stop=(ch == NCH - 1))
                tmm = kb.inc(ins, "sT")
                kb.wait(kb.S, tmm)
                for hi, hc in enumerate(hcs):
                    kb.S.activation(HSB[:, hc, :],
                                    psH[:, ph * 1024 + hi * 512:ph * 1024 + hi * 512 + TT],
                                    AF.Lrelu, alpha=0.01, bias=b1t[:, hc:hc + 1])
                    ins = kb.S.activation(HSB[:, hc, :], HSB[:, hc, :], AF.Square)
                tact = kb.inc(ins, "sS")
                prev[("h", ph)] = tact
            ost_toks = []
            for jc in range(NCH):
                yp = jc % 2
                osl = (tt * NCH + jc) % 4
                kb.wait(kb.T, tw2, prev.get(("y", yp)), kb.tok("sS"))
                for ic in range(NH):
                    ins = kb.T.matmul(psY[yp][:], W2B[:, ic, jc * 128:(jc + 1) * 128],
                                      HSB[:, ic, :], start=(ic == 0), stop=(ic == NH - 1))
                tmy = kb.inc(ins, "sT")
                kb.wait(kb.S, tmy, prev.get(("ev", yp)))
                tev = kb.inc(kb.S.activation(EV[yp][:], psY[yp][:], AF.Identity,
                                             bias=b2t[:, jc:jc + 1]), "sS")
                prev[("y", yp)] = tev
                kb.wait(kb.V, tev, tla, prev.get(("ous", osl)))
                ins = kb.V.tensor_add(OU[osl][:], EV[yp][:], AC[par][:, jc, :])
                tvo = kb.inc(ins, "sV")
                prev[("ev", yp)] = tvo
                tst = kb.stq(out_r[:, jc, ts], OU[osl][:], tvo)
                prev[("ous", osl)] = tst
                ost_toks.append(tst)
            prev[("ou", par)] = ost_toks[-1]
            prev[("zb", par)] = kb.tok("sT")
        kb.barrier()



def phase_ue(kb, com, xhT, ueT, affines):
    nc = kb.nc
    NE = len(affines)
    with ExitStack() as st:
        sb = lambda n, s, d: st.enter_context(nc.sbuf_tensor(uniq(n), s, d))
        XH = [sb(f"ue_xh{p}", [128, NCH, TT], F32) for p in range(2)]
        UB = [sb(f"ue_ub{p}", [128, NCH, TT], BF16) for p in range(4)]
        xh_r = xhT.rearrange("(c p) t -> p c t", p=128)
        ue_r = ueT.rearrange("e (c p) t -> e p c t", p=128)
        prev = {}
        for tt in range(NTT):
            par = tt % 2
            ts = slice(tt * TT, (tt + 1) * TT)
            tl = kb.ldq(XH[par][:], xh_r[:, :, ts], prev.get(("xh", par)))
            for e in range(NE):
                ub = (tt * NE + e) % 4
                g_t, b_t = affines[e]
                kb.wait(kb.V, tl, prev.get(("ub", ub)))
                for ch in range(NCH):
                    ins = kb.V.tensor_scalar(UB[ub][:, ch, :], XH[par][:, ch, :],
                                             g_t[:, ch:ch + 1], b_t[:, ch:ch + 1],
                                             ALU.mult, ALU.add)
                tv = kb.inc(ins, "sV")
                tst = kb.stq(ue_r[e, :, :, ts], UB[ub][:], tv)
                prev[("ub", ub)] = tst
            prev[("xh", par)] = kb.tok("sV")
        kb.barrier()



PHASES_ALL = ["p0", "p1", "p2", "p4a", "p4", "p3",
              "p6", "p7a", "p7", "p7b", "p7c", "p7d",
              "p8a", "p8b", "p8c"]


def build_program(phases=None):
    phases = phases or PHASES_ALL
    kb = KB()
    nc = kb.nc
    with kb.stack:
        ei = lambda n, s, d=F32: nc.dram_tensor(n, s, d, kind="ExternalInput").ap()
        xT = ei("xT", [D, T])
        wq0 = ei("wq0", [D, D]); wk0 = ei("wk0", [D, D]); wv0 = ei("wv0", [D, D])
        wo0 = ei("wo0", [D, D])
        rwT0 = ei("rwT0", [D, 8])
        eb0 = ei("eb0", [1, 8])
        sW1 = ei("sW1", [D, 512]); sW2 = ei("sW2", [512, D])
        rW1 = ei("rW1", [8, D, 512]); rW2 = ei("rW2", [8, 512, D])
        sb1v = ei("sb1v", [512]); sb2v = ei("sb2v", [D])
        rb1v = ei("rb1v", [8, 512]); rb2m = ei("rb2m", [8, D])
        q1w = ei("q1w", [D, D]); k1w = ei("k1w", [D, D])
        v1w = ei("v1w", [D, D]); g1w = ei("g1w", [D, D])
        wgk1 = ei("wgk1", [D, 16]); wgk2 = ei("wgk2", [16, D])
        wo1 = ei("wo1", [D, D])
        w1m = ei("w1m", [D, 3072]); w2m = ei("w2m", [3072, D])
        rwT1 = ei("rwT1", [D, 8]); rb1row = ei("rb1row", [1, 8])
        w1l = ei("w1l", [8, D, 4096]); w2l = ei("w2l", [8, 4096, D])
        b2m8 = ei("b2m8", [8, D])
        ln1wv = ei("ln1wv", [D]); ln1bv = ei("ln1bv", [D])
        ln2wv = ei("ln2wv", [D]); ln2bv = ei("ln2bv", [D])
        bgk2v = ei("bgk2v", [D]); rmswv = ei("rmswv", [256])
        b1mv = ei("b1mv", [3072]); b2mv = ei("b2mv", [D])
        rngv = ei("rngv", [D]); rnbv = ei("rnbv", [D])
        lngv = ei("lngv", [8, D]); lnbv = ei("lnbv", [8, D])
        b1lv = ei("b1lv", [8, 4096])
        consts = declare_consts(nc)
        xn0T = kb.dram("xn0T", [D, T], BF16)
        qnT = kb.dram("qnT", [D, T], BF16)
        knT = kb.dram("knT", [D, T], BF16)
        v0 = kb.dram("v0", [T, 16 * 65], BF16)
        yrT = kb.dram("yrT", [D, T], BF16)
        recT = kb.dram("recT", [16, T], F32)
        FT = kb.dram("FT", [D, T], F32)
        x1T = kb.dram("x1T", [D, T], F32)
        xhT = kb.dram("xhT", [D, T], F32)
        zT = kb.dram("zT", [D, T], BF16)
        q1T = kb.dram("q1T", [D, T], BF16)
        k1T = kb.dram("k1T", [D, T], BF16)
        g1T = kb.dram("g1T", [D, T], BF16)
        v1 = kb.dram("v1", [T, D], BF16)
        wT = kb.dram("wT", [D, T], F32)
        oT = kb.dram("oT", [D, T], F32)
        ogT = kb.dram("ogT", [D, T], BF16)
        xpT = kb.dram("xpT", [D, T], F32)
        acc0T = kb.dram("acc0T", [D, T], F32)
        z2T = kb.dram("z2T", [D, T], BF16)
        ueT = kb.dram("ueT", [9, D, T], BF16)
        out_kind = "ExternalOutput"
        outT = nc.dram_tensor("outT", [D, T], F32, kind=out_kind).ap()

        vecs = {"sb1": (sb1v, 512), "sb2": (sb2v, D)}
        for e in range(8):
            vecs[f"rb1_{e}"] = (rb1v[e, :], 512)
        vecs.update({
            "ln1w": (ln1wv, D), "ln1b": (ln1bv, D),
            "ln2w": (ln2wv, D), "ln2b": (ln2bv, D),
            "bgk2": (bgk2v, D), "rms_w": (rmswv, 256),
            "b1_mlp": (b1mv, 3072), "b2_mlp": (b2mv, D),
            "rn_g": (rngv, D), "rn_b": (rnbv, D),
        })
        for e in range(8):
            vecs[f"lng_{e}"] = (lngv[e, :], D)
            vecs[f"lnb_{e}"] = (lnbv[e, :], D)
            vecs[f"b1l_{e}"] = (b1lv[e, :], 4096)
        com = prep_common(kb, vecs, consts)
        nbg = kb.sbuf("vec_nbgk2", [128, NCH], F32)
        tneg = kb.inc(kb.V.tensor_scalar_mul(nbg[:], com["vecs"]["bgk2"][:], -1.0), "sV")
        com["vecs"]["nbgk2"] = nbg
        for nm in ["sb1"] + [f"rb1_{e}" for e in range(8)]:
            nt = kb.sbuf("vec_n" + nm, [128, 4], F32)
            kb.inc(kb.V.tensor_scalar_mul(nt[:], com["vecs"][nm][:], -1.0), "sV")
            com["vecs"]["n" + nm] = nt
        combT0 = kb.sbuf("combT0", [8, T], BF16)
        combT1 = kb.sbuf("combT1", [8, T], BF16)
        kb.barrier()

        if "p0" in phases:
            phase_ln(kb, com, xT, xn0T)
        if "p1" in phases:
            phase_l0_proj(kb, com, xn0T, wq0, wk0, wv0, qnT, knT, v0)
        if "p4a" in phases:
            phase_router(kb, com, xn0T, rwT0, eb0, combT0, "l0")
            if "combT0d" in DEBUG_OUTPUTS:
                cd = kb.dram("combT0d", [8, T], BF16)
                kb.stq(cd, combT0[:])
                kb.barrier()
        g_attn = g_moe0 = None
        ilv_st = ExitStack()
        if "p2" in phases:
            g_attn = phase_l0_attn(kb, com, qnT, knT, v0, yrT, recT, st=ilv_st)
        if "p4" in phases:
            items = [dict(w1=sW1, w2=sW2, b1=(com["vecs"]["sb1"], 0),
                          b1n=(com["vecs"]["nsb1"], 0), inT=xn0T, cb=None)]
            if "shared_only" not in phases:
                for e in range(8):
                    items.append(dict(w1=rW1[e], w2=rW2[e],
                                      b1=(com["vecs"][f"rb1_{e}"], 0),
                                      b1n=(com["vecs"][f"nrb1_{e}"], 0),
                                      inT=xn0T, cb=e))
            bias2 = dict(lhsT=rb2m, extra=(com["vecs"]["sb2"], 0))
            if "no_bias2" in phases:
                bias2 = None
            act0 = "silu_exp" if "p2" in phases else AF.Silu
            g_moe0 = phase_moe(kb, com, items, act0, FT, combT0, bias2=bias2,
                               slim=("p2" in phases),
                               st=ilv_st if "p2" in phases else None)
        if g_attn is not None or g_moe0 is not None:
            with ilv_st:
                run_gen(g_attn, g_moe0, weights=[4, 1])
            kb.barrier()
        if "p3" in phases:
            phase_l0_merge(kb, com, xT, yrT, recT, FT, wo0, x1T)
        if "seed_x1" in phases:
            kb.ldq(x1T, xT)
            kb.barrier()
        if "p6" in phases:
            phase_ln(kb, com, x1T, zT, dst_f32=xhT,
                     affine=(com["vecs"]["ln1w"], com["vecs"]["ln1b"]))
        if "p7a" in phases:
            phase_l1_proj(kb, com, zT, q1w, k1w, v1w, g1w, wgk1, wgk2,
                          q1T, k1T, g1T, v1, wT)
        if "p7" in phases:
            phase_gla(kb, com, q1T, k1T, v1, wT, oT)
        if "p7b" in phases:
            phase_gate(kb, com, oT, g1T, ogT)
        if "p7c" in phases:
            phase_l1_merge(kb, com, xhT, x1T, ogT, wo1, xpT, acc0T)
        if "p7d" in phases:
            phase_ln(kb, com, xpT, z2T,
                     affine=(com["vecs"]["ln2w"], com["vecs"]["ln2b"]))
            phase_mlp(kb, com, z2T, acc0T, w1m, w2m, outT)
        if "p8a" in phases:
            affs = [(com["vecs"][f"lng_{e}"], com["vecs"][f"lnb_{e}"]) for e in range(8)]
            affs.append((com["vecs"]["rn_g"], com["vecs"]["rn_b"]))
            phase_ue(kb, com, xhT, ueT, affs)
        if "p8b" in phases:
            phase_router(kb, com, ueT[8], rwT1, rb1row, combT1, "l1")
            if "combT1d" in DEBUG_OUTPUTS:
                cd = kb.dram("combT1d", [8, T], BF16)
                kb.stq(cd, combT1[:])
                kb.barrier()
        if "p8c" in phases:
            items = []
            for e in range(8):
                for ib in range(2):
                    items.append(dict(
                        w1=w1l[e][:, ib * 2048:(ib + 1) * 2048],
                        w2=w2l[e][ib * 2048:(ib + 1) * 2048, :],
                        b1=(com["vecs"][f"b1l_{e}"], ib * 16),
                        inT=ueT[e], cb=e))
            run_gen(phase_moe(kb, com, items, AF.Gelu, outT, combT1,
                              bias2=dict(lhsT=b2m8, extra=None), accum_all=True))
            kb.barrier()

        kb.barrier()
    return nc



def host_inputs_per_core(b, inputs):
    ac = np.ascontiguousarray
    im = {"xT": ac(inputs["L0_attn_x"][b].T)}
    im["wq0"] = ac(inputs["L0_attn_w_q"].T)
    im["wk0"] = ac(inputs["L0_attn_w_k"].T)
    im["wv0"] = ac(inputs["L0_attn_w_v"].T)
    im["wo0"] = ac(inputs["L0_attn_w_o"].T)
    im["rwT0"] = ac(inputs["L0_ffn_routing_W"].T)
    im["eb0"] = ac(inputs["L0_ffn_expert_bias"].reshape(1, 8))
    im["sW1"] = inputs["L0_ffn_sW1"]
    im["sW2"] = inputs["L0_ffn_sW2"]
    im["rW1"] = inputs["L0_ffn_rW1"]
    im["rW2"] = inputs["L0_ffn_rW2"]
    im["sb1v"] = inputs["L0_ffn_sb1"]
    im["sb2v"] = inputs["L0_ffn_sb2"]
    im["rb1v"] = inputs["L0_ffn_rb1"]
    im["rb2m"] = inputs["L0_ffn_rb2"]
    im["q1w"] = inputs["L1_attn_Wq"]
    im["k1w"] = inputs["L1_attn_Wk"]
    im["v1w"] = inputs["L1_attn_Wv"]
    im["g1w"] = inputs["L1_attn_Wg"]
    im["wgk1"] = inputs["L1_attn_Wgk1"]
    im["wgk2"] = inputs["L1_attn_Wgk2"]
    im["wo1"] = inputs["L1_attn_Wo"]
    im["w1m"] = inputs["L1_attn_W1"]
    im["w2m"] = inputs["L1_attn_W2"]
    im["rwT1"] = ac(inputs["L1_ffn_router_w"].T)
    im["rb1row"] = ac(inputs["L1_ffn_router_b"].reshape(1, 8))
    im["w1l"] = inputs["L1_ffn_w1"]
    im["w2l"] = inputs["L1_ffn_w2"]
    im["b2m8"] = inputs["L1_ffn_b2"]
    im["ln1wv"] = inputs["L1_attn_ln1_w"]
    im["ln1bv"] = inputs["L1_attn_ln1_b"]
    im["ln2wv"] = inputs["L1_attn_ln2_w"]
    im["ln2bv"] = inputs["L1_attn_ln2_b"]
    im["bgk2v"] = inputs["L1_attn_bgk2"]
    im["rmswv"] = inputs["L1_attn_rms_w"]
    im["b1mv"] = inputs["L1_attn_b1"]
    im["b2mv"] = inputs["L1_attn_b2"]
    im["rngv"] = inputs["L1_ffn_rn_g"]
    im["rnbv"] = inputs["L1_ffn_rn_b"]
    im["lngv"] = inputs["L1_ffn_ln_g"]
    im["lnbv"] = inputs["L1_ffn_ln_b"]
    im["b1lv"] = inputs["L1_ffn_b1"]
    im.update(host_consts())
    return im


def kernel(**inputs):
    nc = build_program()
    in_maps = [host_inputs_per_core(b, inputs) for b in range(8)]
    res = run_bass_kernel_spmd(nc, in_maps, core_ids=list(range(8)))
    out = np.stack([np.ascontiguousarray(r["outT"].T) for r in res.results], axis=0)
    return out
